# version 01
# baseline (speedup 1.0000x reference)
import numpy as np
import ml_dtypes
import concourse.bass as bass
import concourse.mybir as mybir
from concourse.bass_utils import run_bass_kernel_spmd
from concourse import library_config
from concourse.bass_isa import ReduceOp

F32 = mybir.dt.float32
BF16 = mybir.dt.bfloat16
AF = mybir.ActivationFunctionType
ALU = mybir.AluOpType
AX = mybir.AxisListType
bf16 = ml_dtypes.bfloat16

TOK, NT, NH, HD, D = 2048, 16, 16, 64, 1024
NG, CHK = 4, 32
EPS = 1e-5
MAGIC = float(np.float32(1.5 * 2 ** 23))
PAIRS = [[0, 1], [2, 3], [4, 5], [6, 7]]
ALL8 = [list(range(8))]
STAGES = 4


class Sem:
    __slots__ = ("name", "h", "n")

    def __init__(self, name):
        self.name, self.h, self.n = name, None, 0


class Emit:
    ENGINES = ("sync", "scalar", "tensor", "vector", "gpsimd")

    def __init__(self):
        self.items = {e: [] for e in self.ENGINES}
        self.done = {e: {} for e in self.ENGINES}

    def op(self, eng, fn, waits=(), inc=None, dma=False):
        amt = 16 if dma else 1
        d = self.done[eng]
        ws = []
        for s, v in waits:
            if v > 0 and d.get(s, -1) < v:
                ws.append((s, v))
                d[s] = v
        if inc is not None:
            inc.n += amt
        self.items[eng].append((fn, tuple(ws), inc, amt))

    def emit(self, nc):
        with nc.Block() as block:
            for e in self.ENGINES:
                if not self.items[e]:
                    continue

                def mk(items):
                    def body(eng):
                        for fn, ws, inc, amt in items:
                            for s, v in ws:
                                eng.wait_ge(s.h, v)
                            inst = fn(eng)
                            if inc is not None:
                                inst.then_inc(inc.h, amt)
                    return body
                getattr(block, e)(mk(self.items[e]))


def build_program(dumps=(), stages=STAGES, s1cut=None):
    nc = bass.Bass("TRN2", target_bir_lowering=False)
    E = Emit()
    dumps = set(dumps)

    x_d = nc.dram_tensor("x", [TOK, D], F32, kind="ExternalInput")
    out_d = nc.dram_tensor("out", [TOK, D], F32, kind="ExternalOutput")
    wq_d = nc.dram_tensor("wq", [128, 8192], F32, kind="ExternalInput")
    wk_d = nc.dram_tensor("wk", [128, 8192], F32, kind="ExternalInput")
    wv_d = nc.dram_tensor("wv", [128, 8192], F32, kind="ExternalInput")
    wo_d = nc.dram_tensor("wo", [128, 8192], F32, kind="ExternalInput")
    wf_d = nc.dram_tensor("wf", [128, 8192], F32, kind="ExternalInput")
    wvb_d = nc.dram_tensor("wvb", [128, 8192], F32, kind="ExternalInput")
    wob_d = nc.dram_tensor("wob", [128, 8192], F32, kind="ExternalInput")
    w3_d = nc.dram_tensor("w3", [128, 24], F32, kind="ExternalInput")
    gbr_d = nc.dram_tensor("gbr", [2, D], F32, kind="ExternalInput")
    bb_d = nc.dram_tensor("bb", [2, D], F32, kind="ExternalInput")
    pc_d = nc.dram_tensor("pc", [128, 8], F32, kind="ExternalInput")
    ct_d = nc.dram_tensor("ct", [128, 512], F32, kind="ExternalInput")
    st_d = nc.dram_tensor("st", [128, 512], F32, kind="ExternalInput")

    kb_i = nc.dram_tensor("kb_i", [TOK, D], BF16)
    kb_o = nc.dram_tensor("kb_o", [2 * TOK, D], BF16)
    vb_i = nc.dram_tensor("vb_i", [TOK, NH * 65], BF16)
    vb_o = nc.dram_tensor("vb_o", [2 * TOK, NH * 65], BF16)
    vc_i = nc.dram_tensor("vc_i", [TOK, D], BF16)
    vc_o = nc.dram_tensor("vc_o", [2 * TOK, D], BF16)
    pp_i = nc.dram_tensor("pp_i", [4, TOK], F32)
    pp_o = nc.dram_tensor("pp_o", [8, TOK], F32)
    mx_i = nc.dram_tensor("mx_i", [1, 8], F32)
    mx_o = nc.dram_tensor("mx_o", [1, 8], F32)

    ident_d = nc.inline_tensor(np.eye(128, dtype=bf16), name="identb")
    ones2_np = np.zeros((128, 2), dtype=bf16)
    ones2_np[0:64, 0] = 1
    ones2_np[64:128, 1] = 1
    ones2_d = nc.inline_tensor(ones2_np, name="ones2")
    onescol_d = nc.inline_tensor(np.ones((128, 1), dtype=bf16), name="onescol")
    onesrow_d = nc.inline_tensor(np.ones((1, 128), dtype=bf16), name="onesrow")
    identf_d = nc.inline_tensor(np.eye(128, dtype=np.float32), name="identf")
    onesrowf_d = nc.inline_tensor(np.ones((1, 128), dtype=np.float32), name="onesrowf")
    tri_np = np.ascontiguousarray(
        np.tril(np.ones((128, 128), dtype=np.float32)).T.astype(bf16))
    tri_d = nc.inline_tensor(tri_np, name="trikq")
    ska = np.zeros((5, 13), dtype=bf16)
    for i in range(3):
        ska[i, i] = 1; ska[i, 6 + i] = 1
    ska[3, 11] = 1; ska[4, 9] = 1; ska[4, 10] = 1
    skb = np.zeros((4, 13), dtype=bf16)
    for i in range(3):
        skb[i, 3 + i] = 1
    skb[3, 12] = 1
    sqa = np.zeros((5, 13), dtype=bf16)
    for i in range(3):
        sqa[i, i] = -2; sqa[i, 3 + i] = -2
    sqa[3, 9] = 1; sqa[4, 11] = 1; sqa[4, 12] = 1
    sqb = np.zeros((4, 13), dtype=bf16)
    for i in range(3):
        sqb[i, 6 + i] = -2
    sqb[3, 10] = 1
    selka_d = nc.inline_tensor(ska, name="selka")
    selkb_d = nc.inline_tensor(skb, name="selkb")
    selqa_d = nc.inline_tensor(sqa, name="selqa")
    selqb_d = nc.inline_tensor(sqb, name="selqb")
    s34a = np.zeros((3, 4), dtype=np.float32)
    for i in range(3):
        s34a[i, i] = 1
    s34b = np.zeros((3, 4), dtype=np.float32)
    s34b[:, 3] = 1
    sel34a_d = nc.inline_tensor(s34a, name="sel34a")
    sel34b_d = nc.inline_tensor(s34b, name="sel34b")

    dump_t = {}
    for tag, shape, dt in [
        ("xn", (128, D), BF16), ("qtm", (128, 1024), BF16), ("qfm", (128, 2048), BF16),
        ("kfm", (128, 4096), BF16), ("es", (128, 1024), BF16), ("ao", (128, 2048), BF16),
        ("x1", (TOK, D), F32), ("x2", (TOK, D), F32), ("x3", (TOK, D), F32),
        ("tw", (128, 1024), BF16), ("pos", (16, TOK), F32), ("kb13", (16, 4096), BF16),
        ("esb", (128, 4352), BF16), ("scal", (1, 8), F32), ("aofb", (128, 2048), BF16),
        ("vga", (128, 260), BF16), ("xraw", (128, 1024), F32),
        ("stats", (128, 16), F32), ("lnm", (128, 16), F32), ("lnr", (128, 16), F32),
        ("sumsq", (128, 16), F32), ("qsc", (128, 1024), F32),
    ]:
        if tag in dumps:
            dump_t[tag] = nc.dram_tensor("d_" + tag, list(shape), dt, kind="ExternalOutput")

    names = ("sX sCT sW1 sWO sW2 sLN sTR sFM sPRJ sRP sKO sVO sAG sGI sKF sNC sQK "
             "sEX sPV sAOv sAT sAF sOP sRES sDP sMX sMXc sTW sWF sPP sPG sKB sDM "
             "sRL sRS sEB sPB sOB sQT sST sSB sGB "
             "sx0 sx1 sx2 sx3 sk0 sk1 sv0 sv1 sBB").split()
    SM = {n: Sem(n) for n in names}

    def W(sem, v=None):
        return (SM[sem], SM[sem].n if v is None else v)

    def I(sem):
        return SM[sem]

    with (
        nc.sbuf_tensor("XR", [128, NT * 1024], F32) as XR,
        nc.sbuf_tensor("FM", [128, 16384], BF16) as FM,
        nc.sbuf_tensor("GA", [128, 16640], BF16) as GA,
        nc.sbuf_tensor("TWS", [128, 8192], BF16) as TWS,
        nc.sbuf_tensor("WB", [128, 8192], BF16) as WB,
        nc.sbuf_tensor("AOS", [128, 10240], BF16) as AOS,
        nc.sbuf_tensor("PO", [128, 6912], F32) as PO,
        nc.psum_tensor("PS", [128, 4096], F32) as PS,
    ):
        for s in SM.values():
            s.h = nc.alloc_semaphore(s.name)

        def xr(t):
            return XR[:, 1024 * t:1024 * t + 1024]

        def fm(ic):
            return FM[:, 2048 * ic:2048 * ic + 2048]

        def qtm(t):
            return GA[:, 1024 * t:1024 * t + 1024]

        def katm(j):
            return GA[:, 256 * j:256 * j + 256]

        def vga(j):
            return GA[:, 8192 + 260 * j:8192 + 260 * j + 260]

        def kfm(i):
            return TWS[:, 4096 * i:4096 * i + 4096]

        def vbt(j):
            if j < 16:
                return GA[:, 1024 * j:1024 * j + 1024]
            if j < 24:
                return WB[:, 1024 * (j - 16):1024 * (j - 16) + 1024]
            return AOS[:, 1024 * (j - 24):1024 * (j - 24) + 1024]

        def bk(i, w=512):
            return PS[:, 512 * i:512 * i + w]

        ct_s = PO[:, 0:512]
        st_s = PO[:, 512:1024]
        xnrot = [PO[:, 1024 + 512 * i:1536 + 512 * i].bitcast(BF16) for i in range(2)]
        ES = [PO[:, 2048 + 512 * i:2560 + 512 * i].bitcast(BF16) for i in range(2)]
        QSC = PO[:, 2048:3072]
        st6 = PO[:, 3072:3084]
        mv = PO[:, 3084:3086]
        csc = PO[:, 3088:3112]
        dbgS = PO[:, 6296:6312]
        ln_m = PO[:, 3112:3128]
        ln_r = PO[:, 3128:3144]
        smax = PO[:, 3144:3145]
        mx_s = PO[:, 3152:3160]
        rQ8 = PO[:, 3160:3168]
        negC = PO[:, 3168:3172]
        t1r = PO[:, 3200:3712]
        t2r = PO[:, 3712:4224]
        sqscr = PO[:, 3200:3456].bitcast(BF16)
        GREP = PO[:, 4224:5248]
        BREP = PO[:, 5248:6272]
        kvrot = [PO[:, 4224 + 520 * i:4744 + 520 * i].bitcast(BF16) for i in range(2)]
        kt_rot = [PO[:, 5264 + 512 * i:5776 + 512 * i].bitcast(BF16) for i in range(2)]
        pc_s = PO[:, 6288:6296]
        rsumb = PO[:, 6352:6368]
        IDF = PO[:, 6656:6784]
        ONRF = PO[0:1, 6784:6912]
        mkA = PO[:, 6400:6464].bitcast(BF16)
        mkB = PO[:, 6464:6528].bitcast(BF16)
        tri_s = PO[:, 6528:6592].bitcast(BF16)
        POS = PO[:, 0:2048]
        POSG = PO[:, 2048:2560]
        khb = PO[:, 2560:2816].bitcast(BF16)
        klb = PO[:, 2816:3072].bitcast(BF16)
        SEL34A = PO[:, 6312:6316]
        SEL34B = PO[:, 6316:6320]
        SELKA = AOS[:, 10144:10157]
        SELKB = AOS[:, 10160:10173]
        SELQA = AOS[:, 10176:10189]
        SELQB = AOS[:, 10208:10221]
        PSTG = PO[:, 0:2176].bitcast(BF16)
        QB13 = PO[:, 3200:4224].bitcast(BF16)
        KB13 = PO[:, 4224:6272].bitcast(BF16)
        ONES3F = PO[:, 3184:3185]

        ID = AOS[:, 9984:10112]
        ONR2 = AOS[0:1, 8704:8832]
        BVB = AOS[0:1, 8832:9856]
        BOB = AOS[1:2, 8832:9856]

        def aotm(p, t):
            return AOS[:, 2048 * p + 128 * t:2048 * p + 128 * t + 128]

        def aofm(i):
            return AOS[:, 4096 + 2048 * i:4096 + 2048 * i + 2048]

        TRP = [PS[:, 2048 + 512 * i:2560 + 512 * i].bitcast(BF16) for i in range(2)]

        def _finish(_ctx):
            for t in range(NT):
                E.op("sync", lambda e, t=t: e.dma_start(
                    out=out_d[128 * t:128 * t + 128, :], in_=xr(t)),
                    waits=[W("sRES", I("sRES").n), W("sLN", I("sLN").n),
                           W("sFM", I("sFM").n), W("sRP", I("sRP").n),
                           W("sQT", I("sQT").n)],
                    inc=I("sX"), dma=True)
            E.op("sync", lambda e: e.nop(), waits=[W("sX", I("sX").n), W("sDP", I("sDP").n),
                                                   W("sk0", I("sk0").n), W("sk1", I("sk1").n),
                                                   W("sv0", I("sv0").n), W("sv1", I("sv1").n),
                                                   W("sAG", I("sAG").n)])
            E.emit(nc)

        def _preload_tables(e):
            return e.add_instruction(mybir.InstLoadActFuncSet(
                name=nc.get_next_instruction_name(), act_func_set_id=6, ins=[], outs=[]))
        E.op("scalar", _preload_tables)
        for t in range(NT):
            sx = "sx%d" % (t % 4)
            E.op("sync", lambda e, t=t: e.dma_start(out=xr(t), in_=x_d[128 * t:128 * t + 128, :]),
                 waits=([W(sx, 16 * (t // 4))] if t >= 4 else []), inc=I(sx), dma=True)
        for dst, src in [(ct_s, ct_d), (st_s, st_d), (ID, ident_d), (tri_s, tri_d),
                         (pc_s, pc_d)]:
            E.op("sync", lambda e, d=dst, s=src: e.dma_start(out=d, in_=s[:, :]),
                 inc=I("sCT"), dma=True)
        ON2 = AOS[:, 10112:10114]
        ONC = AOS[:, 10114:10115]
        E.op("sync", lambda e: e.dma_start(out=ON2, in_=ones2_d[:, :]), inc=I("sCT"), dma=True)
        E.op("sync", lambda e: e.dma_start(out=ONC, in_=onescol_d[:, :]), inc=I("sCT"), dma=True)
        E.op("sync", lambda e: e.dma_start(out=ONR2, in_=onesrow_d[:, :]), inc=I("sCT"), dma=True)
        E.op("sync", lambda e: e.dma_start(out=IDF, in_=identf_d[:, :]), inc=I("sCT"), dma=True)
        E.op("sync", lambda e: e.dma_start(out=ONRF, in_=onesrowf_d[:, :]), inc=I("sCT"), dma=True)
        for dst, src in [(SELKA[0:5, :], selka_d), (SELKB[0:4, :], selkb_d),
                         (SELQA[0:5, :], selqa_d), (SELQB[0:4, :], selqb_d),
                         (SEL34A[0:3, :], sel34a_d), (SEL34B[0:3, :], sel34b_d)]:
            E.op("sync", lambda e, d=dst, s=src: e.dma_start(out=d, in_=s[:, :]),
                 inc=I("sCT"), dma=True)
        E.op("gpsimd", lambda e: e.dma_start(out=BVB, in_=bb_d[0:1, :]), inc=I("sBB"), dma=True)
        E.op("gpsimd", lambda e: e.dma_start(out=BOB, in_=bb_d[1:2, :]), inc=I("sBB"), dma=True)
        E.op("gpsimd", lambda e: e.dma_start(out=WB[:, :], in_=wq_d[:, :]), inc=I("sW1"), dma=True)
        E.op("gpsimd", lambda e: e.dma_start(out=TWS[:, :], in_=wk_d[:, :]), inc=I("sW2"), dma=True)
        ct_done = I("sCT").n

        def dump(tag, ap, waits):
            if tag in dump_t:
                E.op("sync", lambda e, t=dump_t[tag], a=ap: e.dma_start(out=t.ap(), in_=a),
                     waits=waits, inc=I("sDP"), dma=True)

        def dump_x(tag):
            if tag in dump_t:
                for t in range(NT):
                    E.op("sync", lambda e, t=t: e.dma_start(
                        out=dump_t[tag].ap()[128 * t:128 * t + 128, :], in_=xr(t)),
                        waits=[W("sRES")], inc=I("sDP"), dma=True)

        def emit_preduce(src_col, dst11, op, waits, inc_sem, cnt=128):
            E.op("tensor", lambda e: e.transpose(PS[0:1, 3584:3584 + cnt],
                                                 src_col, IDF[0:cnt, 0:cnt]),
                 waits=list(waits) + [W("sCT", ct_done)], inc=I("sNC"))
            E.op("vector", lambda e: e.tensor_reduce(dst11, PS[0:1, 3584:3584 + cnt],
                                                     AX.X, op), waits=[W("sNC")],
                 inc=inc_sem)

        def emit_bcast(src11, dst_col, waits, inc_sem, width=1):
            base = 3584 if width <= 512 else 3072
            def mm(e):
                inst = None
                for o in range(0, width, 512):
                    wdt = min(512, width - o)
                    inst = e.matmul(PS[:, base + o:base + o + wdt], ONRF,
                                    src11[0:1, o:o + wdt], start=True, stop=True)
                return inst
            E.op("tensor", mm, waits=list(waits) + [W("sCT", ct_done)], inc=I("sNC"))
            E.op("vector", lambda e: e.tensor_copy(dst_col, PS[:, base:base + width]),
                 waits=[W("sNC")], inc=inc_sem)

        def emit_ln_tile(t, kind, out_tile, in_waits):
            E.op("vector", lambda e, t=t: e.scalar_tensor_tensor(
                QSC, xr(t), 1.0, xr(t), ALU.mult, ALU.mult), waits=in_waits,
                inc=I("sST"))
            E.op("vector", lambda e, t=t: e.reduce_sum(csc[:, 9:10], xr(t), axis=AX.X),
                 inc=I("sST"))
            E.op("vector", lambda e: e.reduce_sum(csc[:, 10:11], QSC, axis=AX.X),
                 waits=[W("sST")], inc=I("sST"))
            if "sumsq" in dump_t and t < 16:
                E.op("vector", lambda e, t=t: e.tensor_copy(
                    dbgS[:, t:t + 1], csc[:, 10:11]), waits=[W("sST")])
            E.op("vector", lambda e: e.tensor_scalar_mul(mv[:, 0:1], csc[:, 9:10],
                                                         1.0 / 1024.0), waits=[W("sST")])
            E.op("vector", lambda e: e.tensor_scalar_mul(csc[:, 11:12], csc[:, 10:11],
                                                         1.0 / 1024.0), inc=I("sST"))
            E.op("vector", lambda e: e.scalar_tensor_tensor(
                csc[:, 12:13], mv[:, 0:1], 1.0, mv[:, 0:1], ALU.mult, ALU.mult),
                waits=[W("sST")], inc=I("sST"))
            E.op("vector", lambda e: e.tensor_sub(mv[:, 1:2], csc[:, 11:12], csc[:, 12:13]),
                 waits=[W("sST")], inc=I("sST"))
            E.op("vector", lambda e: e.tensor_scalar_add(csc[:, 0:1], mv[:, 1:2], EPS),
                 waits=[W("sST")], inc=I("sST"))
            E.op("scalar", lambda e: e.activation(csc[:, 1:2], csc[:, 0:1], AF.Ln),
                 waits=[W("sST")])
            E.op("scalar", lambda e: e.activation(csc[:, 2:3], csc[:, 1:2], AF.Exp, scale=-0.5),
                 inc=I("sSB"))
            if kind == "ffn":
                E.op("vector", lambda e: e.scalar_tensor_tensor(
                    csc[:, 3:4], csc[:, 2:3], 1.0, csc[:, 2:3], ALU.mult, ALU.mult),
                    waits=[W("sSB")], inc=I("sST"))
                E.op("vector", lambda e: e.tensor_mul(csc[:, 4:5], csc[:, 3:4], mv[:, 1:2]),
                     waits=[W("sST")], inc=I("sST"))
                E.op("vector", lambda e: e.tensor_scalar_add(csc[:, 4:5], csc[:, 4:5], EPS),
                     waits=[W("sST")], inc=I("sST"))
                E.op("scalar", lambda e: e.activation(csc[:, 5:6], csc[:, 4:5], AF.Ln),
                     waits=[W("sST")])
                E.op("scalar", lambda e: e.activation(csc[:, 6:7], csc[:, 5:6], AF.Exp,
                                                      scale=-0.5), inc=I("sSB"))
                E.op("vector", lambda e: e.tensor_mul(csc[:, 7:8], csc[:, 2:3], csc[:, 6:7]),
                     waits=[W("sSB")], inc=I("sST"))
                E.op("vector", lambda e, t=t: e.tensor_copy(ln_m[:, t:t + 1], mv[:, 0:1]))
                E.op("vector", lambda e, t=t: e.tensor_copy(ln_r[:, t:t + 1], csc[:, 7:8]),
                     waits=[W("sST")], inc=I("sST"))
                rcol = csc[:, 7:8]
            else:
                rcol = csc[:, 2:3]
            if out_tile is not None:
                if t < 16:
                    E.op("vector", lambda e, t=t: e.tensor_copy(ln_m[:, t:t + 1],
                                                                mv[:, 0:1]), waits=[W("sSB")])
                    E.op("vector", lambda e, t=t, r=rcol: e.tensor_copy(ln_r[:, t:t + 1], r))
                E.op("vector", lambda e, t=t, o=out_tile, r=rcol: e.tensor_scalar(
                    o, xr(t), mv[:, 0:1], r, ALU.subtract, ALU.mult),
                    waits=[W("sSB"), W("sST")], inc=I("sLN"))

        tr_par = [0]

        def emit_transpose8(src, dest_fn, waits_t, npiece=8):
            pi = tr_par[0] % 2
            tr_par[0] += 1
            fr = tr_par[0] - 2

            def tr(e, s=src, pi=pi, n=npiece):
                inst = None
                for ic in range(n):
                    inst = e.transpose(TRP[pi][:, 128 * ic:128 * ic + 128],
                                       s[:, 128 * ic:128 * ic + 128], ID)
                return inst
            E.op("tensor", tr, waits=list(waits_t) + ([W("sFM", fr)] if fr > 0 else []),
                 inc=I("sTR"))

            def ev(e, pi=pi, n=npiece):
                inst = None
                for ic in range(n):
                    inst = e.tensor_copy(dest_fn(ic), TRP[pi][:, 128 * ic:128 * ic + 128])
                return inst
            E.op("vector", ev, waits=[W("sTR")], inc=I("sFM"))

        def emit_proj_tile(t, wsrc, wwaits, pb, bias_row=None, fm_wait=None):
            def mm(e, t=t, w=wsrc, pb=pb, br=bias_row):
                inst = None
                for oh in range(2):
                    for ic in range(8):
                        inst = e.matmul(bk(pb + oh), fm(ic)[:, 128 * t:128 * t + 128],
                                        w[:, 1024 * ic + 512 * oh:1024 * ic + 512 * oh + 512],
                                        start=(ic == 0), stop=(ic == 7 and br is None))
                    if br is not None:
                        inst = e.matmul(bk(pb + oh), ONR2, br[0:1, 512 * oh:512 * oh + 512],
                                        start=False, stop=True)
                return inst
            E.op("tensor", mm, waits=list(wwaits) + ([fm_wait] if fm_wait else []),
                 inc=I("sPRJ"))

        def rope_slices(ap2d):
            r = ap2d.rearrange("p (h half i) -> p h half i", h=16, half=2, i=32)
            return r[:, :, 0, :], r[:, :, 1, :]

        def emit_rope_evac(t, pb, dst):
            ps2 = PS[:, 512 * pb:512 * pb + 1024]
            xrr, xii = rope_slices(ps2)
            dr, di = rope_slices(dst)
            cb = ct_s[:, 32 * t:32 * t + 32].unsqueeze(1).broadcast_to([128, 16, 32])
            sb = st_s[:, 32 * t:32 * t + 32].unsqueeze(1).broadcast_to([128, 16, 32])
            t1 = t1r.rearrange("p (h i) -> p h i", h=16)
            t2 = t2r.rearrange("p (h i) -> p h i", h=16)
            E.op("vector", lambda e: e.tensor_tensor(t1, xrr, cb, ALU.mult),
                 waits=[W("sPRJ"), W("sCT", ct_done)])
            E.op("vector", lambda e: e.tensor_tensor(t2, xii, sb, ALU.mult))
            E.op("vector", lambda e: e.tensor_sub(dr, t1, t2))
            E.op("vector", lambda e: e.tensor_tensor(t1, xrr, sb, ALU.mult))
            E.op("vector", lambda e: e.tensor_tensor(t2, xii, cb, ALU.mult))
            E.op("vector", lambda e: e.tensor_add(di, t1, t2), inc=I("sRP"))

        for t in range(NT):
            emit_ln_tile(t, "plain", xnrot[t % 2],
                         [W("sx0", 64), W("sx1", 64), W("sx2", 64), W("sx3", 64),
                          W("sCT", ct_done)])
            if s1cut not in ('ln0', 'ln0b'):
                emit_transpose8(xnrot[t % 2],
                                (lambda tt: (lambda ic: fm(ic)[:, 128 * tt:128 * tt + 128]))(t),
                                [W("sLN")])
        xn_done = I("sFM").n
        dump("xn", FM[:, 0:1024], [W("sFM", xn_done)])
        dump("xraw", xr(0), [W("sLN", 1)])
        dump("stats", PO[:, 3072:3088], [W("sLN", 1)])

        if s1cut in ("ln", "ln0", "ln0b"):
            dump("lnm", ln_m, [W("sLN", I("sLN").n)])
            dump("lnr", ln_r, [W("sLN", I("sLN").n)])
            dump("sumsq", dbgS, [W("sLN", I("sLN").n)])
            dump("qsc", QSC, [W("sLN", I("sLN").n)])
            _finish(locals())
            return nc
        for t in range(NT):
            pb = 2 * (t % 2)
            if t >= 2:
                E.op("tensor", lambda e, v=I("sRP").n - 1: e.nop(), waits=[W("sRP", I("sRP").n - 1)])
            emit_proj_tile(t, WB, [W("sW1", 16)], pb, fm_wait=W("sFM", xn_done))
            emit_rope_evac(t, pb, qtm(t))
        dump("qtm", qtm(0), [W("sRP", 1)])
        E.op("gpsimd", lambda e: e.dma_start(out=WB[:, :], in_=wv_d[:, :]),
             waits=[W("sRP", NT)], inc=I("sW1"), dma=True)
        for t in range(NT):
            pb = 2 * (t % 2)
            E.op("tensor", lambda e: e.nop(), waits=[W("sRP", NT + t - 1)])
            emit_proj_tile(t, TWS, [W("sW2", 16)], pb, fm_wait=W("sFM", xn_done))
            if t >= 2:
                E.op("vector", lambda e, t=t: e.memset(csc[:, 8:9], 0.0),
                     waits=[W("sk%d" % (t % 2), 16 * (t // 2))])
            emit_rope_evac(t, pb, kt_rot[t % 2])
            E.op("sync", lambda e, t=t: e.dma_start(out=kb_i[128 * t:128 * t + 128, :],
                                                    in_=kt_rot[t % 2]),
                 waits=[W("sRP")], inc=I("sk%d" % (t % 2)), dma=True)
        for t in range(NT):
            pb = 2 * (t % 2)
            E.op("tensor", lambda e: e.nop(), waits=[W("sRP", 2 * NT + t - 1)])
            emit_proj_tile(t, WB, [W("sW1", 32)], pb, fm_wait=W("sFM", xn_done))

            def vev(e, t=t, pb=pb):
                ps2 = PS[:, 512 * pb:512 * pb + 1024]
                dst = kvrot[t % 2].rearrange("p (g c) -> p g c", g=16, c=65)
                e.tensor_copy(dst[:, :, 0:64], ps2.rearrange("p (g c) -> p g c", g=16, c=64))
                return e.memset(dst[:, :, 64:65], 1.0)
            vw = [W("sPRJ")]
            if t >= 2:
                vw.append(W("sv%d" % (t % 2), 16 * (t // 2)))
            E.op("vector", vev, waits=vw, inc=I("sRP"))
            E.op("sync", lambda e, t=t: e.dma_start(out=vb_i[128 * t:128 * t + 128, :],
                                                    in_=kvrot[t % 2]),
                 waits=[W("sRP")], inc=I("sv%d" % (t % 2)), dma=True)
        rp_all = I("sRP").n
        for t in range(NT):
            emit_transpose8(qtm(t),
                            (lambda tt: (lambda ic: fm(ic)[:, 128 * tt:128 * tt + 128]))(t),
                            [W("sRP", rp_all)])
        E.op("vector", lambda e: e.memset(csc[:, 8:9], 0.0), inc=I("sQT"))
        qt_done = I("sQT").n
        dump("qfm", fm(0), [W("sQT", qt_done)])

        if s1cut == "proj":
            _finish(locals())
            return nc
        E.op("gpsimd", lambda e: e.collective_compute(
            "AllGather", ALU.bypass, replica_groups=PAIRS,
            ins=[kb_i.ap().opt()], outs=[kb_o.ap().opt()]),
            waits=[W("sk0", I("sk0").n), W("sk1", I("sk1").n)], inc=I("sAG"))
        E.op("gpsimd", lambda e: e.collective_compute(
            "AllGather", ALU.bypass, replica_groups=PAIRS,
            ins=[vb_i.ap().opt()], outs=[vb_o.ap().opt()]),
            waits=[W("sv0", I("sv0").n), W("sv1", I("sv1").n)], inc=I("sAG"))

        if s1cut == "gather0":
            _finish(locals())
            return nc
        ngroups = 1 if s1cut == "g0" else NG
        for g in range(ngroups):
            gw = [W("sAG", 1), W("sQT", qt_done)]
            if g:
                gw.append(W("sPV", I("sPV").n))
            E.op("sync", lambda e, g=g: e.dma_start(
                out=GA[:, 0:8192].rearrange("p (c w) -> p c w", c=CHK),
                in_=kb_o[:, 256 * g:256 * g + 256].rearrange("(c p) w -> p c w", p=128)),
                waits=gw, inc=I("sGI"), dma=True)
            E.op("sync", lambda e, g=g: e.dma_start(
                out=GA[:, 8192:16512].rearrange("p (c w) -> p c w", c=CHK),
                in_=vb_o[:, 260 * g:260 * g + 260].rearrange("(c p) w -> p c w", p=128)),
                waits=[W("sAG", 2)], inc=I("sGI"), dma=True)
            gi_done = I("sGI").n
            aov_now = I("sAOv").n
            for j in range(CHK):
                emit_transpose8(katm(j),
                                (lambda jj: (lambda ic: kfm(ic)[:, 128 * jj:128 * jj + 128]))(j),
                                [W("sGI", gi_done), W("sRP", rp_all),
                                 W("sAOv", aov_now)], npiece=2)
            E.op("vector", lambda e: e.memset(csc[:, 8:9], 0.0), inc=I("sKF"))
            kf_done = I("sKF").n
            if g == 0:
                dump("kfm", kfm(0), [W("sKF", kf_done)])
                dump("vga", vga(0), [W("sGI", gi_done)])

            def sq_piece(src, out_run, first, wv=()):
                E.op("vector", lambda e, s=src: e.scalar_tensor_tensor(
                    sqscr, s, 1.0, s, ALU.mult, ALU.mult), waits=wv,
                    inc=I("sNC"))
                E.op("tensor", lambda e: e.matmul(PS[0:2, 3072:3584], ON2, sqscr,
                                                  start=True, stop=True),
                     waits=[W("sNC"), W("sRES", I("sRES").n), W("sCT", ct_done)], inc=I("sNC"))
                E.op("vector", lambda e: e.reduce_max(csc[0:2, 9:10], PS[0:2, 3072:3584],
                                                      axis=AX.X), waits=[W("sNC")],
                     inc=I("sST"))
                if first:
                    E.op("vector", lambda e: e.tensor_copy(csc[0:2, 10:11], csc[0:2, 9:10]),
                         waits=[W("sST")], inc=I("sNC"))
                else:
                    E.op("vector", lambda e: e.tensor_max(csc[0:2, 10:11], csc[0:2, 10:11],
                                                          csc[0:2, 9:10]),
                         waits=[W("sST")], inc=I("sNC"))

            first = True
            for ti in (2 * g, 2 * g + 1):
                for piece in range(4):
                    sq_piece(fm(ti)[:, 512 * piece:512 * piece + 512], None, first,
                             wv=[W("sQT", qt_done)])
                    first = False
            emit_preduce(csc[0:2, 10:11], csc[0:1, 12:13], ALU.max,
                         [W("sNC")], I("sNC"), cnt=2)
            first = True
            for ti in range(2):
                for piece in range(8):
                    sq_piece(kfm(ti)[:, 512 * piece:512 * piece + 512], None, first,
                             wv=[W("sKF", kf_done)])
                    first = False
            emit_preduce(csc[0:2, 10:11], csc[0:1, 11:12], ALU.max,
                         [W("sNC")], I("sNC"), cnt=2)
            E.op("vector", lambda e: e.tensor_mul(csc[0:1, 13:14], csc[0:1, 11:12],
                                                  csc[0:1, 12:13]), waits=[W("sNC")],
                 inc=I("sST"))
            E.op("scalar", lambda e: e.activation(csc[0:1, 14:15], csc[0:1, 13:14], AF.Ln),
                 waits=[W("sST")])
            E.op("scalar", lambda e: e.activation(csc[0:1, 13:14], csc[0:1, 14:15], AF.Exp,
                                                  scale=0.5), inc=I("sSB"))
            E.op("vector", lambda e: e.tensor_scalar(csc[0:1, 14:15], csc[0:1, 13:14],
                                                     0.12875, 1.0, ALU.mult, ALU.add),
                 waits=[W("sSB")], inc=I("sST"))
            E.op("vector", lambda e: e.tensor_scalar(csc[0:1, 14:15], csc[0:1, 14:15],
                                                     60.0, -1.0, ALU.min, ALU.mult),
                 waits=[W("sST")], inc=I("sNC"))
            emit_bcast(csc[0:1, 14:15], negC[:, g:g + 1], [W("sNC")], I("sNC"))
            nc_done = I("sNC").n

            for p in range(2):
                for qb in range(4):
                    pvb = 4 + 2 * (qb % 2)
                    for c in range(CHK):
                        pi = c % 2
                        qkb = 2 * pi

                        def mmqk(e, g=g, p=p, qb=qb, c=c, qkb=qkb):
                            kf = kfm(p)
                            qf = fm(2 * g + p)
                            e.ldweights(kf[0:64, 128 * c:128 * c + 128], tile_position=(0, 0))
                            e.ldweights(kf[64:128, 128 * c:128 * c + 128],
                                        tile_position=(64, 0))
                            e.matmul(bk(qkb), kf[0:64, 128 * c:128 * c + 128],
                                     qf[0:64, 512 * qb:512 * qb + 512],
                                     start=True, stop=True, tile_position=(0, 0),
                                     skip_group_check=True)
                            return e.matmul(bk(qkb + 1), kf[64:128, 128 * c:128 * c + 128],
                                            qf[64:128, 512 * qb:512 * qb + 512],
                                            start=True, stop=True, tile_position=(64, 0),
                                            skip_group_check=True)
                        wq_ = [W("sKF", kf_done), W("sNC", nc_done)]
                        if I("sEX").n >= 1:
                            wq_.append(W("sEX", I("sEX").n - 1))
                        E.op("tensor", mmqk, waits=wq_, inc=I("sQK"))

                        ew = [W("sQK"), W("sNC", nc_done)]
                        if I("sPV").n >= 1:
                            ew.append(W("sPV", I("sPV").n - 1))
                        E.op("scalar", lambda e, pi=pi, qkb=qkb, g=g: e.activation(
                            ES[pi][:, :], PS[:, 512 * qkb:512 * qkb + 1024], AF.Exp,
                            bias=negC[:, g:g + 1], scale=0.125), waits=ew, inc=I("sEX"))

                        def mmpv(e, p=p, c=c, pi=pi, pvb=pvb):
                            inst = None
                            for h in range(2):
                                for qt in range(4):
                                    lhs = ES[pi][:, 512 * h + 128 * qt:512 * h + 128 * qt + 128]
                                    e.ldweights(lhs)
                                    inst = e.matmul(
                                        PS[:, 512 * (pvb + h) + 65 * qt:
                                           512 * (pvb + h) + 65 * qt + 65],
                                        lhs, vga(c)[:, 65 * (2 * p + h):65 * (2 * p + h) + 65],
                                        start=(c == 0 and qt == 0), stop=(c == CHK - 1),
                                        skip_group_check=True)
                            return inst
                        pw = [W("sEX"), W("sGI", gi_done), W("sAT", I("sAT").n)]
                        if I("sAOv").n >= 2:
                            pw.append(W("sAOv", I("sAOv").n - 2))
                        E.op("tensor", mmpv, waits=pw, inc=I("sPV"))

                    def pvrec(e, pvb=pvb):
                        inst = None
                        for h in range(2):
                            for qt in range(4):
                                acc = PS[:, 512 * (pvb + h) + 65 * qt:
                                         512 * (pvb + h) + 65 * qt + 65]
                                inst = e.reciprocal(rQ8[:, 4 * h + qt:4 * h + qt + 1],
                                                    acc[:, 64:65])
                        return inst
                    E.op("vector", pvrec, waits=[W("sPV")], inc=I("sAOv"))

                    def pvev(e, p=p, qb=qb, pvb=pvb):
                        inst = None
                        for h in range(2):
                            for qt in range(4):
                                acc = PS[:, 512 * (pvb + h) + 65 * qt:
                                         512 * (pvb + h) + 65 * qt + 65]
                                inst = e.tensor_scalar(
                                    aotm(p, 4 * qb + qt)[:, 64 * h:64 * h + 64],
                                    acc[:, 0:64], rQ8[:, 4 * h + qt:4 * h + qt + 1],
                                    None, ALU.mult)
                        return inst
                    E.op("vector", pvev, waits=[W("sAOv")], inc=I("sAOv"))
            ao_done = I("sAOv").n
            for p in range(2):
                for t in range(NT):
                    emit_transpose8(aotm(p, t),
                                    (lambda pp, tt: (lambda _:
                                     aofm(pp)[:, 128 * tt:128 * tt + 128]))(p, t),
                                    [W("sAOv", ao_done), W("sOP", I("sOP").n)], npiece=1)
            E.op("vector", lambda e: e.memset(csc[:, 8:9], 0.0), inc=I("sAT"))
            at_done = I("sAT").n
            if g == 0:
                dump("es", ES[1], [W("sEX", 2)])
                dump("ao", aofm(0), [W("sAT", at_done)])
            E.op("gpsimd", lambda e, g=g: e.dma_start(
                out=WB[:, 2048 * (g % 2):2048 * (g % 2) + 2048],
                in_=wo_d[:, 2048 * g:2048 * g + 2048]),
                waits=[W("sRP", rp_all), W("sOP", I("sOP").n)], inc=I("sWO"), dma=True)
            wo_done = I("sWO").n
            for t in range(NT):
                pb = 2 * (t % 2)

                def mmo(e, t=t, g=g, pb=pb):
                    inst = None
                    for oh in range(2):
                        for icc in range(2):
                            inst = e.matmul(bk(pb + oh), aofm(icc)[:, 128 * t:128 * t + 128],
                                            WB[:, 2048 * (g % 2) + 1024 * icc + 512 * oh:
                                               2048 * (g % 2) + 1024 * icc + 512 * oh + 512],
                                            start=(icc == 0), stop=(icc == 1))
                    return inst
                wts = [W("sAT", at_done), W("sWO", wo_done)]
                if t >= 2:
                    wts.append(W("sRES", I("sRES").n - 1))
                E.op("tensor", mmo, waits=wts, inc=I("sOP"))

                def resid(e, t=t, pb=pb):
                    inst = None
                    for oh in range(2):
                        inst = e.scalar_tensor_tensor(
                            xr(t)[:, 512 * oh:512 * oh + 512], bk(pb + oh), 1.0,
                            xr(t)[:, 512 * oh:512 * oh + 512], ALU.mult, ALU.add)
                    return inst
                E.op("vector", resid, waits=[W("sOP")], inc=I("sRES"))
        if s1cut == "g0":
            _finish(locals())
            return nc
        dump_x("x1")

        def emit_ternary():
            WF = WB.bitcast(F32)
            wfc = [WF[:, 1024 * i:1024 * i + 1024] for i in range(2)]
            absw = WF[:, 2048:3072]
            sgn = WF[:, 3072:4096]
            res1 = I("sRES").n
            sb_marks = []
            tw_marks = []
            for ic in range(8):
                wts = [W("sRES", res1), W("sOP", I("sOP").n)]
                if ic >= 2:
                    wts.append(W("sSB", sb_marks[ic - 2]))
                E.op("sync", lambda e, ic=ic: e.dma_start(
                    out=wfc[ic % 2], in_=wf_d[:, 1024 * ic:1024 * ic + 1024]),
                    waits=wts, inc=I("sWF"), dma=True)
                aw = [W("sWF")]
                if ic >= 1:
                    aw.append(W("sTW", tw_marks[ic - 1]))
                E.op("scalar", lambda e, ic=ic: e.activation(absw, wfc[ic % 2], AF.Abs),
                     waits=aw, inc=I("sSB"))
                sb_marks.append(I("sSB").n)
                E.op("vector", lambda e: e.reduce_sum(csc[:, 9:10], absw, axis=AX.X),
                     waits=[W("sSB")], inc=I("sST"))
                if ic == 0:
                    E.op("vector", lambda e: e.tensor_copy(csc[:, 10:11], csc[:, 9:10]),
                         waits=[W("sST")], inc=I("sTW"))
                else:
                    E.op("vector", lambda e: e.tensor_add(csc[:, 10:11], csc[:, 10:11],
                                                          csc[:, 9:10]),
                         waits=[W("sST")], inc=I("sTW"))
                tw_marks.append(I("sTW").n)
            emit_preduce(csc[:, 10:11], csc[0:1, 11:12], ALU.add, [W("sTW")], I("sTW"))
            E.op("vector", lambda e: e.tensor_scalar_mul(csc[0:1, 11:12], csc[0:1, 11:12],
                                                         0.7 / (1024.0 * 1024.0)),
                 waits=[W("sTW")], inc=I("sTW"))
            emit_bcast(csc[0:1, 11:12], csc[:, 16:17], [W("sTW")], I("sTW"))
            d_done = I("sTW").n
            sb_marks = []
            tw_marks = []
            for ic in range(8):
                wts = [W("sTW", d_done)]
                if ic >= 2:
                    wts.append(W("sTW", I("sTW").n - 1))
                E.op("sync", lambda e, ic=ic: e.dma_start(
                    out=wfc[ic % 2], in_=wf_d[:, 1024 * ic:1024 * ic + 1024]),
                    waits=wts, inc=I("sWF"), dma=True)
                aw = [W("sWF"), W("sTW", d_done if ic == 0 else tw_marks[ic - 1])]
                E.op("scalar", lambda e, ic=ic: e.activation(absw, wfc[ic % 2], AF.Abs),
                     waits=aw, inc=I("sSB"))
                E.op("scalar", lambda e, ic=ic: e.sign(sgn, wfc[ic % 2]), inc=I("sSB"))
                E.op("vector", lambda e: e.tensor_single_scalar(QSC, absw, csc[:, 16:17],
                                                                ALU.is_gt), waits=[W("sSB")])
                E.op("vector", lambda e, ic=ic: e.tensor_mul(
                    TWS[:, 1024 * ic:1024 * ic + 1024], sgn, QSC))
                E.op("vector", lambda e: e.tensor_mul(absw, absw, QSC))
                E.op("vector", lambda e: e.reduce_sum(csc[:, 9:10], absw, axis=AX.X),
                     inc=I("sST"))
                E.op("vector", lambda e: e.reduce_sum(csc[:, 14:15], QSC, axis=AX.X),
                     inc=I("sST"))
                if ic == 0:
                    E.op("vector", lambda e: e.tensor_copy(csc[:, 10:11], csc[:, 9:10]),
                         waits=[W("sST")])
                    E.op("vector", lambda e: e.tensor_copy(csc[:, 15:16], csc[:, 14:15]),
                         inc=I("sTW"))
                else:
                    E.op("vector", lambda e: e.tensor_add(csc[:, 10:11], csc[:, 10:11],
                                                          csc[:, 9:10]), waits=[W("sST")])
                    E.op("vector", lambda e: e.tensor_add(csc[:, 15:16], csc[:, 15:16],
                                                          csc[:, 14:15]), inc=I("sTW"))
                tw_marks.append(I("sTW").n)
            emit_preduce(csc[:, 10:11], csc[0:1, 9:10], ALU.add, [W("sTW")], I("sTW"))
            emit_preduce(csc[:, 15:16], csc[0:1, 14:15], ALU.add, [W("sTW")], I("sTW"))
            E.op("vector", lambda e: e.tensor_scalar_max(csc[0:1, 14:15], csc[0:1, 14:15],
                                                         1.0), waits=[W("sTW")], inc=I("sST"))
            E.op("vector", lambda e: e.reciprocal(csc[0:1, 15:16], csc[0:1, 14:15]),
                 waits=[W("sST")], inc=I("sST"))
            E.op("vector", lambda e: e.tensor_mul(csc[0:1, 17:18], csc[0:1, 9:10],
                                                  csc[0:1, 15:16]),
                 waits=[W("sST")], inc=I("sTW"))

        def emit_ffn(which):
            res_base = I("sRES").n
            E.op("sync", lambda e: e.dma_start(out=GREP[0:1, :], in_=gbr_d[0:1, :]),
                 waits=[W("sKB", I("sKB").n), W("sRES", res_base)], inc=I("sGB"), dma=True)
            E.op("sync", lambda e: e.dma_start(out=BREP[0:1, :], in_=gbr_d[1:2, :]),
                 inc=I("sGB"), dma=True)
            emit_bcast(GREP[0:1, :], GREP, [W("sGB")], I("sGB"), width=1024)
            emit_bcast(BREP[0:1, :], BREP, [W("sGB", I("sGB").n)], I("sGB"), width=1024)
            gb_done = I("sGB").n
            for t in range(NT):
                emit_ln_tile(t, "ffn", None, [W("sRES", res_base)])
                E.op("vector", lambda e, t=t: e.tensor_scalar(
                    QSC, xr(t), ln_m[:, t:t + 1], ln_r[:, t:t + 1], ALU.subtract, ALU.mult),
                    waits=[W("sST"), W("sGB", gb_done), W("sEX", I("sEX").n)])
                E.op("vector", lambda e: e.tensor_mul(QSC, QSC, GREP))
                E.op("vector", lambda e: e.tensor_add(QSC, QSC, BREP))
                E.op("vector", lambda e: e.reduce_max(csc[:, 9:10], QSC, axis=AX.X,
                                                      apply_absolute_value=True),
                     inc=I("sST"))
                if t == 0:
                    E.op("vector", lambda e: e.tensor_copy(smax, csc[:, 9:10]),
                         waits=[W("sST")], inc=I("sMX"))
                else:
                    E.op("vector", lambda e: e.tensor_max(smax, smax, csc[:, 9:10]),
                         waits=[W("sST")], inc=I("sMX"))
            emit_preduce(smax, mx_s[0:1, 0:1], ALU.max, [W("sMX", I("sMX").n)], I("sMX"))
            E.op("gpsimd", lambda e: e.dma_start(out=mx_i[0:1, 0:1], in_=mx_s[0:1, 0:1]),
                 waits=[W("sMX", I("sMX").n)], inc=I("sMXc"), dma=True)
            E.op("gpsimd", lambda e: e.collective_compute(
                "AllReduce", ALU.max, replica_groups=ALL8,
                ins=[mx_i.ap().opt()], outs=[mx_o.ap().opt()]),
                waits=[W("sMXc")], inc=I("sAG"))
            E.op("gpsimd", lambda e: e.dma_start(out=mx_s[0:1, 1:2], in_=mx_o[0:1, 0:1]),
                 waits=[W("sAG", I("sAG").n)], inc=I("sMXc"), dma=True)
            E.op("vector", lambda e: e.tensor_scalar_max(mx_s[0:1, 2:3], mx_s[0:1, 1:2], 1e-8),
                 waits=[W("sMXc", I("sMXc").n)], inc=I("sST"))
            E.op("vector", lambda e: e.reciprocal(mx_s[0:1, 3:4], mx_s[0:1, 2:3]),
                 waits=[W("sST")], inc=I("sST"))
            E.op("vector", lambda e: e.tensor_scalar_mul(mx_s[0:1, 4:5], mx_s[0:1, 3:4],
                                                         127.0), waits=[W("sST")],
                 inc=I("sMX"))
            emit_bcast(mx_s[0:1, 4:5], csc[:, 18:19], [W("sMX", I("sMX").n)], I("sMX"))
            E.op("vector", lambda e: e.tensor_scalar_mul(mx_s[0:1, 5:6], mx_s[0:1, 2:3],
                                                         1.0 / 127.0),
                 waits=[W("sMX", I("sMX").n), W("sTW", I("sTW").n)], inc=I("sST"))
            E.op("vector", lambda e: e.tensor_mul(mx_s[0:1, 6:7], mx_s[0:1, 5:6],
                                                  csc[0:1, 17:18]),
                 waits=[W("sTW", I("sTW").n), W("sST")], inc=I("sMX"))
            emit_bcast(mx_s[0:1, 6:7], csc[:, 19:20], [W("sMX", I("sMX").n)], I("sMX"))
            so_done = I("sMX").n
            if which == 0:
                dump("tw", TWS[:, 0:1024], [W("sTW", I("sTW").n)])
                dump("scal", mx_s[0:1, 0:8], [W("sMX", so_done)])
            for t in range(NT):
                E.op("vector", lambda e, t=t: e.tensor_scalar(
                    QSC, xr(t), ln_m[:, t:t + 1], ln_r[:, t:t + 1], ALU.subtract, ALU.mult),
                    waits=[W("sMX", so_done)])
                E.op("vector", lambda e: e.tensor_mul(QSC, QSC, GREP))
                E.op("vector", lambda e: e.tensor_add(QSC, QSC, BREP))
                E.op("vector", lambda e: e.tensor_scalar(QSC, QSC, csc[:, 18:19], None,
                                                         ALU.mult))
                E.op("vector", lambda e: e.tensor_scalar(QSC, QSC, -127.0, 127.0,
                                                         ALU.max, ALU.min))
                E.op("vector", lambda e, t=t: e.tensor_scalar(
                    xnrot[t % 2], QSC, MAGIC, MAGIC, ALU.add, ALU.subtract), inc=I("sLN"))
                emit_transpose8(xnrot[t % 2],
                                (lambda tt: (lambda ic: fm(ic)[:, 128 * tt:128 * tt + 128]))(t),
                                [W("sLN")])
            fm_done = I("sFM").n
            for t in range(NT):
                pb = 2 * (t % 2)

                def mmf(e, t=t, pb=pb):
                    inst = None
                    for oh in range(2):
                        for ic in range(8):
                            inst = e.matmul(bk(pb + oh), fm(ic)[:, 128 * t:128 * t + 128],
                                            TWS[:, 1024 * ic + 512 * oh:
                                                1024 * ic + 512 * oh + 512],
                                            start=(ic == 0), stop=(ic == 7))
                    return inst
                wts = [W("sFM", fm_done), W("sTW", I("sTW").n)]
                if t >= 2:
                    wts.append(W("sRES", I("sRES").n - 1))
                E.op("tensor", mmf, waits=wts, inc=I("sOP"))

                def residf(e, t=t, pb=pb):
                    inst = None
                    for oh in range(2):
                        inst = e.scalar_tensor_tensor(
                            xr(t)[:, 512 * oh:512 * oh + 512], bk(pb + oh), csc[:, 19:20],
                            xr(t)[:, 512 * oh:512 * oh + 512], ALU.mult, ALU.add)
                    return inst
                E.op("vector", residf, waits=[W("sOP"), W("sMX", so_done)], inc=I("sRES"))

        def emit_attn_b():
            res_base = I("sRES").n
            E.op("vector", lambda e: e.tensor_scalar(mkA, tri_s, pc_s[:, 1:2], pc_s[:, 0:1],
                                                     ALU.mult, ALU.add),
                 waits=[W("sCT", ct_done), W("sRES", res_base)])
            E.op("vector", lambda e: e.tensor_scalar(mkB, tri_s, pc_s[:, 0:1], None,
                                                     ALU.mult), inc=I("sKB"))
            mk_done = I("sKB").n
            for t in range(NT):
                emit_ln_tile(t, "plain", xnrot[t % 2], [W("sRES", res_base)])
                emit_transpose8(xnrot[t % 2],
                                (lambda tt: (lambda ic: fm(ic)[:, 128 * tt:128 * tt + 128]))(t),
                                [W("sLN")])
            fm_done = I("sFM").n
            E.op("gpsimd", lambda e: e.dma_start(out=WB[:, :], in_=wvb_d[:, :]),
                 waits=[W("sRES", res_base), W("sOP", I("sOP").n)], inc=I("sW1"), dma=True)
            wvb_done = I("sW1").n
            for t in range(NT):
                pb = 2 * (t % 2)
                if t >= 2:
                    E.op("tensor", lambda e: e.nop(), waits=[W("sRP", I("sRP").n - 1)])
                emit_proj_tile(t, WB, [W("sW1", wvb_done), W("sCT", ct_done),
                                       W("sBB", 32)], pb,
                               bias_row=BVB, fm_wait=W("sFM", fm_done))
                bw = [W("sPRJ")]
                if t >= 2:
                    bw.append(W("sv%d" % (t % 2), 16 * (I("sv%d" % (t % 2)).n // 16)))
                E.op("vector", lambda e, t=t, pb=pb, bw=bw: e.tensor_copy(
                    kt_rot[t % 2], PS[:, 512 * pb:512 * pb + 1024]),
                    waits=bw, inc=I("sRP"))
                E.op("sync", lambda e, t=t: e.dma_start(
                    out=vc_i[128 * t:128 * t + 128, :], in_=kt_rot[t % 2]),
                    waits=[W("sRP")], inc=I("sv%d" % (t % 2)), dma=True)
            vo_done = None
            W3S = AOS[:, 10115:10139]
            E.op("gpsimd", lambda e: e.dma_start(out=W3S, in_=w3_d[:, :]),
                 inc=I("sW2"), dma=True)

            def mmp(e):
                inst = None
                for piece in range(4):
                    for ic in range(8):
                        inst = e.matmul(PS[0:3, 2048 + 512 * piece:2048 + 512 * piece + 512],
                                        W3S[:, 3 * ic:3 * ic + 3],
                                        fm(ic)[:, 512 * piece:512 * piece + 512],
                                        start=(ic == 0), stop=(ic == 7))
                return inst
            E.op("tensor", mmp, waits=[W("sFM", fm_done), W("sW2", I("sW2").n),
                                       W("sTR", I("sTR").n)], inc=I("sPP"))
            E.op("vector", lambda e: e.tensor_scalar(POS[0:3, :], PS[0:3, 2048:4096],
                                                     pc_s[:, 2:3][0:3, :], None, ALU.add),
                 waits=[W("sPP"), W("sCT", ct_done)], inc=I("sPP"))
            for piece in range(4):
                E.op("vector", lambda e, p=piece: e.scalar_tensor_tensor(
                    t1r[0:3, :], POS[0:3, 512 * p:512 * p + 512], 1.0,
                    POS[0:3, 512 * p:512 * p + 512], ALU.mult, ALU.mult),
                    waits=[W("sPP", I("sPP").n)], inc=I("sPP"))

                def mmn(e, p=piece):
                    e.matmul(PS[0:4, 2048 + 512 * p:2048 + 512 * p + 512],
                             SEL34A[0:3, 0:4], POS[0:3, 512 * p:512 * p + 512],
                             start=True, stop=False)
                    return e.matmul(PS[0:4, 2048 + 512 * p:2048 + 512 * p + 512],
                                    SEL34B[0:3, 0:4], t1r[0:3, :],
                                    start=False, stop=True)
                E.op("tensor", mmn, waits=[W("sPP"), W("sCT", ct_done)], inc=I("sPP"))
                E.op("vector", lambda e, p=piece: e.tensor_copy(
                    POS[0:4, 512 * p:512 * p + 512],
                    PS[0:4, 2048 + 512 * p:2048 + 512 * p + 512]),
                    waits=[W("sPP")], inc=I("sPP"))
            pp_done = I("sPP").n
            dump("pos", POS[0:16, :], [W("sPP", pp_done)])
            E.op("gpsimd", lambda e: e.dma_start(out=pp_i[:, :], in_=POS[0:4, :]),
                 waits=[W("sPP", pp_done)], inc=I("sPG"), dma=True)
            E.op("gpsimd", lambda e: e.collective_compute(
                "AllGather", ALU.bypass, replica_groups=PAIRS,
                ins=[pp_i.ap().opt()], outs=[pp_o.ap().opt()]),
                waits=[W("sPG")], inc=I("sAG"))
            E.op("gpsimd", lambda e: e.collective_compute(
                "AllGather", ALU.bypass, replica_groups=PAIRS,
                ins=[vc_i.ap().opt()], outs=[vc_o.ap().opt()]),
                waits=[W("sv0", I("sv0").n), W("sv1", I("sv1").n)], inc=I("sAG"))
            ag_done = I("sAG").n
            E.op("sync", lambda e: e.dma_start(
                out=GA[:, 0:16384].rearrange("p (c w) -> p c w", c=16),
                in_=vc_o[0:2048, :].rearrange("(c p) w -> p c w", p=128)),
                waits=[W("sAG", ag_done), W("sPV", I("sPV").n)], inc=I("sGI"), dma=True)
            E.op("sync", lambda e: e.dma_start(
                out=WB[:, :].rearrange("p (c w) -> p c w", c=8),
                in_=vc_o[2048:3072, :].rearrange("(c p) w -> p c w", p=128)),
                waits=[W("sAG", ag_done), W("sPRJ", I("sPRJ").n)], inc=I("sGI"), dma=True)
            E.op("sync", lambda e: e.dma_start(
                out=AOS[:, 0:8192].rearrange("p (c w) -> p c w", c=8),
                in_=vc_o[3072:4096, :].rearrange("(c p) w -> p c w", p=128)),
                waits=[W("sAG", ag_done), W("sAT", I("sAT").n), W("sOP", I("sOP").n)],
                inc=I("sGI"), dma=True)
            gi_done = I("sGI").n
            E.op("vector", lambda e: e.memset(khb[0:5, :], 1.0),
                 waits=[W("sPP", pp_done)], inc=I("sKB"))
            mm_marks = {}
            evac_marks = {}
            src_marks = {}
            for piece in range(12):
                isq = piece >= 8
                if not isq:
                    half, pp = piece // 4, piece % 4

                    def ldp(e, half=half, pp=pp):
                        return e.dma_start(out=POSG[0:4, :],
                                           in_=pp_o[4 * half:4 * half + 4,
                                                    512 * pp:512 * pp + 512])
                    kw = [W("sAG", ag_done), W("sv0", I("sv0").n), W("sv1", I("sv1").n)]
                    if piece:
                        kw.append(W("sKB", src_marks[piece - 1]))
                    E.op("sync", ldp, waits=kw, inc=I("sGI"), dma=True)
                    dst = KB13[:, 2048 * half + 512 * pp:2048 * half + 512 * pp + 512]
                    srcf = POSG[0:4, :]
                    cw = [W("sGI", I("sGI").n)]
                    sela, selb = SELKA[0:5, :], SELKB[0:4, :]
                else:
                    pp = piece - 8
                    dst = QB13[:, 512 * pp:512 * pp + 512]
                    srcf = POS[0:4, 512 * pp:512 * pp + 512]
                    cw = [W("sPP", pp_done)]
                    sela, selb = SELQA[0:5, :], SELQB[0:4, :]
                if piece:
                    cw.append(W("sPP", mm_marks[piece - 1]))
                E.op("vector", lambda e, s=srcf: e.tensor_copy(khb[0:4, :], s),
                     waits=cw)
                E.op("vector", lambda e, s=srcf: e.tensor_sub(klb[0:4, :], s,
                                                              khb[0:4, :]),
                     inc=I("sKB"))
                src_marks[piece] = I("sKB").n
                psb = PS[0:13, 3072 + 512 * (piece % 2):3584 + 512 * (piece % 2)]

                def mmaug(e, psb=psb, sela=sela, selb=selb):
                    e.matmul(psb, sela, khb[0:5, :], start=True, stop=False)
                    return e.matmul(psb, selb, klb[0:4, :], start=False, stop=True)
                mw = [W("sKB", src_marks[piece]), W("sCT", ct_done)]
                if piece >= 2:
                    mw.append(W("sKB", evac_marks[piece - 2]))
                E.op("tensor", mmaug, waits=mw, inc=I("sPP"))
                mm_marks[piece] = I("sPP").n
                E.op("vector", lambda e, d=dst, psb=psb: e.tensor_copy(d[0:13, :], psb),
                     waits=[W("sPP", mm_marks[piece])], inc=I("sKB"))
                evac_marks[piece] = I("sKB").n
            kb_done = I("sKB").n
            dump("kb13", KB13[0:16, :], [W("sKB", kb_done)])
            for t in range(NT):
                J = list(range(0, t + 1)) + list(range(16, 16 + t + 1))
                nj = len(J)
                for bi in range(0, nj, 4):
                    js = J[bi:bi + 4]
                    pi = (I("sDM").n) % 2

                    def mmd(e, js=js, pi=pi, t=t):
                        inst = None
                        for idx, j in enumerate(js):
                            inst = e.matmul(
                                PS[:, 512 * pi + 128 * idx:512 * pi + 128 * idx + 128],
                                KB13[0:13, 128 * j:128 * j + 128],
                                QB13[0:13, 128 * t:128 * t + 128],
                                start=(idx == 0), stop=(idx == len(js) - 1),
                                skip_group_check=True)
                        return inst
                    wd = [W("sKB", kb_done), W("sRES", res_base)]
                    if I("sRL").n >= 1:
                        wd.append(W("sRL", I("sRL").n - 1))
                    E.op("tensor", mmd, waits=wd, inc=I("sDM"))
                    E.op("vector", lambda e, bi=bi, pi=pi, n4=len(js): e.tensor_scalar(
                        PSTG[:, 128 * bi:128 * bi + 128 * n4],
                        PS[:, 512 * pi:512 * pi + 128 * n4], 1e-20, None, ALU.max),
                        waits=[W("sDM")], inc=I("sRL"))
                E.op("scalar", lambda e, nj=nj: e.activation(
                    PSTG[:, 0:128 * nj], PSTG[:, 0:128 * nj], AF.Ln),
                    waits=[W("sRL", I("sRL").n)])
                E.op("scalar", lambda e, nj=nj: e.activation(
                    PSTG[:, 0:128 * nj], PSTG[:, 0:128 * nj], AF.Exp, scale=0.5))
                E.op("scalar", lambda e, nj=nj: e.activation(
                    PSTG[:, 0:128 * nj], PSTG[:, 0:128 * nj], AF.Exp, scale=-0.125),
                    inc=I("sEB"))
                E.op("vector", lambda e, t=t: e.tensor_mul(
                    PSTG[:, 128 * t:128 * t + 128], PSTG[:, 128 * t:128 * t + 128], mkA),
                    waits=[W("sEB"), W("sKB", mk_done)])
                E.op("vector", lambda e, nj=nj: e.tensor_mul(
                    PSTG[:, 128 * (nj - 1):128 * nj], PSTG[:, 128 * (nj - 1):128 * nj], mkB),
                    inc=I("sPB"))
                pb_done = I("sPB").n
                if t == 1:
                    dump("esb", PSTG[:, 0:4352], [W("sPB", pb_done)])
                def mms(e, t=t, nj=nj):
                    inst = None
                    for idx in range(nj):
                        e.ldweights(PSTG[:, 128 * idx:128 * idx + 128])
                        inst = e.matmul(PS[:, 1024 + t:1025 + t],
                                        PSTG[:, 128 * idx:128 * idx + 128], ONC,
                                        start=(idx == 0), stop=(idx == nj - 1),
                                        skip_group_check=True)
                    return inst
                E.op("tensor", mms, waits=[W("sPB", pb_done), W("sRS", I("sRS").n),
                                           W("sCT", ct_done)], inc=I("sDM"))
                E.op("vector", lambda e, t=t: e.reciprocal(rsumb[:, t:t + 1],
                                                           PS[:, 1024 + t:1025 + t]),
                     waits=[W("sDM")], inc=I("sRS"))

                def mmpv2(e, t=t, J=J):
                    inst = None
                    for ds in range(8):
                        for idx, j in enumerate(J):
                            e.ldweights(vbt(j)[:, 128 * ds:128 * ds + 128])
                            inst = e.matmul(
                                PS[:, 2048 + 128 * ds:2048 + 128 * ds + 128],
                                vbt(j)[:, 128 * ds:128 * ds + 128],
                                PSTG[:, 128 * idx:128 * idx + 128],
                                start=((ds == 0 or ds == 4) and idx == 0),
                                stop=(idx == len(J) - 1), skip_group_check=True)
                    return inst
                pw = [W("sPB", pb_done), W("sGI", gi_done)]
                if I("sOB").n >= 1:
                    pw.append(W("sOB", I("sOB").n))
                E.op("tensor", mmpv2, waits=pw, inc=I("sPV"))

                def evb(e, t=t):
                    inst = None
                    for ds in range(8):
                        inst = e.tensor_copy(fm(ds)[:, 128 * t:128 * t + 128],
                                             PS[:, 2048 + 128 * ds:2048 + 128 * ds + 128])
                    return inst
                E.op("vector", evb, waits=[W("sPV")], inc=I("sOB"))
            ob_done = I("sOB").n
            dump("aofb", fm(0), [W("sOB", ob_done)])
            BOREP = GREP
            E.op("sync", lambda e: e.dma_start(out=GREP[0:1, :], in_=bb_d[1:2, :]),
                 waits=[W("sPV", I("sPV").n), W("sOB", I("sOB").n)], inc=I("sGB"), dma=True)
            emit_bcast(GREP[0:1, :], GREP, [W("sGB", I("sGB").n)], I("sGB"), width=1024)
            borep_done = I("sGB").n
            E.op("gpsimd", lambda e: e.dma_start(out=WB[:, :], in_=wob_d[:, :]),
                 waits=[W("sPV", I("sPV").n)], inc=I("sW1"), dma=True)
            wob_done = I("sW1").n
            for t in range(NT):
                pb = 2 * (t % 2)

                def mmob(e, t=t, pb=pb):
                    inst = None
                    for oh in range(2):
                        for ic in range(8):
                            inst = e.matmul(bk(pb + oh), fm(ic)[:, 128 * t:128 * t + 128],
                                            WB[:, 1024 * ic + 512 * oh:
                                               1024 * ic + 512 * oh + 512],
                                            start=(ic == 0), stop=(ic == 7))
                    return inst
                wts = [W("sOB", ob_done), W("sW1", wob_done)]
                if t >= 2:
                    wts.append(W("sRES", I("sRES").n - 1))
                E.op("tensor", mmob, waits=wts, inc=I("sOP"))

                def residb(e, t=t, pb=pb):
                    for oh in range(2):
                        e.scalar_tensor_tensor(
                            xr(t)[:, 512 * oh:512 * oh + 512], bk(pb + oh),
                            rsumb[:, t:t + 1], xr(t)[:, 512 * oh:512 * oh + 512],
                            ALU.mult, ALU.add)
                    return e.tensor_add(xr(t), xr(t), BOREP)
                E.op("vector", residb, waits=[W("sOP"), W("sRS", I("sRS").n),
                                              W("sGB", borep_done)], inc=I("sRES"))

        if stages >= 2:
            emit_ternary()
            emit_ffn(0)
            dump_x("x2")
        if stages >= 3:
            emit_attn_b()
            dump_x("x3")
        if stages >= 4:
            emit_ffn(1)

        for t in range(NT):
            E.op("sync", lambda e, t=t: e.dma_start(out=out_d[128 * t:128 * t + 128, :],
                                                    in_=xr(t)),
                 waits=[W("sRES", I("sRES").n)], inc=I("sX"), dma=True)
        E.op("sync", lambda e: e.nop(), waits=[W("sX", I("sX").n), W("sDP", I("sDP").n)])

        E.emit(nc)
    return nc


def _rope_perm():
    p = []
    for h in range(NH):
        p += [64 * h + 2 * i for i in range(32)]
        p += [64 * h + 2 * i + 1 for i in range(32)]
    return np.array(p, dtype=np.int64)


def _wmat(w, perm=None):
    wt = np.asarray(w).T
    if perm is not None:
        wt = wt[:, perm]
    return np.ascontiguousarray(
        wt.reshape(8, 128, 1024).transpose(1, 0, 2).reshape(128, 8192)).astype(np.float32)


def _stage_inputs(inputs, core):
    b, p = core // 2, core % 2
    x = np.asarray(inputs["L0_attn_a_x"])
    xs = np.ascontiguousarray(x.reshape(4, 32, 128, D)[b, p::2].reshape(TOK, D))
    perm = _rope_perm()
    tpos = ((2 * np.arange(16)[:, None] + p) * 128 + np.arange(128)[None, :]).reshape(-1)
    inv = 1.0 / 10000.0 ** (np.arange(0, HD, 2, dtype=np.float64) / HD)
    ang = tpos[:, None] * inv[None, :]
    ct = np.ascontiguousarray(np.cos(ang).astype(np.float32).reshape(16, 128, 32)
                              .transpose(1, 0, 2).reshape(128, 512))
    st = np.ascontiguousarray(np.sin(ang).astype(np.float32).reshape(16, 128, 32)
                              .transpose(1, 0, 2).reshape(128, 512))
    gbr = np.ascontiguousarray(np.stack([np.asarray(inputs["L0_ffn_gamma"]),
                                         np.asarray(inputs["L0_ffn_beta"])])).astype(np.float32)
    bb = np.ascontiguousarray(np.stack([np.asarray(inputs["L0_attn_b_bv"]),
                                        np.asarray(inputs["L0_attn_b_bo"])])).astype(np.float32)
    pcm = np.zeros((128, 8), np.float32)
    pcm[:, 0] = float(p)
    pcm[:, 1] = 1.0 - float(p)
    pcm[0:3, 2] = np.asarray(inputs["L0_attn_b_b3d"])
    w3 = np.asarray(inputs["L0_attn_b_w3d"]).T
    w3s = np.ascontiguousarray(w3.reshape(8, 128, 3).transpose(1, 0, 2).reshape(128, 24)
                               ).astype(np.float32)
    return {
        "x": xs.astype(np.float32),
        "wq": _wmat(inputs["L0_attn_a_wq"], perm),
        "wk": _wmat(inputs["L0_attn_a_wk"], perm),
        "wv": _wmat(inputs["L0_attn_a_wv"]),
        "wo": _wmat(inputs["L0_attn_a_wo"]),
        "wf": _wmat(inputs["L0_ffn_weight"]),
        "wvb": _wmat(inputs["L0_attn_b_wv"]),
        "wob": _wmat(inputs["L0_attn_b_wo"]),
        "w3": w3s, "gbr": gbr, "bb": bb, "pc": pcm, "ct": ct, "st": st,
    }


_NC_CACHE = {}


def kernel(**inputs):
    key = ("k", STAGES)
    if key not in _NC_CACHE:
        _NC_CACHE[key] = build_program(stages=STAGES)
    nc = _NC_CACHE[key]
    in_maps = [_stage_inputs(inputs, c) for c in range(8)]
    res = run_bass_kernel_spmd(nc, in_maps, core_ids=list(range(8)))
    out = np.zeros((4, 4096, D), np.float32)
    ov = out.reshape(4, 32, 128, D)
    for c in range(8):
        b, p = c // 2, c % 2
        ov[b, p::2] = res.results[c]["out"].reshape(16, 128, D)
    return out


# version 02
# speedup vs baseline: 1.3288x; 1.3288x over previous
import numpy as np
import ml_dtypes
import concourse.bass as bass
import concourse.mybir as mybir
from concourse.bass_utils import run_bass_kernel_spmd
from concourse import library_config
from concourse.bass_isa import ReduceOp

F32 = mybir.dt.float32
BF16 = mybir.dt.bfloat16
AF = mybir.ActivationFunctionType
ALU = mybir.AluOpType
AX = mybir.AxisListType
bf16 = ml_dtypes.bfloat16

TOK, NT, NH, HD, D = 2048, 16, 16, 64, 1024
NG, CHK = 4, 32
EPS = 1e-5
MAGIC = float(np.float32(1.5 * 2 ** 23))
PAIRS = [[0, 1], [2, 3], [4, 5], [6, 7]]
ALL8 = [list(range(8))]
STAGES = 4


class Sem:
    __slots__ = ("name", "h", "n")

    def __init__(self, name):
        self.name, self.h, self.n = name, None, 0


class Emit:
    ENGINES = ("sync", "scalar", "tensor", "vector", "gpsimd")

    def __init__(self):
        self.items = {e: [] for e in self.ENGINES}
        self.done = {e: {} for e in self.ENGINES}

    def op(self, eng, fn, waits=(), inc=None, dma=False):
        amt = 16 if dma else 1
        d = self.done[eng]
        ws = []
        for s, v in waits:
            if v > 0 and d.get(s, -1) < v:
                ws.append((s, v))
                d[s] = v
        if inc is not None:
            inc.n += amt
        self.items[eng].append((fn, tuple(ws), inc, amt))

    def emit(self, nc):
        with nc.Block() as block:
            for e in self.ENGINES:
                if not self.items[e]:
                    continue

                def mk(items):
                    def body(eng):
                        for fn, ws, inc, amt in items:
                            for s, v in ws:
                                eng.wait_ge(s.h, v)
                            inst = fn(eng)
                            if inc is not None:
                                inst.then_inc(inc.h, amt)
                    return body
                getattr(block, e)(mk(self.items[e]))


def build_program(dumps=(), stages=STAGES, s1cut=None):
    nc = bass.Bass("TRN2", target_bir_lowering=False)
    E = Emit()
    dumps = set(dumps)

    x_d = nc.dram_tensor("x", [TOK, D], F32, kind="ExternalInput")
    out_d = nc.dram_tensor("out", [TOK, D], F32, kind="ExternalOutput")
    wq_d = nc.dram_tensor("wq", [128, 8192], F32, kind="ExternalInput")
    wk_d = nc.dram_tensor("wk", [128, 8192], F32, kind="ExternalInput")
    wv_d = nc.dram_tensor("wv", [128, 8192], F32, kind="ExternalInput")
    wo_d = nc.dram_tensor("wo", [128, 8192], F32, kind="ExternalInput")
    wf_d = nc.dram_tensor("wf", [128, 8192], F32, kind="ExternalInput")
    wvb_d = nc.dram_tensor("wvb", [128, 8192], F32, kind="ExternalInput")
    wob_d = nc.dram_tensor("wob", [128, 8192], F32, kind="ExternalInput")
    w3_d = nc.dram_tensor("w3", [128, 24], F32, kind="ExternalInput")
    gbr_d = nc.dram_tensor("gbr", [2, D], F32, kind="ExternalInput")
    bb_d = nc.dram_tensor("bb", [2, D], F32, kind="ExternalInput")
    pc_d = nc.dram_tensor("pc", [128, 8], F32, kind="ExternalInput")
    ct_d = nc.dram_tensor("ct", [128, 512], F32, kind="ExternalInput")
    st_d = nc.dram_tensor("st", [128, 512], F32, kind="ExternalInput")

    kb_i = nc.dram_tensor("kb_i", [TOK, D], BF16)
    kb_o = nc.dram_tensor("kb_o", [2 * TOK, D], BF16)
    vb_i = nc.dram_tensor("vb_i", [TOK, NH * 65], BF16)
    vb_o = nc.dram_tensor("vb_o", [2 * TOK, NH * 65], BF16)
    vc_i = nc.dram_tensor("vc_i", [TOK, D], BF16)
    vc_o = nc.dram_tensor("vc_o", [2 * TOK, D], BF16)
    pp_i = nc.dram_tensor("pp_i", [4, TOK], F32)
    pp_o = nc.dram_tensor("pp_o", [8, TOK], F32)
    mx_i = nc.dram_tensor("mx_i", [1, 8], F32)
    mx_o = nc.dram_tensor("mx_o", [1, 8], F32)

    ident_d = nc.inline_tensor(np.eye(128, dtype=bf16), name="identb")
    ones2_np = np.zeros((128, 2), dtype=bf16)
    ones2_np[0:64, 0] = 1
    ones2_np[64:128, 1] = 1
    ones2_d = nc.inline_tensor(ones2_np, name="ones2")
    onescol_d = nc.inline_tensor(np.ones((128, 1), dtype=bf16), name="onescol")
    onesrow_d = nc.inline_tensor(np.ones((1, 128), dtype=bf16), name="onesrow")
    identf_d = nc.inline_tensor(np.eye(128, dtype=np.float32), name="identf")
    onesrowf_d = nc.inline_tensor(np.ones((1, 128), dtype=np.float32), name="onesrowf")
    tri_np = np.ascontiguousarray(
        np.tril(np.ones((128, 128), dtype=np.float32)).T.astype(bf16))
    tri_d = nc.inline_tensor(tri_np, name="trikq")
    ska = np.zeros((5, 13), dtype=bf16)
    for i in range(3):
        ska[i, i] = 1; ska[i, 6 + i] = 1
    ska[3, 11] = 1; ska[4, 9] = 1; ska[4, 10] = 1
    skb = np.zeros((4, 13), dtype=bf16)
    for i in range(3):
        skb[i, 3 + i] = 1
    skb[3, 12] = 1
    sqa = np.zeros((5, 13), dtype=bf16)
    for i in range(3):
        sqa[i, i] = -2; sqa[i, 3 + i] = -2
    sqa[3, 9] = 1; sqa[4, 11] = 1; sqa[4, 12] = 1
    sqb = np.zeros((4, 13), dtype=bf16)
    for i in range(3):
        sqb[i, 6 + i] = -2
    sqb[3, 10] = 1
    selka_d = nc.inline_tensor(ska, name="selka")
    selkb_d = nc.inline_tensor(skb, name="selkb")
    selqa_d = nc.inline_tensor(sqa, name="selqa")
    selqb_d = nc.inline_tensor(sqb, name="selqb")
    s34a = np.zeros((3, 4), dtype=np.float32)
    for i in range(3):
        s34a[i, i] = 1
    s34b = np.zeros((3, 4), dtype=np.float32)
    s34b[:, 3] = 1
    sel34a_d = nc.inline_tensor(s34a, name="sel34a")
    sel34b_d = nc.inline_tensor(s34b, name="sel34b")

    dump_t = {}
    for tag, shape, dt in [
        ("xn", (128, D), BF16), ("qtm", (128, 1024), BF16), ("qfm", (128, 2048), BF16),
        ("kfm", (128, 4096), BF16), ("es", (128, 1024), BF16), ("ao", (128, 2048), BF16),
        ("x1", (TOK, D), F32), ("x2", (TOK, D), F32), ("x3", (TOK, D), F32),
        ("tw", (128, 1024), BF16), ("pos", (16, TOK), F32), ("kb13", (16, 4096), BF16),
        ("esb", (128, 4352), BF16), ("scal", (1, 8), F32), ("aofb", (128, 2048), BF16),
        ("vga", (128, 260), BF16), ("xraw", (128, 1024), F32),
        ("stats", (128, 16), F32), ("lnm", (128, 16), F32), ("lnr", (128, 16), F32),
        ("sumsq", (128, 16), F32), ("qsc", (128, 1024), F32),
    ]:
        if tag in dumps:
            dump_t[tag] = nc.dram_tensor("d_" + tag, list(shape), dt, kind="ExternalOutput")

    names = ("sX sCT sW1 sWO sW2 sLN sTR sFM sPRJ sRP sKO sVO sAG sGI sKF sNC sQK "
             "sEX sPV sAOv sAT sAF sOP sRES sDP sMX sMXc sTW sWF sPP sPG sKB sDM "
             "sRL sRS sEB sPB sOB sQT sST sSB sGB "
             "sx0 sx1 sx2 sx3 sk0 sk1 sv0 sv1 sBB").split()
    SM = {n: Sem(n) for n in names}

    def W(sem, v=None):
        return (SM[sem], SM[sem].n if v is None else v)

    def I(sem):
        return SM[sem]

    with (
        nc.sbuf_tensor("XR", [128, NT * 1024], F32) as XR,
        nc.sbuf_tensor("FM", [128, 16384], BF16) as FM,
        nc.sbuf_tensor("GA", [128, 16640], BF16) as GA,
        nc.sbuf_tensor("TWS", [128, 8192], BF16) as TWS,
        nc.sbuf_tensor("WB", [128, 8192], BF16) as WB,
        nc.sbuf_tensor("AOS", [128, 10240], BF16) as AOS,
        nc.sbuf_tensor("PO", [128, 6912], F32) as PO,
        nc.psum_tensor("PS", [128, 4096], F32) as PS,
    ):
        for s in SM.values():
            s.h = nc.alloc_semaphore(s.name)

        def xr(t):
            return XR[:, 1024 * t:1024 * t + 1024]

        def fm(ic):
            return FM[:, 2048 * ic:2048 * ic + 2048]

        def qtm(t):
            return GA[:, 1024 * t:1024 * t + 1024]

        def katm(j):
            return GA[:, 256 * j:256 * j + 256]

        def vga(j):
            return GA[:, 8192 + 260 * j:8192 + 260 * j + 260]

        def kfm(i):
            return TWS[:, 4096 * i:4096 * i + 4096]

        def vbt(j):
            if j < 16:
                return GA[:, 1024 * j:1024 * j + 1024]
            if j < 24:
                return WB[:, 1024 * (j - 16):1024 * (j - 16) + 1024]
            return AOS[:, 1024 * (j - 24):1024 * (j - 24) + 1024]

        def bk(i, w=512):
            return PS[:, 512 * i:512 * i + w]

        ct_s = PO[:, 0:512]
        st_s = PO[:, 512:1024]
        xnrot = [PO[:, 1024 + 512 * i:1536 + 512 * i].bitcast(BF16) for i in range(2)]
        ES = [PO[:, 2048 + 512 * i:2560 + 512 * i].bitcast(BF16) for i in range(2)]
        QSC = PO[:, 2048:3072]
        st6 = PO[:, 3072:3084]
        mv = PO[:, 3084:3086]
        csc = PO[:, 3088:3112]
        dbgS = PO[:, 6296:6312]
        ln_m = PO[:, 3112:3128]
        ln_r = PO[:, 3128:3144]
        smax = PO[:, 3144:3145]
        mx_s = PO[:, 3152:3160]
        rQ8 = PO[:, 3160:3168]
        negC = PO[:, 3168:3172]
        t1r = PO[:, 3200:3712]
        t2r = PO[:, 3712:4224]
        sqscr = PO[:, 3200:3456].bitcast(BF16)
        GREP = PO[:, 4224:5248]
        BREP = PO[:, 5248:6272]
        kvrot = [PO[:, 4224 + 520 * i:4744 + 520 * i].bitcast(BF16) for i in range(2)]
        kt_rot = [PO[:, 5264 + 512 * i:5776 + 512 * i].bitcast(BF16) for i in range(2)]
        pc_s = PO[:, 6288:6296]
        rsumb = PO[:, 6352:6368]
        IDF = PO[:, 6656:6784]
        ONRF = PO[0:1, 6784:6912]
        mkA = PO[:, 6400:6464].bitcast(BF16)
        mkB = PO[:, 6464:6528].bitcast(BF16)
        tri_s = PO[:, 6528:6592].bitcast(BF16)
        POS = PO[:, 0:2048]
        POSG = PO[:, 2048:2560]
        khb = PO[:, 2560:2816].bitcast(BF16)
        klb = PO[:, 2816:3072].bitcast(BF16)
        SEL34A = PO[:, 6312:6316]
        SEL34B = PO[:, 6316:6320]
        SELKA = AOS[:, 10144:10157]
        SELKB = AOS[:, 10160:10173]
        SELQA = AOS[:, 10176:10189]
        SELQB = AOS[:, 10208:10221]
        PSTG = PO[:, 0:2176].bitcast(BF16)
        QB13 = PO[:, 3200:4224].bitcast(BF16)
        KB13 = PO[:, 4224:6272].bitcast(BF16)
        ONES3F = PO[:, 3184:3185]

        ID = AOS[:, 9984:10112]
        ONR2 = AOS[0:1, 8704:8832]
        BVB = AOS[0:1, 8832:9856]
        BOB = AOS[1:2, 8832:9856]

        def aotm(p, t):
            return AOS[:, 2048 * p + 128 * t:2048 * p + 128 * t + 128]

        def aofm(i):
            return AOS[:, 4096 + 2048 * i:4096 + 2048 * i + 2048]

        TRP = [PS[:, 2048 + 512 * i:2560 + 512 * i].bitcast(BF16) for i in range(2)]

        def _finish(_ctx):
            for t in range(NT):
                E.op("sync", lambda e, t=t: e.dma_start(
                    out=out_d[128 * t:128 * t + 128, :], in_=xr(t)),
                    waits=[W("sRES", I("sRES").n), W("sLN", I("sLN").n),
                           W("sFM", I("sFM").n), W("sRP", I("sRP").n),
                           W("sQT", I("sQT").n)],
                    inc=I("sX"), dma=True)
            E.op("sync", lambda e: e.nop(), waits=[W("sX", I("sX").n), W("sDP", I("sDP").n),
                                                   W("sk0", I("sk0").n), W("sk1", I("sk1").n),
                                                   W("sv0", I("sv0").n), W("sv1", I("sv1").n),
                                                   W("sAG", I("sAG").n)])
            E.emit(nc)

        def _preload_tables(e):
            return e.add_instruction(mybir.InstLoadActFuncSet(
                name=nc.get_next_instruction_name(), act_func_set_id=6, ins=[], outs=[]))
        E.op("scalar", _preload_tables)
        for t in range(NT):
            sx = "sx%d" % (t % 4)
            E.op("sync", lambda e, t=t: e.dma_start(out=xr(t), in_=x_d[128 * t:128 * t + 128, :]),
                 waits=([W(sx, 16 * (t // 4))] if t >= 4 else []), inc=I(sx), dma=True)
        for dst, src in [(ct_s, ct_d), (st_s, st_d), (ID, ident_d), (tri_s, tri_d),
                         (pc_s, pc_d)]:
            E.op("sync", lambda e, d=dst, s=src: e.dma_start(out=d, in_=s[:, :]),
                 inc=I("sCT"), dma=True)
        ON2 = AOS[:, 10112:10114]
        ONC = AOS[:, 10114:10115]
        E.op("sync", lambda e: e.dma_start(out=ON2, in_=ones2_d[:, :]), inc=I("sCT"), dma=True)
        E.op("sync", lambda e: e.dma_start(out=ONC, in_=onescol_d[:, :]), inc=I("sCT"), dma=True)
        E.op("sync", lambda e: e.dma_start(out=ONR2, in_=onesrow_d[:, :]), inc=I("sCT"), dma=True)
        E.op("sync", lambda e: e.dma_start(out=IDF, in_=identf_d[:, :]), inc=I("sCT"), dma=True)
        E.op("sync", lambda e: e.dma_start(out=ONRF, in_=onesrowf_d[:, :]), inc=I("sCT"), dma=True)
        for dst, src in [(SELKA[0:5, :], selka_d), (SELKB[0:4, :], selkb_d),
                         (SELQA[0:5, :], selqa_d), (SELQB[0:4, :], selqb_d),
                         (SEL34A[0:3, :], sel34a_d), (SEL34B[0:3, :], sel34b_d)]:
            E.op("sync", lambda e, d=dst, s=src: e.dma_start(out=d, in_=s[:, :]),
                 inc=I("sCT"), dma=True)
        E.op("gpsimd", lambda e: e.dma_start(out=BVB, in_=bb_d[0:1, :]), inc=I("sBB"), dma=True)
        E.op("gpsimd", lambda e: e.dma_start(out=BOB, in_=bb_d[1:2, :]), inc=I("sBB"), dma=True)
        E.op("gpsimd", lambda e: e.dma_start(out=WB[:, :], in_=wq_d[:, :]), inc=I("sW1"), dma=True)
        E.op("gpsimd", lambda e: e.dma_start(out=TWS[:, :], in_=wk_d[:, :]), inc=I("sW2"), dma=True)
        ct_done = I("sCT").n

        def dump(tag, ap, waits):
            if tag in dump_t:
                E.op("sync", lambda e, t=dump_t[tag], a=ap: e.dma_start(out=t.ap(), in_=a),
                     waits=waits, inc=I("sDP"), dma=True)

        def dump_x(tag):
            if tag in dump_t:
                for t in range(NT):
                    E.op("sync", lambda e, t=t: e.dma_start(
                        out=dump_t[tag].ap()[128 * t:128 * t + 128, :], in_=xr(t)),
                        waits=[W("sRES")], inc=I("sDP"), dma=True)

        def emit_preduce(src_col, dst11, op, waits, inc_sem, cnt=128):
            E.op("tensor", lambda e: e.transpose(PS[0:1, 3584:3584 + cnt],
                                                 src_col, IDF[0:cnt, 0:cnt]),
                 waits=list(waits) + [W("sCT", ct_done)], inc=I("sNC"))
            E.op("vector", lambda e: e.tensor_reduce(dst11, PS[0:1, 3584:3584 + cnt],
                                                     AX.X, op), waits=[W("sNC")],
                 inc=inc_sem)

        def emit_bcast(src11, dst_col, waits, inc_sem, width=1):
            base = 3584 if width <= 512 else 3072
            def mm(e):
                inst = None
                for o in range(0, width, 512):
                    wdt = min(512, width - o)
                    inst = e.matmul(PS[:, base + o:base + o + wdt], ONRF,
                                    src11[0:1, o:o + wdt], start=True, stop=True)
                return inst
            E.op("tensor", mm, waits=list(waits) + [W("sCT", ct_done)], inc=I("sNC"))
            E.op("vector", lambda e: e.tensor_copy(dst_col, PS[:, base:base + width]),
                 waits=[W("sNC")], inc=inc_sem)

        def emit_ln_tile(t, kind, out_tile, in_waits):
            E.op("vector", lambda e, t=t: e.scalar_tensor_tensor(
                QSC, xr(t), 1.0, xr(t), ALU.mult, ALU.mult), waits=in_waits,
                inc=I("sST"))
            E.op("vector", lambda e, t=t: e.reduce_sum(csc[:, 9:10], xr(t), axis=AX.X),
                 inc=I("sST"))
            E.op("vector", lambda e: e.reduce_sum(csc[:, 10:11], QSC, axis=AX.X),
                 waits=[W("sST")], inc=I("sST"))
            if "sumsq" in dump_t and t < 16:
                E.op("vector", lambda e, t=t: e.tensor_copy(
                    dbgS[:, t:t + 1], csc[:, 10:11]), waits=[W("sST")])
            E.op("vector", lambda e: e.tensor_scalar_mul(mv[:, 0:1], csc[:, 9:10],
                                                         1.0 / 1024.0), waits=[W("sST")])
            E.op("vector", lambda e: e.tensor_scalar_mul(csc[:, 11:12], csc[:, 10:11],
                                                         1.0 / 1024.0), inc=I("sST"))
            E.op("vector", lambda e: e.scalar_tensor_tensor(
                csc[:, 12:13], mv[:, 0:1], 1.0, mv[:, 0:1], ALU.mult, ALU.mult),
                waits=[W("sST")], inc=I("sST"))
            E.op("vector", lambda e: e.tensor_sub(mv[:, 1:2], csc[:, 11:12], csc[:, 12:13]),
                 waits=[W("sST")], inc=I("sST"))
            E.op("vector", lambda e: e.tensor_scalar_add(csc[:, 0:1], mv[:, 1:2], EPS),
                 waits=[W("sST")], inc=I("sST"))
            E.op("scalar", lambda e: e.activation(csc[:, 1:2], csc[:, 0:1], AF.Ln),
                 waits=[W("sST")])
            E.op("scalar", lambda e: e.activation(csc[:, 2:3], csc[:, 1:2], AF.Exp, scale=-0.5),
                 inc=I("sSB"))
            if kind == "ffn":
                E.op("vector", lambda e: e.scalar_tensor_tensor(
                    csc[:, 3:4], csc[:, 2:3], 1.0, csc[:, 2:3], ALU.mult, ALU.mult),
                    waits=[W("sSB")], inc=I("sST"))
                E.op("vector", lambda e: e.tensor_mul(csc[:, 4:5], csc[:, 3:4], mv[:, 1:2]),
                     waits=[W("sST")], inc=I("sST"))
                E.op("vector", lambda e: e.tensor_scalar_add(csc[:, 4:5], csc[:, 4:5], EPS),
                     waits=[W("sST")], inc=I("sST"))
                E.op("scalar", lambda e: e.activation(csc[:, 5:6], csc[:, 4:5], AF.Ln),
                     waits=[W("sST")])
                E.op("scalar", lambda e: e.activation(csc[:, 6:7], csc[:, 5:6], AF.Exp,
                                                      scale=-0.5), inc=I("sSB"))
                E.op("vector", lambda e: e.tensor_mul(csc[:, 7:8], csc[:, 2:3], csc[:, 6:7]),
                     waits=[W("sSB")], inc=I("sST"))
                E.op("vector", lambda e, t=t: e.tensor_copy(ln_m[:, t:t + 1], mv[:, 0:1]))
                E.op("vector", lambda e, t=t: e.tensor_copy(ln_r[:, t:t + 1], csc[:, 7:8]),
                     waits=[W("sST")], inc=I("sST"))
                rcol = csc[:, 7:8]
            else:
                rcol = csc[:, 2:3]
            if out_tile is not None:
                if t < 16:
                    E.op("vector", lambda e, t=t: e.tensor_copy(ln_m[:, t:t + 1],
                                                                mv[:, 0:1]), waits=[W("sSB")])
                    E.op("vector", lambda e, t=t, r=rcol: e.tensor_copy(ln_r[:, t:t + 1], r))
                E.op("vector", lambda e, t=t, o=out_tile, r=rcol: e.tensor_scalar(
                    o, xr(t), mv[:, 0:1], r, ALU.subtract, ALU.mult),
                    waits=[W("sSB"), W("sST")], inc=I("sLN"))

        tr_par = [0]

        def emit_transpose8(src, dest_fn, waits_t, npiece=8):
            pi = tr_par[0] % 2
            tr_par[0] += 1
            fr = tr_par[0] - 2

            def tr(e, s=src, pi=pi, n=npiece):
                inst = None
                for ic in range(n):
                    inst = e.transpose(TRP[pi][:, 128 * ic:128 * ic + 128],
                                       s[:, 128 * ic:128 * ic + 128], ID)
                return inst
            E.op("tensor", tr, waits=list(waits_t) + ([W("sFM", fr)] if fr > 0 else []),
                 inc=I("sTR"))

            def ev(e, pi=pi, n=npiece):
                inst = None
                for ic in range(n):
                    inst = e.tensor_copy(dest_fn(ic), TRP[pi][:, 128 * ic:128 * ic + 128])
                return inst
            E.op("vector", ev, waits=[W("sTR")], inc=I("sFM"))

        def emit_proj_tile(t, wsrc, wwaits, pb, bias_row=None, fm_wait=None):
            def mm(e, t=t, w=wsrc, pb=pb, br=bias_row):
                inst = None
                for oh in range(2):
                    for ic in range(8):
                        inst = e.matmul(bk(pb + oh), fm(ic)[:, 128 * t:128 * t + 128],
                                        w[:, 1024 * ic + 512 * oh:1024 * ic + 512 * oh + 512],
                                        start=(ic == 0), stop=(ic == 7 and br is None))
                    if br is not None:
                        inst = e.matmul(bk(pb + oh), ONR2, br[0:1, 512 * oh:512 * oh + 512],
                                        start=False, stop=True)
                return inst
            E.op("tensor", mm, waits=list(wwaits) + ([fm_wait] if fm_wait else []),
                 inc=I("sPRJ"))

        def rope_slices(ap2d):
            r = ap2d.rearrange("p (h half i) -> p h half i", h=16, half=2, i=32)
            return r[:, :, 0, :], r[:, :, 1, :]

        def emit_rope_evac(t, pb, dst):
            ps2 = PS[:, 512 * pb:512 * pb + 1024]
            xrr, xii = rope_slices(ps2)
            dr, di = rope_slices(dst)
            cb = ct_s[:, 32 * t:32 * t + 32].unsqueeze(1).broadcast_to([128, 16, 32])
            sb = st_s[:, 32 * t:32 * t + 32].unsqueeze(1).broadcast_to([128, 16, 32])
            t1 = t1r.rearrange("p (h i) -> p h i", h=16)
            t2 = t2r.rearrange("p (h i) -> p h i", h=16)
            E.op("vector", lambda e: e.tensor_tensor(t1, xrr, cb, ALU.mult),
                 waits=[W("sPRJ"), W("sCT", ct_done)])
            E.op("vector", lambda e: e.tensor_tensor(t2, xii, sb, ALU.mult))
            E.op("vector", lambda e: e.tensor_sub(dr, t1, t2))
            E.op("vector", lambda e: e.tensor_tensor(t1, xrr, sb, ALU.mult))
            E.op("vector", lambda e: e.tensor_tensor(t2, xii, cb, ALU.mult))
            E.op("vector", lambda e: e.tensor_add(di, t1, t2), inc=I("sRP"))

        for t in range(NT):
            emit_ln_tile(t, "plain", xnrot[t % 2],
                         [W("sx0", 64), W("sx1", 64), W("sx2", 64), W("sx3", 64),
                          W("sCT", ct_done)])
            if s1cut not in ('ln0', 'ln0b'):
                emit_transpose8(xnrot[t % 2],
                                (lambda tt: (lambda ic: fm(ic)[:, 128 * tt:128 * tt + 128]))(t),
                                [W("sLN")])
        xn_done = I("sFM").n
        dump("xn", FM[:, 0:1024], [W("sFM", xn_done)])
        dump("xraw", xr(0), [W("sLN", 1)])
        dump("stats", PO[:, 3072:3088], [W("sLN", 1)])

        if s1cut in ("ln", "ln0", "ln0b"):
            dump("lnm", ln_m, [W("sLN", I("sLN").n)])
            dump("lnr", ln_r, [W("sLN", I("sLN").n)])
            dump("sumsq", dbgS, [W("sLN", I("sLN").n)])
            dump("qsc", QSC, [W("sLN", I("sLN").n)])
            _finish(locals())
            return nc
        for t in range(NT):
            pb = 2 * (t % 2)
            if t >= 2:
                E.op("tensor", lambda e, v=I("sRP").n - 1: e.nop(), waits=[W("sRP", I("sRP").n - 1)])
            emit_proj_tile(t, WB, [W("sW1", 16)], pb, fm_wait=W("sFM", xn_done))
            emit_rope_evac(t, pb, qtm(t))
        dump("qtm", qtm(0), [W("sRP", 1)])
        E.op("gpsimd", lambda e: e.dma_start(out=WB[:, :], in_=wv_d[:, :]),
             waits=[W("sRP", NT)], inc=I("sW1"), dma=True)
        for t in range(NT):
            pb = 2 * (t % 2)
            E.op("tensor", lambda e: e.nop(), waits=[W("sRP", NT + t - 1)])
            emit_proj_tile(t, TWS, [W("sW2", 16)], pb, fm_wait=W("sFM", xn_done))
            if t >= 2:
                E.op("vector", lambda e, t=t: e.memset(csc[:, 8:9], 0.0),
                     waits=[W("sk%d" % (t % 2), 16 * (t // 2))])
            emit_rope_evac(t, pb, kt_rot[t % 2])
            E.op("sync", lambda e, t=t: e.dma_start(out=kb_i[128 * t:128 * t + 128, :],
                                                    in_=kt_rot[t % 2]),
                 waits=[W("sRP")], inc=I("sk%d" % (t % 2)), dma=True)
        for t in range(NT):
            pb = 2 * (t % 2)
            E.op("tensor", lambda e: e.nop(), waits=[W("sRP", 2 * NT + t - 1)])
            emit_proj_tile(t, WB, [W("sW1", 32)], pb, fm_wait=W("sFM", xn_done))

            def vev(e, t=t, pb=pb):
                ps2 = PS[:, 512 * pb:512 * pb + 1024]
                dst = kvrot[t % 2].rearrange("p (g c) -> p g c", g=16, c=65)
                e.tensor_copy(dst[:, :, 0:64], ps2.rearrange("p (g c) -> p g c", g=16, c=64))
                return e.memset(dst[:, :, 64:65], 1.0)
            vw = [W("sPRJ")]
            if t >= 2:
                vw.append(W("sv%d" % (t % 2), 16 * (t // 2)))
            E.op("vector", vev, waits=vw, inc=I("sRP"))
            E.op("sync", lambda e, t=t: e.dma_start(out=vb_i[128 * t:128 * t + 128, :],
                                                    in_=kvrot[t % 2]),
                 waits=[W("sRP")], inc=I("sv%d" % (t % 2)), dma=True)
        rp_all = I("sRP").n
        for t in range(NT):
            emit_transpose8(qtm(t),
                            (lambda tt: (lambda ic: fm(ic)[:, 128 * tt:128 * tt + 128]))(t),
                            [W("sRP", rp_all)])
        E.op("vector", lambda e: e.memset(csc[:, 8:9], 0.0), inc=I("sQT"))
        qt_done = I("sQT").n
        dump("qfm", fm(0), [W("sQT", qt_done)])

        if s1cut == "proj":
            _finish(locals())
            return nc
        E.op("gpsimd", lambda e: e.collective_compute(
            "AllGather", ALU.bypass, replica_groups=PAIRS,
            ins=[kb_i.ap().opt()], outs=[kb_o.ap().opt()]),
            waits=[W("sk0", I("sk0").n), W("sk1", I("sk1").n)], inc=I("sAG"))
        E.op("gpsimd", lambda e: e.collective_compute(
            "AllGather", ALU.bypass, replica_groups=PAIRS,
            ins=[vb_i.ap().opt()], outs=[vb_o.ap().opt()]),
            waits=[W("sv0", I("sv0").n), W("sv1", I("sv1").n)], inc=I("sAG"))

        if s1cut == "gather0":
            _finish(locals())
            return nc
        ngroups = 1 if s1cut == "g0" else NG
        for g in range(ngroups):
            gw = [W("sAG", 1), W("sQT", qt_done)]
            if g:
                gw.append(W("sPV", I("sPV").n))
            E.op("sync", lambda e, g=g: e.dma_start(
                out=GA[:, 0:8192].rearrange("p (c w) -> p c w", c=CHK),
                in_=kb_o[:, 256 * g:256 * g + 256].rearrange("(c p) w -> p c w", p=128)),
                waits=gw, inc=I("sGI"), dma=True)
            E.op("sync", lambda e, g=g: e.dma_start(
                out=GA[:, 8192:16512].rearrange("p (c w) -> p c w", c=CHK),
                in_=vb_o[:, 260 * g:260 * g + 260].rearrange("(c p) w -> p c w", p=128)),
                waits=[W("sAG", 2)], inc=I("sGI"), dma=True)
            gi_done = I("sGI").n
            aov_now = I("sAOv").n
            for j in range(CHK):
                emit_transpose8(katm(j),
                                (lambda jj: (lambda ic: kfm(ic)[:, 128 * jj:128 * jj + 128]))(j),
                                [W("sGI", gi_done), W("sRP", rp_all),
                                 W("sAOv", aov_now)], npiece=2)
            E.op("vector", lambda e: e.memset(csc[:, 8:9], 0.0), inc=I("sKF"))
            kf_done = I("sKF").n
            if g == 0:
                dump("kfm", kfm(0), [W("sKF", kf_done)])
                dump("vga", vga(0), [W("sGI", gi_done)])

            def sq_piece(src, out_run, first, wv=()):
                E.op("vector", lambda e, s=src: e.scalar_tensor_tensor(
                    sqscr, s, 1.0, s, ALU.mult, ALU.mult), waits=wv,
                    inc=I("sNC"))
                E.op("tensor", lambda e: e.matmul(PS[0:2, 3072:3584], ON2, sqscr,
                                                  start=True, stop=True),
                     waits=[W("sNC"), W("sRES", I("sRES").n), W("sCT", ct_done)], inc=I("sNC"))
                E.op("vector", lambda e: e.reduce_max(csc[0:2, 9:10], PS[0:2, 3072:3584],
                                                      axis=AX.X), waits=[W("sNC")],
                     inc=I("sST"))
                if first:
                    E.op("vector", lambda e: e.tensor_copy(csc[0:2, 10:11], csc[0:2, 9:10]),
                         waits=[W("sST")], inc=I("sNC"))
                else:
                    E.op("vector", lambda e: e.tensor_max(csc[0:2, 10:11], csc[0:2, 10:11],
                                                          csc[0:2, 9:10]),
                         waits=[W("sST")], inc=I("sNC"))

            first = True
            for ti in (2 * g, 2 * g + 1):
                for piece in range(4):
                    sq_piece(fm(ti)[:, 512 * piece:512 * piece + 512], None, first,
                             wv=[W("sQT", qt_done)])
                    first = False
            emit_preduce(csc[0:2, 10:11], csc[0:1, 12:13], ALU.max,
                         [W("sNC")], I("sNC"), cnt=2)
            first = True
            for ti in range(2):
                for piece in range(8):
                    sq_piece(kfm(ti)[:, 512 * piece:512 * piece + 512], None, first,
                             wv=[W("sKF", kf_done)])
                    first = False
            emit_preduce(csc[0:2, 10:11], csc[0:1, 11:12], ALU.max,
                         [W("sNC")], I("sNC"), cnt=2)
            E.op("vector", lambda e: e.tensor_mul(csc[0:1, 13:14], csc[0:1, 11:12],
                                                  csc[0:1, 12:13]), waits=[W("sNC")],
                 inc=I("sST"))
            E.op("scalar", lambda e: e.activation(csc[0:1, 14:15], csc[0:1, 13:14], AF.Ln),
                 waits=[W("sST")])
            E.op("scalar", lambda e: e.activation(csc[0:1, 13:14], csc[0:1, 14:15], AF.Exp,
                                                  scale=0.5), inc=I("sSB"))
            E.op("vector", lambda e: e.tensor_scalar(csc[0:1, 14:15], csc[0:1, 13:14],
                                                     0.12875, 1.0, ALU.mult, ALU.add),
                 waits=[W("sSB")], inc=I("sST"))
            E.op("vector", lambda e: e.tensor_scalar(csc[0:1, 14:15], csc[0:1, 14:15],
                                                     60.0, -1.0, ALU.min, ALU.mult),
                 waits=[W("sST")], inc=I("sNC"))
            emit_bcast(csc[0:1, 14:15], negC[:, g:g + 1], [W("sNC")], I("sNC"))
            nc_done = I("sNC").n

            trips = [(p, qb, c) for p in range(2) for qb in range(4)
                     for c in range(CHK)]

            def emit_qk(i):
                p, qb, c = trips[i]
                qkb = 2 * (i % 2)

                def mmqk(e, g=g, p=p, qb=qb, c=c, qkb=qkb):
                    kf = kfm(p)
                    qf = fm(2 * g + p)
                    e.ldweights(kf[0:64, 128 * c:128 * c + 128], tile_position=(0, 0))
                    e.ldweights(kf[64:128, 128 * c:128 * c + 128],
                                tile_position=(64, 0))
                    e.matmul(bk(qkb), kf[0:64, 128 * c:128 * c + 128],
                             qf[0:64, 512 * qb:512 * qb + 512],
                             start=True, stop=True, tile_position=(0, 0),
                             skip_group_check=True)
                    return e.matmul(bk(qkb + 1), kf[64:128, 128 * c:128 * c + 128],
                                    qf[64:128, 512 * qb:512 * qb + 512],
                                    start=True, stop=True, tile_position=(64, 0),
                                    skip_group_check=True)
                wq_ = [W("sKF", kf_done), W("sNC", nc_done)]
                if I("sEX").n >= 1:
                    wq_.append(W("sEX", I("sEX").n - 1))
                E.op("tensor", mmqk, waits=wq_, inc=I("sQK"))

            emit_qk(0)
            for i, (p, qb, c) in enumerate(trips):
                pi = i % 2
                qkb = 2 * pi
                pvb = 4 + 2 * (qb % 2)
                ew = [W("sQK"), W("sNC", nc_done)]
                if I("sPV").n >= 1:
                    ew.append(W("sPV", I("sPV").n - 1))
                E.op("scalar", lambda e, pi=pi, qkb=qkb, g=g: e.activation(
                    ES[pi][:, :], PS[:, 512 * qkb:512 * qkb + 1024], AF.Exp,
                    bias=negC[:, g:g + 1], scale=0.125), waits=ew, inc=I("sEX"))
                if i + 1 < len(trips):
                    emit_qk(i + 1)

                def mmpv(e, p=p, c=c, pi=pi, pvb=pvb):
                    inst = None
                    for h in range(2):
                        for qt in range(4):
                            lhs = ES[pi][:, 512 * h + 128 * qt:512 * h + 128 * qt + 128]
                            e.ldweights(lhs)
                            inst = e.matmul(
                                PS[:, 512 * (pvb + h) + 65 * qt:
                                   512 * (pvb + h) + 65 * qt + 65],
                                lhs, vga(c)[:, 65 * (2 * p + h):65 * (2 * p + h) + 65],
                                start=(c == 0 and qt == 0), stop=(c == CHK - 1),
                                skip_group_check=True)
                    return inst
                pw = [W("sEX"), W("sGI", gi_done), W("sAT", I("sAT").n)]
                if I("sAOv").n >= 2:
                    pw.append(W("sAOv", I("sAOv").n - 2))
                E.op("tensor", mmpv, waits=pw, inc=I("sPV"))

                if c == CHK - 1:
                    rq = rQ8

                    def pvrec(e, pvb=pvb, rq=rq):
                        inst = None
                        for h in range(2):
                            for qt in range(4):
                                acc = PS[:, 512 * (pvb + h) + 65 * qt:
                                         512 * (pvb + h) + 65 * qt + 65]
                                inst = e.reciprocal(rq[:, 4 * h + qt:4 * h + qt + 1],
                                                    acc[:, 64:65])
                        return inst
                    E.op("vector", pvrec, waits=[W("sPV")], inc=I("sAOv"))

                    def pvev(e, p=p, qb=qb, pvb=pvb, rq=rq):
                        inst = None
                        for h in range(2):
                            for qt in range(4):
                                acc = PS[:, 512 * (pvb + h) + 65 * qt:
                                         512 * (pvb + h) + 65 * qt + 65]
                                inst = e.tensor_scalar(
                                    aotm(p, 4 * qb + qt)[:, 64 * h:64 * h + 64],
                                    acc[:, 0:64], rq[:, 4 * h + qt:4 * h + qt + 1],
                                    None, ALU.mult)
                        return inst
                    E.op("vector", pvev, waits=[W("sAOv")], inc=I("sAOv"))
            ao_done = I("sAOv").n
            for p in range(2):
                for t in range(NT):
                    emit_transpose8(aotm(p, t),
                                    (lambda pp, tt: (lambda _:
                                     aofm(pp)[:, 128 * tt:128 * tt + 128]))(p, t),
                                    [W("sAOv", ao_done), W("sOP", I("sOP").n)], npiece=1)
            E.op("vector", lambda e: e.memset(csc[:, 8:9], 0.0), inc=I("sAT"))
            at_done = I("sAT").n
            if g == 0:
                dump("es", ES[1], [W("sEX", 2)])
                dump("ao", aofm(0), [W("sAT", at_done)])
            E.op("gpsimd", lambda e, g=g: e.dma_start(
                out=WB[:, 2048 * (g % 2):2048 * (g % 2) + 2048],
                in_=wo_d[:, 2048 * g:2048 * g + 2048]),
                waits=[W("sRP", rp_all), W("sOP", I("sOP").n)], inc=I("sWO"), dma=True)
            wo_done = I("sWO").n
            for t in range(NT):
                pb = 2 * (t % 2)

                def mmo(e, t=t, g=g, pb=pb):
                    inst = None
                    for oh in range(2):
                        for icc in range(2):
                            inst = e.matmul(bk(pb + oh), aofm(icc)[:, 128 * t:128 * t + 128],
                                            WB[:, 2048 * (g % 2) + 1024 * icc + 512 * oh:
                                               2048 * (g % 2) + 1024 * icc + 512 * oh + 512],
                                            start=(icc == 0), stop=(icc == 1))
                    return inst
                wts = [W("sAT", at_done), W("sWO", wo_done)]
                if t >= 2:
                    wts.append(W("sRES", I("sRES").n - 1))
                E.op("tensor", mmo, waits=wts, inc=I("sOP"))

                def resid(e, t=t, pb=pb):
                    inst = None
                    for oh in range(2):
                        inst = e.scalar_tensor_tensor(
                            xr(t)[:, 512 * oh:512 * oh + 512], bk(pb + oh), 1.0,
                            xr(t)[:, 512 * oh:512 * oh + 512], ALU.mult, ALU.add)
                    return inst
                E.op("vector", resid, waits=[W("sOP")], inc=I("sRES"))
        if s1cut == "g0":
            _finish(locals())
            return nc
        dump_x("x1")

        def emit_ternary():
            WF = WB.bitcast(F32)
            wfc = [WF[:, 1024 * i:1024 * i + 1024] for i in range(2)]
            absw = WF[:, 2048:3072]
            sgn = WF[:, 3072:4096]
            res1 = I("sRES").n
            sb_marks = []
            tw_marks = []
            for ic in range(8):
                wts = [W("sRES", res1), W("sOP", I("sOP").n)]
                if ic >= 2:
                    wts.append(W("sSB", sb_marks[ic - 2]))
                E.op("sync", lambda e, ic=ic: e.dma_start(
                    out=wfc[ic % 2], in_=wf_d[:, 1024 * ic:1024 * ic + 1024]),
                    waits=wts, inc=I("sWF"), dma=True)
                aw = [W("sWF")]
                if ic >= 1:
                    aw.append(W("sTW", tw_marks[ic - 1]))
                E.op("scalar", lambda e, ic=ic: e.activation(absw, wfc[ic % 2], AF.Abs),
                     waits=aw, inc=I("sSB"))
                sb_marks.append(I("sSB").n)
                E.op("vector", lambda e: e.reduce_sum(csc[:, 9:10], absw, axis=AX.X),
                     waits=[W("sSB")], inc=I("sST"))
                if ic == 0:
                    E.op("vector", lambda e: e.tensor_copy(csc[:, 10:11], csc[:, 9:10]),
                         waits=[W("sST")], inc=I("sTW"))
                else:
                    E.op("vector", lambda e: e.tensor_add(csc[:, 10:11], csc[:, 10:11],
                                                          csc[:, 9:10]),
                         waits=[W("sST")], inc=I("sTW"))
                tw_marks.append(I("sTW").n)
            emit_preduce(csc[:, 10:11], csc[0:1, 11:12], ALU.add, [W("sTW")], I("sTW"))
            E.op("vector", lambda e: e.tensor_scalar_mul(csc[0:1, 11:12], csc[0:1, 11:12],
                                                         0.7 / (1024.0 * 1024.0)),
                 waits=[W("sTW")], inc=I("sTW"))
            emit_bcast(csc[0:1, 11:12], csc[:, 16:17], [W("sTW")], I("sTW"))
            d_done = I("sTW").n
            sb_marks = []
            tw_marks = []
            for ic in range(8):
                wts = [W("sTW", d_done)]
                if ic >= 2:
                    wts.append(W("sTW", I("sTW").n - 1))
                E.op("sync", lambda e, ic=ic: e.dma_start(
                    out=wfc[ic % 2], in_=wf_d[:, 1024 * ic:1024 * ic + 1024]),
                    waits=wts, inc=I("sWF"), dma=True)
                aw = [W("sWF"), W("sTW", d_done if ic == 0 else tw_marks[ic - 1])]
                E.op("scalar", lambda e, ic=ic: e.activation(absw, wfc[ic % 2], AF.Abs),
                     waits=aw, inc=I("sSB"))
                E.op("scalar", lambda e, ic=ic: e.sign(sgn, wfc[ic % 2]), inc=I("sSB"))
                E.op("vector", lambda e: e.tensor_single_scalar(QSC, absw, csc[:, 16:17],
                                                                ALU.is_gt), waits=[W("sSB")])
                E.op("vector", lambda e, ic=ic: e.tensor_mul(
                    TWS[:, 1024 * ic:1024 * ic + 1024], sgn, QSC))
                E.op("vector", lambda e: e.tensor_mul(absw, absw, QSC))
                E.op("vector", lambda e: e.reduce_sum(csc[:, 9:10], absw, axis=AX.X),
                     inc=I("sST"))
                E.op("vector", lambda e: e.reduce_sum(csc[:, 14:15], QSC, axis=AX.X),
                     inc=I("sST"))
                if ic == 0:
                    E.op("vector", lambda e: e.tensor_copy(csc[:, 10:11], csc[:, 9:10]),
                         waits=[W("sST")])
                    E.op("vector", lambda e: e.tensor_copy(csc[:, 15:16], csc[:, 14:15]),
                         inc=I("sTW"))
                else:
                    E.op("vector", lambda e: e.tensor_add(csc[:, 10:11], csc[:, 10:11],
                                                          csc[:, 9:10]), waits=[W("sST")])
                    E.op("vector", lambda e: e.tensor_add(csc[:, 15:16], csc[:, 15:16],
                                                          csc[:, 14:15]), inc=I("sTW"))
                tw_marks.append(I("sTW").n)
            emit_preduce(csc[:, 10:11], csc[0:1, 9:10], ALU.add, [W("sTW")], I("sTW"))
            emit_preduce(csc[:, 15:16], csc[0:1, 14:15], ALU.add, [W("sTW")], I("sTW"))
            E.op("vector", lambda e: e.tensor_scalar_max(csc[0:1, 14:15], csc[0:1, 14:15],
                                                         1.0), waits=[W("sTW")], inc=I("sST"))
            E.op("vector", lambda e: e.reciprocal(csc[0:1, 15:16], csc[0:1, 14:15]),
                 waits=[W("sST")], inc=I("sST"))
            E.op("vector", lambda e: e.tensor_mul(csc[0:1, 17:18], csc[0:1, 9:10],
                                                  csc[0:1, 15:16]),
                 waits=[W("sST")], inc=I("sTW"))

        def emit_ffn(which):
            res_base = I("sRES").n
            E.op("sync", lambda e: e.dma_start(out=GREP[0:1, :], in_=gbr_d[0:1, :]),
                 waits=[W("sKB", I("sKB").n), W("sRES", res_base)], inc=I("sGB"), dma=True)
            E.op("sync", lambda e: e.dma_start(out=BREP[0:1, :], in_=gbr_d[1:2, :]),
                 inc=I("sGB"), dma=True)
            emit_bcast(GREP[0:1, :], GREP, [W("sGB")], I("sGB"), width=1024)
            emit_bcast(BREP[0:1, :], BREP, [W("sGB", I("sGB").n)], I("sGB"), width=1024)
            gb_done = I("sGB").n
            for t in range(NT):
                emit_ln_tile(t, "ffn", None, [W("sRES", res_base)])
                E.op("vector", lambda e, t=t: e.tensor_scalar(
                    QSC, xr(t), ln_m[:, t:t + 1], ln_r[:, t:t + 1], ALU.subtract, ALU.mult),
                    waits=[W("sST"), W("sGB", gb_done), W("sEX", I("sEX").n)])
                E.op("vector", lambda e: e.tensor_mul(QSC, QSC, GREP))
                E.op("vector", lambda e: e.tensor_add(QSC, QSC, BREP))
                E.op("vector", lambda e: e.reduce_max(csc[:, 9:10], QSC, axis=AX.X,
                                                      apply_absolute_value=True),
                     inc=I("sST"))
                if t == 0:
                    E.op("vector", lambda e: e.tensor_copy(smax, csc[:, 9:10]),
                         waits=[W("sST")], inc=I("sMX"))
                else:
                    E.op("vector", lambda e: e.tensor_max(smax, smax, csc[:, 9:10]),
                         waits=[W("sST")], inc=I("sMX"))
            emit_preduce(smax, mx_s[0:1, 0:1], ALU.max, [W("sMX", I("sMX").n)], I("sMX"))
            E.op("gpsimd", lambda e: e.dma_start(out=mx_i[0:1, 0:1], in_=mx_s[0:1, 0:1]),
                 waits=[W("sMX", I("sMX").n)], inc=I("sMXc"), dma=True)
            E.op("gpsimd", lambda e: e.collective_compute(
                "AllReduce", ALU.max, replica_groups=ALL8,
                ins=[mx_i.ap().opt()], outs=[mx_o.ap().opt()]),
                waits=[W("sMXc")], inc=I("sAG"))
            E.op("gpsimd", lambda e: e.dma_start(out=mx_s[0:1, 1:2], in_=mx_o[0:1, 0:1]),
                 waits=[W("sAG", I("sAG").n)], inc=I("sMXc"), dma=True)
            E.op("vector", lambda e: e.tensor_scalar_max(mx_s[0:1, 2:3], mx_s[0:1, 1:2], 1e-8),
                 waits=[W("sMXc", I("sMXc").n)], inc=I("sST"))
            E.op("vector", lambda e: e.reciprocal(mx_s[0:1, 3:4], mx_s[0:1, 2:3]),
                 waits=[W("sST")], inc=I("sST"))
            E.op("vector", lambda e: e.tensor_scalar_mul(mx_s[0:1, 4:5], mx_s[0:1, 3:4],
                                                         127.0), waits=[W("sST")],
                 inc=I("sMX"))
            emit_bcast(mx_s[0:1, 4:5], csc[:, 18:19], [W("sMX", I("sMX").n)], I("sMX"))
            E.op("vector", lambda e: e.tensor_scalar_mul(mx_s[0:1, 5:6], mx_s[0:1, 2:3],
                                                         1.0 / 127.0),
                 waits=[W("sMX", I("sMX").n), W("sTW", I("sTW").n)], inc=I("sST"))
            E.op("vector", lambda e: e.tensor_mul(mx_s[0:1, 6:7], mx_s[0:1, 5:6],
                                                  csc[0:1, 17:18]),
                 waits=[W("sTW", I("sTW").n), W("sST")], inc=I("sMX"))
            emit_bcast(mx_s[0:1, 6:7], csc[:, 19:20], [W("sMX", I("sMX").n)], I("sMX"))
            so_done = I("sMX").n
            if which == 0:
                dump("tw", TWS[:, 0:1024], [W("sTW", I("sTW").n)])
                dump("scal", mx_s[0:1, 0:8], [W("sMX", so_done)])
            for t in range(NT):
                E.op("vector", lambda e, t=t: e.tensor_scalar(
                    QSC, xr(t), ln_m[:, t:t + 1], ln_r[:, t:t + 1], ALU.subtract, ALU.mult),
                    waits=[W("sMX", so_done)])
                E.op("vector", lambda e: e.tensor_mul(QSC, QSC, GREP))
                E.op("vector", lambda e: e.tensor_add(QSC, QSC, BREP))
                E.op("vector", lambda e: e.tensor_scalar(QSC, QSC, csc[:, 18:19], None,
                                                         ALU.mult))
                E.op("vector", lambda e: e.tensor_scalar(QSC, QSC, -127.0, 127.0,
                                                         ALU.max, ALU.min))
                E.op("vector", lambda e, t=t: e.tensor_scalar(
                    xnrot[t % 2], QSC, MAGIC, MAGIC, ALU.add, ALU.subtract), inc=I("sLN"))
                emit_transpose8(xnrot[t % 2],
                                (lambda tt: (lambda ic: fm(ic)[:, 128 * tt:128 * tt + 128]))(t),
                                [W("sLN")])
            fm_done = I("sFM").n
            for t in range(NT):
                pb = 2 * (t % 2)

                def mmf(e, t=t, pb=pb):
                    inst = None
                    for oh in range(2):
                        for ic in range(8):
                            inst = e.matmul(bk(pb + oh), fm(ic)[:, 128 * t:128 * t + 128],
                                            TWS[:, 1024 * ic + 512 * oh:
                                                1024 * ic + 512 * oh + 512],
                                            start=(ic == 0), stop=(ic == 7))
                    return inst
                wts = [W("sFM", fm_done), W("sTW", I("sTW").n)]
                if t >= 2:
                    wts.append(W("sRES", I("sRES").n - 1))
                E.op("tensor", mmf, waits=wts, inc=I("sOP"))

                def residf(e, t=t, pb=pb):
                    inst = None
                    for oh in range(2):
                        inst = e.scalar_tensor_tensor(
                            xr(t)[:, 512 * oh:512 * oh + 512], bk(pb + oh), csc[:, 19:20],
                            xr(t)[:, 512 * oh:512 * oh + 512], ALU.mult, ALU.add)
                    return inst
                E.op("vector", residf, waits=[W("sOP"), W("sMX", so_done)], inc=I("sRES"))

        def emit_attn_b():
            res_base = I("sRES").n
            E.op("vector", lambda e: e.tensor_scalar(mkA, tri_s, pc_s[:, 1:2], pc_s[:, 0:1],
                                                     ALU.mult, ALU.add),
                 waits=[W("sCT", ct_done), W("sRES", res_base)])
            E.op("vector", lambda e: e.tensor_scalar(mkB, tri_s, pc_s[:, 0:1], None,
                                                     ALU.mult), inc=I("sKB"))
            mk_done = I("sKB").n
            for t in range(NT):
                emit_ln_tile(t, "plain", xnrot[t % 2], [W("sRES", res_base)])
                emit_transpose8(xnrot[t % 2],
                                (lambda tt: (lambda ic: fm(ic)[:, 128 * tt:128 * tt + 128]))(t),
                                [W("sLN")])
            fm_done = I("sFM").n
            E.op("gpsimd", lambda e: e.dma_start(out=WB[:, :], in_=wvb_d[:, :]),
                 waits=[W("sRES", res_base), W("sOP", I("sOP").n)], inc=I("sW1"), dma=True)
            wvb_done = I("sW1").n
            for t in range(NT):
                pb = 2 * (t % 2)
                if t >= 2:
                    E.op("tensor", lambda e: e.nop(), waits=[W("sRP", I("sRP").n - 1)])
                emit_proj_tile(t, WB, [W("sW1", wvb_done), W("sCT", ct_done),
                                       W("sBB", 32)], pb,
                               bias_row=BVB, fm_wait=W("sFM", fm_done))
                bw = [W("sPRJ")]
                if t >= 2:
                    bw.append(W("sv%d" % (t % 2), 16 * (I("sv%d" % (t % 2)).n // 16)))
                E.op("vector", lambda e, t=t, pb=pb, bw=bw: e.tensor_copy(
                    kt_rot[t % 2], PS[:, 512 * pb:512 * pb + 1024]),
                    waits=bw, inc=I("sRP"))
                E.op("sync", lambda e, t=t: e.dma_start(
                    out=vc_i[128 * t:128 * t + 128, :], in_=kt_rot[t % 2]),
                    waits=[W("sRP")], inc=I("sv%d" % (t % 2)), dma=True)
            vo_done = None
            W3S = AOS[:, 10115:10139]
            E.op("gpsimd", lambda e: e.dma_start(out=W3S, in_=w3_d[:, :]),
                 inc=I("sW2"), dma=True)

            def mmp(e):
                inst = None
                for piece in range(4):
                    for ic in range(8):
                        inst = e.matmul(PS[0:3, 2048 + 512 * piece:2048 + 512 * piece + 512],
                                        W3S[:, 3 * ic:3 * ic + 3],
                                        fm(ic)[:, 512 * piece:512 * piece + 512],
                                        start=(ic == 0), stop=(ic == 7))
                return inst
            E.op("tensor", mmp, waits=[W("sFM", fm_done), W("sW2", I("sW2").n),
                                       W("sTR", I("sTR").n)], inc=I("sPP"))
            E.op("vector", lambda e: e.tensor_scalar(POS[0:3, :], PS[0:3, 2048:4096],
                                                     pc_s[:, 2:3][0:3, :], None, ALU.add),
                 waits=[W("sPP"), W("sCT", ct_done)], inc=I("sPP"))
            for piece in range(4):
                E.op("vector", lambda e, p=piece: e.scalar_tensor_tensor(
                    t1r[0:3, :], POS[0:3, 512 * p:512 * p + 512], 1.0,
                    POS[0:3, 512 * p:512 * p + 512], ALU.mult, ALU.mult),
                    waits=[W("sPP", I("sPP").n)], inc=I("sPP"))

                def mmn(e, p=piece):
                    e.matmul(PS[0:4, 2048 + 512 * p:2048 + 512 * p + 512],
                             SEL34A[0:3, 0:4], POS[0:3, 512 * p:512 * p + 512],
                             start=True, stop=False)
                    return e.matmul(PS[0:4, 2048 + 512 * p:2048 + 512 * p + 512],
                                    SEL34B[0:3, 0:4], t1r[0:3, :],
                                    start=False, stop=True)
                E.op("tensor", mmn, waits=[W("sPP"), W("sCT", ct_done)], inc=I("sPP"))
                E.op("vector", lambda e, p=piece: e.tensor_copy(
                    POS[0:4, 512 * p:512 * p + 512],
                    PS[0:4, 2048 + 512 * p:2048 + 512 * p + 512]),
                    waits=[W("sPP")], inc=I("sPP"))
            pp_done = I("sPP").n
            dump("pos", POS[0:16, :], [W("sPP", pp_done)])
            E.op("gpsimd", lambda e: e.dma_start(out=pp_i[:, :], in_=POS[0:4, :]),
                 waits=[W("sPP", pp_done)], inc=I("sPG"), dma=True)
            E.op("gpsimd", lambda e: e.collective_compute(
                "AllGather", ALU.bypass, replica_groups=PAIRS,
                ins=[pp_i.ap().opt()], outs=[pp_o.ap().opt()]),
                waits=[W("sPG")], inc=I("sAG"))
            E.op("gpsimd", lambda e: e.collective_compute(
                "AllGather", ALU.bypass, replica_groups=PAIRS,
                ins=[vc_i.ap().opt()], outs=[vc_o.ap().opt()]),
                waits=[W("sv0", I("sv0").n), W("sv1", I("sv1").n)], inc=I("sAG"))
            ag_done = I("sAG").n
            E.op("sync", lambda e: e.dma_start(
                out=GA[:, 0:16384].rearrange("p (c w) -> p c w", c=16),
                in_=vc_o[0:2048, :].rearrange("(c p) w -> p c w", p=128)),
                waits=[W("sAG", ag_done), W("sPV", I("sPV").n)], inc=I("sGI"), dma=True)
            E.op("sync", lambda e: e.dma_start(
                out=WB[:, :].rearrange("p (c w) -> p c w", c=8),
                in_=vc_o[2048:3072, :].rearrange("(c p) w -> p c w", p=128)),
                waits=[W("sAG", ag_done), W("sPRJ", I("sPRJ").n)], inc=I("sGI"), dma=True)
            E.op("sync", lambda e: e.dma_start(
                out=AOS[:, 0:8192].rearrange("p (c w) -> p c w", c=8),
                in_=vc_o[3072:4096, :].rearrange("(c p) w -> p c w", p=128)),
                waits=[W("sAG", ag_done), W("sAT", I("sAT").n), W("sOP", I("sOP").n)],
                inc=I("sGI"), dma=True)
            gi_done = I("sGI").n
            E.op("vector", lambda e: e.memset(khb[0:5, :], 1.0),
                 waits=[W("sPP", pp_done)], inc=I("sKB"))
            mm_marks = {}
            evac_marks = {}
            src_marks = {}
            for piece in range(12):
                isq = piece >= 8
                if not isq:
                    half, pp = piece // 4, piece % 4

                    def ldp(e, half=half, pp=pp):
                        return e.dma_start(out=POSG[0:4, :],
                                           in_=pp_o[4 * half:4 * half + 4,
                                                    512 * pp:512 * pp + 512])
                    kw = [W("sAG", ag_done), W("sv0", I("sv0").n), W("sv1", I("sv1").n)]
                    if piece:
                        kw.append(W("sKB", src_marks[piece - 1]))
                    E.op("sync", ldp, waits=kw, inc=I("sGI"), dma=True)
                    dst = KB13[:, 2048 * half + 512 * pp:2048 * half + 512 * pp + 512]
                    srcf = POSG[0:4, :]
                    cw = [W("sGI", I("sGI").n)]
                    sela, selb = SELKA[0:5, :], SELKB[0:4, :]
                else:
                    pp = piece - 8
                    dst = QB13[:, 512 * pp:512 * pp + 512]
                    srcf = POS[0:4, 512 * pp:512 * pp + 512]
                    cw = [W("sPP", pp_done)]
                    sela, selb = SELQA[0:5, :], SELQB[0:4, :]
                if piece:
                    cw.append(W("sPP", mm_marks[piece - 1]))
                E.op("vector", lambda e, s=srcf: e.tensor_copy(khb[0:4, :], s),
                     waits=cw)
                E.op("vector", lambda e, s=srcf: e.tensor_sub(klb[0:4, :], s,
                                                              khb[0:4, :]),
                     inc=I("sKB"))
                src_marks[piece] = I("sKB").n
                psb = PS[0:13, 3072 + 512 * (piece % 2):3584 + 512 * (piece % 2)]

                def mmaug(e, psb=psb, sela=sela, selb=selb):
                    e.matmul(psb, sela, khb[0:5, :], start=True, stop=False)
                    return e.matmul(psb, selb, klb[0:4, :], start=False, stop=True)
                mw = [W("sKB", src_marks[piece]), W("sCT", ct_done)]
                if piece >= 2:
                    mw.append(W("sKB", evac_marks[piece - 2]))
                E.op("tensor", mmaug, waits=mw, inc=I("sPP"))
                mm_marks[piece] = I("sPP").n
                E.op("vector", lambda e, d=dst, psb=psb: e.tensor_copy(d[0:13, :], psb),
                     waits=[W("sPP", mm_marks[piece])], inc=I("sKB"))
                evac_marks[piece] = I("sKB").n
            kb_done = I("sKB").n
            dump("kb13", KB13[0:16, :], [W("sKB", kb_done)])
            for t in range(NT):
                J = list(range(0, t + 1)) + list(range(16, 16 + t + 1))
                nj = len(J)
                for bi in range(0, nj, 4):
                    js = J[bi:bi + 4]
                    pi = (I("sDM").n) % 2

                    def mmd(e, js=js, pi=pi, t=t):
                        inst = None
                        for idx, j in enumerate(js):
                            inst = e.matmul(
                                PS[:, 512 * pi + 128 * idx:512 * pi + 128 * idx + 128],
                                KB13[0:13, 128 * j:128 * j + 128],
                                QB13[0:13, 128 * t:128 * t + 128],
                                start=(idx == 0), stop=(idx == len(js) - 1),
                                skip_group_check=True)
                        return inst
                    wd = [W("sKB", kb_done), W("sRES", res_base)]
                    if I("sRL").n >= 1:
                        wd.append(W("sRL", I("sRL").n - 1))
                    E.op("tensor", mmd, waits=wd, inc=I("sDM"))
                    E.op("vector", lambda e, bi=bi, pi=pi, n4=len(js): e.tensor_scalar(
                        PSTG[:, 128 * bi:128 * bi + 128 * n4],
                        PS[:, 512 * pi:512 * pi + 128 * n4], 1e-20, None, ALU.max),
                        waits=[W("sDM")], inc=I("sRL"))
                E.op("scalar", lambda e, nj=nj: e.activation(
                    PSTG[:, 0:128 * nj], PSTG[:, 0:128 * nj], AF.Ln),
                    waits=[W("sRL", I("sRL").n)])
                E.op("scalar", lambda e, nj=nj: e.activation(
                    PSTG[:, 0:128 * nj], PSTG[:, 0:128 * nj], AF.Exp, scale=0.5))
                E.op("scalar", lambda e, nj=nj: e.activation(
                    PSTG[:, 0:128 * nj], PSTG[:, 0:128 * nj], AF.Exp, scale=-0.125),
                    inc=I("sEB"))
                E.op("vector", lambda e, t=t: e.tensor_mul(
                    PSTG[:, 128 * t:128 * t + 128], PSTG[:, 128 * t:128 * t + 128], mkA),
                    waits=[W("sEB"), W("sKB", mk_done)])
                E.op("vector", lambda e, nj=nj: e.tensor_mul(
                    PSTG[:, 128 * (nj - 1):128 * nj], PSTG[:, 128 * (nj - 1):128 * nj], mkB),
                    inc=I("sPB"))
                pb_done = I("sPB").n
                if t == 1:
                    dump("esb", PSTG[:, 0:4352], [W("sPB", pb_done)])
                def mms(e, t=t, nj=nj):
                    inst = None
                    for idx in range(nj):
                        e.ldweights(PSTG[:, 128 * idx:128 * idx + 128])
                        inst = e.matmul(PS[:, 1024 + t:1025 + t],
                                        PSTG[:, 128 * idx:128 * idx + 128], ONC,
                                        start=(idx == 0), stop=(idx == nj - 1),
                                        skip_group_check=True)
                    return inst
                E.op("tensor", mms, waits=[W("sPB", pb_done), W("sRS", I("sRS").n),
                                           W("sCT", ct_done)], inc=I("sDM"))
                E.op("vector", lambda e, t=t: e.reciprocal(rsumb[:, t:t + 1],
                                                           PS[:, 1024 + t:1025 + t]),
                     waits=[W("sDM")], inc=I("sRS"))

                def mmpv2(e, t=t, J=J):
                    inst = None
                    for ds in range(8):
                        for idx, j in enumerate(J):
                            e.ldweights(vbt(j)[:, 128 * ds:128 * ds + 128])
                            inst = e.matmul(
                                PS[:, 2048 + 128 * ds:2048 + 128 * ds + 128],
                                vbt(j)[:, 128 * ds:128 * ds + 128],
                                PSTG[:, 128 * idx:128 * idx + 128],
                                start=((ds == 0 or ds == 4) and idx == 0),
                                stop=(idx == len(J) - 1), skip_group_check=True)
                    return inst
                pw = [W("sPB", pb_done), W("sGI", gi_done)]
                if I("sOB").n >= 1:
                    pw.append(W("sOB", I("sOB").n))
                E.op("tensor", mmpv2, waits=pw, inc=I("sPV"))

                def evb(e, t=t):
                    inst = None
                    for ds in range(8):
                        inst = e.tensor_copy(fm(ds)[:, 128 * t:128 * t + 128],
                                             PS[:, 2048 + 128 * ds:2048 + 128 * ds + 128])
                    return inst
                E.op("vector", evb, waits=[W("sPV")], inc=I("sOB"))
            ob_done = I("sOB").n
            dump("aofb", fm(0), [W("sOB", ob_done)])
            BOREP = GREP
            E.op("sync", lambda e: e.dma_start(out=GREP[0:1, :], in_=bb_d[1:2, :]),
                 waits=[W("sPV", I("sPV").n), W("sOB", I("sOB").n)], inc=I("sGB"), dma=True)
            emit_bcast(GREP[0:1, :], GREP, [W("sGB", I("sGB").n)], I("sGB"), width=1024)
            borep_done = I("sGB").n
            E.op("gpsimd", lambda e: e.dma_start(out=WB[:, :], in_=wob_d[:, :]),
                 waits=[W("sPV", I("sPV").n)], inc=I("sW1"), dma=True)
            wob_done = I("sW1").n
            for t in range(NT):
                pb = 2 * (t % 2)

                def mmob(e, t=t, pb=pb):
                    inst = None
                    for oh in range(2):
                        for ic in range(8):
                            inst = e.matmul(bk(pb + oh), fm(ic)[:, 128 * t:128 * t + 128],
                                            WB[:, 1024 * ic + 512 * oh:
                                               1024 * ic + 512 * oh + 512],
                                            start=(ic == 0), stop=(ic == 7))
                    return inst
                wts = [W("sOB", ob_done), W("sW1", wob_done)]
                if t >= 2:
                    wts.append(W("sRES", I("sRES").n - 1))
                E.op("tensor", mmob, waits=wts, inc=I("sOP"))

                def residb(e, t=t, pb=pb):
                    for oh in range(2):
                        e.scalar_tensor_tensor(
                            xr(t)[:, 512 * oh:512 * oh + 512], bk(pb + oh),
                            rsumb[:, t:t + 1], xr(t)[:, 512 * oh:512 * oh + 512],
                            ALU.mult, ALU.add)
                    return e.tensor_add(xr(t), xr(t), BOREP)
                E.op("vector", residb, waits=[W("sOP"), W("sRS", I("sRS").n),
                                              W("sGB", borep_done)], inc=I("sRES"))

        if stages >= 2:
            emit_ternary()
            emit_ffn(0)
            dump_x("x2")
        if stages >= 3:
            emit_attn_b()
            dump_x("x3")
        if stages >= 4:
            emit_ffn(1)

        for t in range(NT):
            E.op("sync", lambda e, t=t: e.dma_start(out=out_d[128 * t:128 * t + 128, :],
                                                    in_=xr(t)),
                 waits=[W("sRES", I("sRES").n)], inc=I("sX"), dma=True)
        E.op("sync", lambda e: e.nop(), waits=[W("sX", I("sX").n), W("sDP", I("sDP").n)])

        E.emit(nc)
    return nc


def _rope_perm():
    p = []
    for h in range(NH):
        p += [64 * h + 2 * i for i in range(32)]
        p += [64 * h + 2 * i + 1 for i in range(32)]
    return np.array(p, dtype=np.int64)


def _wmat(w, perm=None):
    wt = np.asarray(w).T
    if perm is not None:
        wt = wt[:, perm]
    return np.ascontiguousarray(
        wt.reshape(8, 128, 1024).transpose(1, 0, 2).reshape(128, 8192)).astype(np.float32)


def _stage_inputs(inputs, core):
    b, p = core // 2, core % 2
    x = np.asarray(inputs["L0_attn_a_x"])
    xs = np.ascontiguousarray(x.reshape(4, 32, 128, D)[b, p::2].reshape(TOK, D))
    perm = _rope_perm()
    tpos = ((2 * np.arange(16)[:, None] + p) * 128 + np.arange(128)[None, :]).reshape(-1)
    inv = 1.0 / 10000.0 ** (np.arange(0, HD, 2, dtype=np.float64) / HD)
    ang = tpos[:, None] * inv[None, :]
    ct = np.ascontiguousarray(np.cos(ang).astype(np.float32).reshape(16, 128, 32)
                              .transpose(1, 0, 2).reshape(128, 512))
    st = np.ascontiguousarray(np.sin(ang).astype(np.float32).reshape(16, 128, 32)
                              .transpose(1, 0, 2).reshape(128, 512))
    gbr = np.ascontiguousarray(np.stack([np.asarray(inputs["L0_ffn_gamma"]),
                                         np.asarray(inputs["L0_ffn_beta"])])).astype(np.float32)
    bb = np.ascontiguousarray(np.stack([np.asarray(inputs["L0_attn_b_bv"]),
                                        np.asarray(inputs["L0_attn_b_bo"])])).astype(np.float32)
    pcm = np.zeros((128, 8), np.float32)
    pcm[:, 0] = float(p)
    pcm[:, 1] = 1.0 - float(p)
    pcm[0:3, 2] = np.asarray(inputs["L0_attn_b_b3d"])
    w3 = np.asarray(inputs["L0_attn_b_w3d"]).T
    w3s = np.ascontiguousarray(w3.reshape(8, 128, 3).transpose(1, 0, 2).reshape(128, 24)
                               ).astype(np.float32)
    return {
        "x": xs.astype(np.float32),
        "wq": _wmat(inputs["L0_attn_a_wq"], perm),
        "wk": _wmat(inputs["L0_attn_a_wk"], perm),
        "wv": _wmat(inputs["L0_attn_a_wv"]),
        "wo": _wmat(inputs["L0_attn_a_wo"]),
        "wf": _wmat(inputs["L0_ffn_weight"]),
        "wvb": _wmat(inputs["L0_attn_b_wv"]),
        "wob": _wmat(inputs["L0_attn_b_wo"]),
        "w3": w3s, "gbr": gbr, "bb": bb, "pc": pcm, "ct": ct, "st": st,
    }


_NC_CACHE = {}


def kernel(**inputs):
    key = ("k", STAGES)
    if key not in _NC_CACHE:
        _NC_CACHE[key] = build_program(stages=STAGES)
    nc = _NC_CACHE[key]
    in_maps = [_stage_inputs(inputs, c) for c in range(8)]
    res = run_bass_kernel_spmd(nc, in_maps, core_ids=list(range(8)))
    out = np.zeros((4, 4096, D), np.float32)
    ov = out.reshape(4, 32, 128, D)
    for c in range(8):
        b, p = c // 2, c % 2
        ov[b, p::2] = res.results[c]["out"].reshape(16, 128, D)
    return out


# version 04
# speedup vs baseline: 1.3448x; 1.0120x over previous
import numpy as np
import ml_dtypes
import concourse.bass as bass
import concourse.mybir as mybir
from concourse.bass_utils import run_bass_kernel_spmd
from concourse import library_config
from concourse.bass_isa import ReduceOp

F32 = mybir.dt.float32
BF16 = mybir.dt.bfloat16
AF = mybir.ActivationFunctionType
ALU = mybir.AluOpType
AX = mybir.AxisListType
bf16 = ml_dtypes.bfloat16

TOK, NT, NH, HD, D = 2048, 16, 16, 64, 1024
NG, CHK = 4, 32
EPS = 1e-5
MAGIC = float(np.float32(1.5 * 2 ** 23))
PAIRS = [[0, 1], [2, 3], [4, 5], [6, 7]]
ALL8 = [list(range(8))]
STAGES = 4


class Sem:
    __slots__ = ("name", "h", "n")

    def __init__(self, name):
        self.name, self.h, self.n = name, None, 0


class Emit:
    ENGINES = ("sync", "scalar", "tensor", "vector", "gpsimd")

    def __init__(self):
        self.items = {e: [] for e in self.ENGINES}
        self.done = {e: {} for e in self.ENGINES}

    def op(self, eng, fn, waits=(), inc=None, dma=False):
        amt = 16 if dma else 1
        d = self.done[eng]
        ws = []
        for s, v in waits:
            if v > 0 and d.get(s, -1) < v:
                ws.append((s, v))
                d[s] = v
        if inc is not None:
            inc.n += amt
        self.items[eng].append((fn, tuple(ws), inc, amt))

    def emit(self, nc):
        with nc.Block() as block:
            for e in self.ENGINES:
                if not self.items[e]:
                    continue

                def mk(items):
                    def body(eng):
                        for fn, ws, inc, amt in items:
                            for s, v in ws:
                                eng.wait_ge(s.h, v)
                            inst = fn(eng)
                            if inc is not None:
                                inst.then_inc(inc.h, amt)
                    return body
                getattr(block, e)(mk(self.items[e]))


def build_program(dumps=(), stages=STAGES, s1cut=None):
    nc = bass.Bass("TRN2", target_bir_lowering=False)
    E = Emit()
    dumps = set(dumps)

    x_d = nc.dram_tensor("x", [TOK, D], F32, kind="ExternalInput")
    out_d = nc.dram_tensor("out", [TOK, D], F32, kind="ExternalOutput")
    wq_d = nc.dram_tensor("wq", [128, 8192], F32, kind="ExternalInput")
    wk_d = nc.dram_tensor("wk", [128, 8192], F32, kind="ExternalInput")
    wv_d = nc.dram_tensor("wv", [128, 8192], F32, kind="ExternalInput")
    wo_d = nc.dram_tensor("wo", [128, 8192], F32, kind="ExternalInput")
    wf_d = nc.dram_tensor("wf", [128, 8192], F32, kind="ExternalInput")
    wvb_d = nc.dram_tensor("wvb", [128, 8192], F32, kind="ExternalInput")
    wob_d = nc.dram_tensor("wob", [128, 8192], F32, kind="ExternalInput")
    w3_d = nc.dram_tensor("w3", [128, 24], F32, kind="ExternalInput")
    gbr_d = nc.dram_tensor("gbr", [2, D], F32, kind="ExternalInput")
    bb_d = nc.dram_tensor("bb", [2, D], F32, kind="ExternalInput")
    pc_d = nc.dram_tensor("pc", [128, 8], F32, kind="ExternalInput")
    ct_d = nc.dram_tensor("ct", [128, 512], F32, kind="ExternalInput")
    st_d = nc.dram_tensor("st", [128, 512], F32, kind="ExternalInput")

    kb_i = nc.dram_tensor("kb_i", [TOK, D], BF16)
    kb_o = nc.dram_tensor("kb_o", [2 * TOK, D], BF16)
    vb_i = nc.dram_tensor("vb_i", [TOK, NH * 65], BF16)
    vb_o = nc.dram_tensor("vb_o", [2 * TOK, NH * 65], BF16)
    vc_i = nc.dram_tensor("vc_i", [TOK, D], BF16)
    vc_o = nc.dram_tensor("vc_o", [2 * TOK, D], BF16)
    pp_i = nc.dram_tensor("pp_i", [4, TOK], F32)
    pp_o = nc.dram_tensor("pp_o", [8, TOK], F32)
    mx_i = nc.dram_tensor("mx_i", [1, 8], F32)
    mx_o = nc.dram_tensor("mx_o", [1, 8], F32)

    ident_d = nc.inline_tensor(np.eye(128, dtype=bf16), name="identb")
    ones2_np = np.zeros((128, 2), dtype=bf16)
    ones2_np[0:64, 0] = 1
    ones2_np[64:128, 1] = 1
    ones2_d = nc.inline_tensor(ones2_np, name="ones2")
    onescol_d = nc.inline_tensor(np.ones((128, 1), dtype=bf16), name="onescol")
    onesrow_d = nc.inline_tensor(np.ones((1, 128), dtype=bf16), name="onesrow")
    identf_d = nc.inline_tensor(np.eye(128, dtype=np.float32), name="identf")
    onesrowf_d = nc.inline_tensor(np.ones((1, 128), dtype=np.float32), name="onesrowf")
    tri_np = np.ascontiguousarray(
        np.tril(np.ones((128, 128), dtype=np.float32)).T.astype(bf16))
    tri_d = nc.inline_tensor(tri_np, name="trikq")
    ska = np.zeros((5, 13), dtype=bf16)
    for i in range(3):
        ska[i, i] = 1; ska[i, 6 + i] = 1
    ska[3, 11] = 1; ska[4, 9] = 1; ska[4, 10] = 1
    skb = np.zeros((4, 13), dtype=bf16)
    for i in range(3):
        skb[i, 3 + i] = 1
    skb[3, 12] = 1
    sqa = np.zeros((5, 13), dtype=bf16)
    for i in range(3):
        sqa[i, i] = -2; sqa[i, 3 + i] = -2
    sqa[3, 9] = 1; sqa[4, 11] = 1; sqa[4, 12] = 1
    sqb = np.zeros((4, 13), dtype=bf16)
    for i in range(3):
        sqb[i, 6 + i] = -2
    sqb[3, 10] = 1
    selka_d = nc.inline_tensor(ska, name="selka")
    selkb_d = nc.inline_tensor(skb, name="selkb")
    selqa_d = nc.inline_tensor(sqa, name="selqa")
    selqb_d = nc.inline_tensor(sqb, name="selqb")
    s34a = np.zeros((3, 4), dtype=np.float32)
    for i in range(3):
        s34a[i, i] = 1
    s34b = np.zeros((3, 4), dtype=np.float32)
    s34b[:, 3] = 1
    sel34a_d = nc.inline_tensor(s34a, name="sel34a")
    sel34b_d = nc.inline_tensor(s34b, name="sel34b")

    dump_t = {}
    for tag, shape, dt in [
        ("xn", (128, D), BF16), ("qtm", (128, 1024), BF16), ("qfm", (128, 2048), BF16),
        ("kfm", (128, 4096), BF16), ("es", (128, 1024), BF16), ("ao", (128, 2048), BF16),
        ("x1", (TOK, D), F32), ("x2", (TOK, D), F32), ("x3", (TOK, D), F32),
        ("tw", (128, 1024), BF16), ("pos", (16, TOK), F32), ("kb13", (16, 4096), BF16),
        ("esb", (128, 4352), BF16), ("scal", (1, 8), F32), ("aofb", (128, 2048), BF16),
        ("vga", (128, 260), BF16), ("xraw", (128, 1024), F32),
        ("stats", (128, 16), F32), ("lnm", (128, 16), F32), ("lnr", (128, 16), F32),
        ("sumsq", (128, 16), F32), ("qsc", (128, 1024), F32),
    ]:
        if tag in dumps:
            dump_t[tag] = nc.dram_tensor("d_" + tag, list(shape), dt, kind="ExternalOutput")

    names = ("sX sCT sW1 sWO sW2 sLN sTR sFM sPRJ sRP sKO sVO sAG sGI sKF sNC sQK "
             "sEX sPV sAOv sAT sAF sOP sRES sDP sMX sMXc sTW sWF sPP sPG sKB sDM "
             "sRL sRS sEB sPB sOB sQT sST sSB sGB "
             "sx0 sx1 sx2 sx3 sk0 sk1 sv0 sv1 sBB").split()
    SM = {n: Sem(n) for n in names}

    def W(sem, v=None):
        return (SM[sem], SM[sem].n if v is None else v)

    def I(sem):
        return SM[sem]

    with (
        nc.sbuf_tensor("XR", [128, NT * 1024], F32) as XR,
        nc.sbuf_tensor("FM", [128, 16384], BF16) as FM,
        nc.sbuf_tensor("GA", [128, 16640], BF16) as GA,
        nc.sbuf_tensor("TWS", [128, 8192], BF16) as TWS,
        nc.sbuf_tensor("WB", [128, 8192], BF16) as WB,
        nc.sbuf_tensor("AOS", [128, 10240], BF16) as AOS,
        nc.sbuf_tensor("PO", [128, 6912], F32) as PO,
        nc.psum_tensor("PS", [128, 4096], F32) as PS,
    ):
        for s in SM.values():
            s.h = nc.alloc_semaphore(s.name)

        def xr(t):
            return XR[:, 1024 * t:1024 * t + 1024]

        def fm(ic):
            return FM[:, 2048 * ic:2048 * ic + 2048]

        def qtm(t):
            return GA[:, 1024 * t:1024 * t + 1024]

        def katm(j):
            return GA[:, 256 * j:256 * j + 256]

        def vga(j):
            return GA[:, 8192 + 260 * j:8192 + 260 * j + 260]

        def kfm(i):
            return TWS[:, 4096 * i:4096 * i + 4096]

        def vbt(j):
            if j < 16:
                return GA[:, 1024 * j:1024 * j + 1024]
            if j < 24:
                return WB[:, 1024 * (j - 16):1024 * (j - 16) + 1024]
            return AOS[:, 1024 * (j - 24):1024 * (j - 24) + 1024]

        def bk(i, w=512):
            return PS[:, 512 * i:512 * i + w]

        ct_s = PO[:, 0:512]
        st_s = PO[:, 512:1024]
        xnrot = [PO[:, 1024 + 512 * i:1536 + 512 * i].bitcast(BF16) for i in range(2)]
        ES = [PO[:, 2048 + 512 * i:2560 + 512 * i].bitcast(BF16) for i in range(2)]
        QSC = PO[:, 2048:3072]
        st6 = PO[:, 3072:3084]
        mv = PO[:, 3084:3086]
        csc = PO[:, 3088:3112]
        dbgS = PO[:, 6296:6312]
        ln_m = PO[:, 3112:3128]
        ln_r = PO[:, 3128:3144]
        smax = PO[:, 3144:3145]
        mx_s = PO[:, 3152:3160]
        rQ8 = PO[:, 3160:3168]
        negC = PO[:, 3168:3172]
        t1r = PO[:, 3200:3712]
        t2r = PO[:, 3712:4224]
        sqscr = PO[:, 3200:3456].bitcast(BF16)
        GREP = PO[:, 4224:5248]
        BREP = PO[:, 5248:6272]
        kvrot = [PO[:, 4224 + 520 * i:4744 + 520 * i].bitcast(BF16) for i in range(2)]
        kt_rot = [PO[:, 5264 + 512 * i:5776 + 512 * i].bitcast(BF16) for i in range(2)]
        pc_s = PO[:, 6288:6296]
        rsumb = PO[:, 6352:6368]
        IDF = PO[:, 6656:6784]
        ONRF = PO[0:1, 6784:6912]
        mkA = PO[:, 6400:6464].bitcast(BF16)
        mkB = PO[:, 6464:6528].bitcast(BF16)
        tri_s = PO[:, 6528:6592].bitcast(BF16)
        POS = PO[:, 0:2048]
        POSG = PO[:, 2048:2560]
        khb = PO[:, 2560:2816].bitcast(BF16)
        klb = PO[:, 2816:3072].bitcast(BF16)
        SEL34A = PO[:, 6312:6316]
        SEL34B = PO[:, 6316:6320]
        SELKA = AOS[:, 10144:10157]
        SELKB = AOS[:, 10160:10173]
        SELQA = AOS[:, 10176:10189]
        SELQB = AOS[:, 10208:10221]
        PSTG = PO[:, 0:2176].bitcast(BF16)
        QB13 = PO[:, 3200:4224].bitcast(BF16)
        KB13 = PO[:, 4224:6272].bitcast(BF16)
        ONES3F = PO[:, 3184:3185]

        ID = AOS[:, 9984:10112]
        ONR2 = AOS[0:1, 8704:8832]
        BVB = AOS[0:1, 8832:9856]
        BOB = AOS[1:2, 8832:9856]

        def aotm(p, t):
            return AOS[:, 2048 * p + 128 * t:2048 * p + 128 * t + 128]

        def aofm(i):
            return AOS[:, 4096 + 2048 * i:4096 + 2048 * i + 2048]

        TRP = [PS[:, 2048 + 512 * i:2560 + 512 * i].bitcast(BF16) for i in range(2)]

        def _finish(_ctx):
            for t in range(NT):
                E.op("sync", lambda e, t=t: e.dma_start(
                    out=out_d[128 * t:128 * t + 128, :], in_=xr(t)),
                    waits=[W("sRES", I("sRES").n), W("sLN", I("sLN").n),
                           W("sFM", I("sFM").n), W("sRP", I("sRP").n),
                           W("sQT", I("sQT").n)],
                    inc=I("sX"), dma=True)
            E.op("sync", lambda e: e.nop(), waits=[W("sX", I("sX").n), W("sDP", I("sDP").n),
                                                   W("sk0", I("sk0").n), W("sk1", I("sk1").n),
                                                   W("sv0", I("sv0").n), W("sv1", I("sv1").n),
                                                   W("sAG", I("sAG").n)])
            E.emit(nc)

        def _preload_tables(e):
            return e.add_instruction(mybir.InstLoadActFuncSet(
                name=nc.get_next_instruction_name(), act_func_set_id=6, ins=[], outs=[]))
        E.op("scalar", _preload_tables)
        for t in range(NT):
            sx = "sx%d" % (t % 4)
            E.op("sync", lambda e, t=t: e.dma_start(out=xr(t), in_=x_d[128 * t:128 * t + 128, :]),
                 waits=([W(sx, 16 * (t // 4))] if t >= 4 else []), inc=I(sx), dma=True)
        for dst, src in [(ct_s, ct_d), (st_s, st_d), (ID, ident_d), (tri_s, tri_d),
                         (pc_s, pc_d)]:
            E.op("sync", lambda e, d=dst, s=src: e.dma_start(out=d, in_=s[:, :]),
                 inc=I("sCT"), dma=True)
        ON2 = AOS[:, 10112:10114]
        ONC = AOS[:, 10114:10115]
        E.op("sync", lambda e: e.dma_start(out=ON2, in_=ones2_d[:, :]), inc=I("sCT"), dma=True)
        E.op("sync", lambda e: e.dma_start(out=ONC, in_=onescol_d[:, :]), inc=I("sCT"), dma=True)
        E.op("sync", lambda e: e.dma_start(out=ONR2, in_=onesrow_d[:, :]), inc=I("sCT"), dma=True)
        E.op("sync", lambda e: e.dma_start(out=IDF, in_=identf_d[:, :]), inc=I("sCT"), dma=True)
        E.op("sync", lambda e: e.dma_start(out=ONRF, in_=onesrowf_d[:, :]), inc=I("sCT"), dma=True)
        for dst, src in [(SELKA[0:5, :], selka_d), (SELKB[0:4, :], selkb_d),
                         (SELQA[0:5, :], selqa_d), (SELQB[0:4, :], selqb_d),
                         (SEL34A[0:3, :], sel34a_d), (SEL34B[0:3, :], sel34b_d)]:
            E.op("sync", lambda e, d=dst, s=src: e.dma_start(out=d, in_=s[:, :]),
                 inc=I("sCT"), dma=True)
        E.op("gpsimd", lambda e: e.dma_start(out=BVB, in_=bb_d[0:1, :]), inc=I("sBB"), dma=True)
        E.op("gpsimd", lambda e: e.dma_start(out=BOB, in_=bb_d[1:2, :]), inc=I("sBB"), dma=True)
        E.op("gpsimd", lambda e: e.dma_start(out=WB[:, :], in_=wq_d[:, :]), inc=I("sW1"), dma=True)
        E.op("gpsimd", lambda e: e.dma_start(out=TWS[:, :], in_=wk_d[:, :]), inc=I("sW2"), dma=True)
        ct_done = I("sCT").n

        def dump(tag, ap, waits):
            if tag in dump_t:
                E.op("sync", lambda e, t=dump_t[tag], a=ap: e.dma_start(out=t.ap(), in_=a),
                     waits=waits, inc=I("sDP"), dma=True)

        def dump_x(tag):
            if tag in dump_t:
                for t in range(NT):
                    E.op("sync", lambda e, t=t: e.dma_start(
                        out=dump_t[tag].ap()[128 * t:128 * t + 128, :], in_=xr(t)),
                        waits=[W("sRES")], inc=I("sDP"), dma=True)

        def emit_preduce(src_col, dst11, op, waits, inc_sem, cnt=128):
            E.op("tensor", lambda e: e.transpose(PS[0:1, 3584:3584 + cnt],
                                                 src_col, IDF[0:cnt, 0:cnt]),
                 waits=list(waits) + [W("sCT", ct_done)], inc=I("sNC"))
            E.op("vector", lambda e: e.tensor_reduce(dst11, PS[0:1, 3584:3584 + cnt],
                                                     AX.X, op), waits=[W("sNC")],
                 inc=inc_sem)

        def emit_bcast(src11, dst_col, waits, inc_sem, width=1):
            base = 3584 if width <= 512 else 3072
            def mm(e):
                inst = None
                for o in range(0, width, 512):
                    wdt = min(512, width - o)
                    inst = e.matmul(PS[:, base + o:base + o + wdt], ONRF,
                                    src11[0:1, o:o + wdt], start=True, stop=True)
                return inst
            E.op("tensor", mm, waits=list(waits) + [W("sCT", ct_done)], inc=I("sNC"))
            E.op("vector", lambda e: e.tensor_copy(dst_col, PS[:, base:base + width]),
                 waits=[W("sNC")], inc=inc_sem)

        def emit_ln_tile(t, kind, out_tile, in_waits):
            E.op("vector", lambda e, t=t: e.scalar_tensor_tensor(
                QSC, xr(t), 1.0, xr(t), ALU.mult, ALU.mult), waits=in_waits,
                inc=I("sST"))
            E.op("vector", lambda e, t=t: e.reduce_sum(csc[:, 9:10], xr(t), axis=AX.X),
                 inc=I("sST"))
            E.op("vector", lambda e: e.reduce_sum(csc[:, 10:11], QSC, axis=AX.X),
                 waits=[W("sST")], inc=I("sST"))
            if "sumsq" in dump_t and t < 16:
                E.op("vector", lambda e, t=t: e.tensor_copy(
                    dbgS[:, t:t + 1], csc[:, 10:11]), waits=[W("sST")])
            E.op("vector", lambda e: e.tensor_scalar_mul(mv[:, 0:1], csc[:, 9:10],
                                                         1.0 / 1024.0), waits=[W("sST")])
            E.op("vector", lambda e: e.tensor_scalar_mul(csc[:, 11:12], csc[:, 10:11],
                                                         1.0 / 1024.0), inc=I("sST"))
            E.op("vector", lambda e: e.scalar_tensor_tensor(
                csc[:, 12:13], mv[:, 0:1], 1.0, mv[:, 0:1], ALU.mult, ALU.mult),
                waits=[W("sST")], inc=I("sST"))
            E.op("vector", lambda e: e.tensor_sub(mv[:, 1:2], csc[:, 11:12], csc[:, 12:13]),
                 waits=[W("sST")], inc=I("sST"))
            E.op("vector", lambda e: e.tensor_scalar_add(csc[:, 0:1], mv[:, 1:2], EPS),
                 waits=[W("sST")], inc=I("sST"))
            E.op("scalar", lambda e: e.activation(csc[:, 1:2], csc[:, 0:1], AF.Ln),
                 waits=[W("sST")])
            E.op("scalar", lambda e: e.activation(csc[:, 2:3], csc[:, 1:2], AF.Exp, scale=-0.5),
                 inc=I("sSB"))
            if kind == "ffn":
                E.op("vector", lambda e: e.scalar_tensor_tensor(
                    csc[:, 3:4], csc[:, 2:3], 1.0, csc[:, 2:3], ALU.mult, ALU.mult),
                    waits=[W("sSB")], inc=I("sST"))
                E.op("vector", lambda e: e.tensor_mul(csc[:, 4:5], csc[:, 3:4], mv[:, 1:2]),
                     waits=[W("sST")], inc=I("sST"))
                E.op("vector", lambda e: e.tensor_scalar_add(csc[:, 4:5], csc[:, 4:5], EPS),
                     waits=[W("sST")], inc=I("sST"))
                E.op("scalar", lambda e: e.activation(csc[:, 5:6], csc[:, 4:5], AF.Ln),
                     waits=[W("sST")])
                E.op("scalar", lambda e: e.activation(csc[:, 6:7], csc[:, 5:6], AF.Exp,
                                                      scale=-0.5), inc=I("sSB"))
                E.op("vector", lambda e: e.tensor_mul(csc[:, 7:8], csc[:, 2:3], csc[:, 6:7]),
                     waits=[W("sSB")], inc=I("sST"))
                E.op("vector", lambda e, t=t: e.tensor_copy(ln_m[:, t:t + 1], mv[:, 0:1]))
                E.op("vector", lambda e, t=t: e.tensor_copy(ln_r[:, t:t + 1], csc[:, 7:8]),
                     waits=[W("sST")], inc=I("sST"))
                rcol = csc[:, 7:8]
            else:
                rcol = csc[:, 2:3]
            if out_tile is not None:
                if t < 16:
                    E.op("vector", lambda e, t=t: e.tensor_copy(ln_m[:, t:t + 1],
                                                                mv[:, 0:1]), waits=[W("sSB")])
                    E.op("vector", lambda e, t=t, r=rcol: e.tensor_copy(ln_r[:, t:t + 1], r))
                E.op("vector", lambda e, t=t, o=out_tile, r=rcol: e.tensor_scalar(
                    o, xr(t), mv[:, 0:1], r, ALU.subtract, ALU.mult),
                    waits=[W("sSB"), W("sST")], inc=I("sLN"))

        tr_par = [0]

        def emit_transpose8(src, dest_fn, waits_t, npiece=8):
            pi = tr_par[0] % 2
            tr_par[0] += 1
            fr = tr_par[0] - 2

            def tr(e, s=src, pi=pi, n=npiece):
                inst = None
                for ic in range(n):
                    inst = e.transpose(TRP[pi][:, 128 * ic:128 * ic + 128],
                                       s[:, 128 * ic:128 * ic + 128], ID)
                return inst
            E.op("tensor", tr, waits=list(waits_t) + ([W("sFM", fr)] if fr > 0 else []),
                 inc=I("sTR"))

            def ev(e, pi=pi, n=npiece):
                inst = None
                for ic in range(n):
                    inst = e.tensor_copy(dest_fn(ic), TRP[pi][:, 128 * ic:128 * ic + 128])
                return inst
            E.op("vector", ev, waits=[W("sTR")], inc=I("sFM"))

        def emit_proj_tile(t, wsrc, wwaits, pb, bias_row=None, fm_wait=None):
            def mm(e, t=t, w=wsrc, pb=pb, br=bias_row):
                inst = None
                for oh in range(2):
                    for ic in range(8):
                        inst = e.matmul(bk(pb + oh), fm(ic)[:, 128 * t:128 * t + 128],
                                        w[:, 1024 * ic + 512 * oh:1024 * ic + 512 * oh + 512],
                                        start=(ic == 0), stop=(ic == 7 and br is None))
                    if br is not None:
                        inst = e.matmul(bk(pb + oh), ONR2, br[0:1, 512 * oh:512 * oh + 512],
                                        start=False, stop=True)
                return inst
            E.op("tensor", mm, waits=list(wwaits) + ([fm_wait] if fm_wait else []),
                 inc=I("sPRJ"))

        def rope_slices(ap2d):
            r = ap2d.rearrange("p (h half i) -> p h half i", h=16, half=2, i=32)
            return r[:, :, 0, :], r[:, :, 1, :]

        def emit_rope_evac(t, pb, dst):
            ps2 = PS[:, 512 * pb:512 * pb + 1024]
            xrr, xii = rope_slices(ps2)
            dr, di = rope_slices(dst)
            cb = ct_s[:, 32 * t:32 * t + 32].unsqueeze(1).broadcast_to([128, 16, 32])
            sb = st_s[:, 32 * t:32 * t + 32].unsqueeze(1).broadcast_to([128, 16, 32])
            t1 = t1r.rearrange("p (h i) -> p h i", h=16)
            t2 = t2r.rearrange("p (h i) -> p h i", h=16)
            E.op("vector", lambda e: e.tensor_tensor(t1, xrr, cb, ALU.mult),
                 waits=[W("sPRJ"), W("sCT", ct_done)])
            E.op("vector", lambda e: e.tensor_tensor(t2, xii, sb, ALU.mult))
            E.op("vector", lambda e: e.tensor_sub(dr, t1, t2))
            E.op("vector", lambda e: e.tensor_tensor(t1, xrr, sb, ALU.mult))
            E.op("vector", lambda e: e.tensor_tensor(t2, xii, cb, ALU.mult))
            E.op("vector", lambda e: e.tensor_add(di, t1, t2), inc=I("sRP"))

        for t in range(NT):
            emit_ln_tile(t, "plain", xnrot[t % 2],
                         [W("sx0", 64), W("sx1", 64), W("sx2", 64), W("sx3", 64),
                          W("sCT", ct_done)])
            if s1cut not in ('ln0', 'ln0b'):
                emit_transpose8(xnrot[t % 2],
                                (lambda tt: (lambda ic: fm(ic)[:, 128 * tt:128 * tt + 128]))(t),
                                [W("sLN")])
        xn_done = I("sFM").n
        dump("xn", FM[:, 0:1024], [W("sFM", xn_done)])
        dump("xraw", xr(0), [W("sLN", 1)])
        dump("stats", PO[:, 3072:3088], [W("sLN", 1)])

        if s1cut in ("ln", "ln0", "ln0b"):
            dump("lnm", ln_m, [W("sLN", I("sLN").n)])
            dump("lnr", ln_r, [W("sLN", I("sLN").n)])
            dump("sumsq", dbgS, [W("sLN", I("sLN").n)])
            dump("qsc", QSC, [W("sLN", I("sLN").n)])
            _finish(locals())
            return nc
        for t in range(NT):
            pb = 2 * (t % 2)
            if t >= 2:
                E.op("tensor", lambda e, v=I("sRP").n - 1: e.nop(), waits=[W("sRP", I("sRP").n - 1)])
            emit_proj_tile(t, WB, [W("sW1", 16)], pb, fm_wait=W("sFM", xn_done))
            emit_rope_evac(t, pb, qtm(t))
        dump("qtm", qtm(0), [W("sRP", 1)])
        E.op("gpsimd", lambda e: e.dma_start(out=WB[:, :], in_=wv_d[:, :]),
             waits=[W("sRP", NT)], inc=I("sW1"), dma=True)
        for t in range(NT):
            pb = 2 * (t % 2)
            E.op("tensor", lambda e: e.nop(), waits=[W("sRP", NT + t - 1)])
            emit_proj_tile(t, TWS, [W("sW2", 16)], pb, fm_wait=W("sFM", xn_done))
            if t >= 2:
                E.op("vector", lambda e, t=t: e.memset(csc[:, 8:9], 0.0),
                     waits=[W("sk%d" % (t % 2), 16 * (t // 2))])
            emit_rope_evac(t, pb, kt_rot[t % 2])
            E.op("sync", lambda e, t=t: e.dma_start(out=kb_i[128 * t:128 * t + 128, :],
                                                    in_=kt_rot[t % 2]),
                 waits=[W("sRP")], inc=I("sk%d" % (t % 2)), dma=True)
        for t in range(NT):
            pb = 2 * (t % 2)
            E.op("tensor", lambda e: e.nop(), waits=[W("sRP", 2 * NT + t - 1)])
            emit_proj_tile(t, WB, [W("sW1", 32)], pb, fm_wait=W("sFM", xn_done))

            def vev(e, t=t, pb=pb):
                ps2 = PS[:, 512 * pb:512 * pb + 1024]
                dst = kvrot[t % 2].rearrange("p (g c) -> p g c", g=16, c=65)
                e.tensor_copy(dst[:, :, 0:64], ps2.rearrange("p (g c) -> p g c", g=16, c=64))
                return e.memset(dst[:, :, 64:65], 1.0)
            vw = [W("sPRJ")]
            if t >= 2:
                vw.append(W("sv%d" % (t % 2), 16 * (t // 2)))
            E.op("vector", vev, waits=vw, inc=I("sRP"))
            E.op("sync", lambda e, t=t: e.dma_start(out=vb_i[128 * t:128 * t + 128, :],
                                                    in_=kvrot[t % 2]),
                 waits=[W("sRP")], inc=I("sv%d" % (t % 2)), dma=True)
        rp_all = I("sRP").n
        for t in range(NT):
            emit_transpose8(qtm(t),
                            (lambda tt: (lambda ic: fm(ic)[:, 128 * tt:128 * tt + 128]))(t),
                            [W("sRP", rp_all)])
        E.op("vector", lambda e: e.memset(csc[:, 8:9], 0.0), inc=I("sQT"))
        qt_done = I("sQT").n
        dump("qfm", fm(0), [W("sQT", qt_done)])

        if s1cut == "proj":
            _finish(locals())
            return nc
        E.op("gpsimd", lambda e: e.collective_compute(
            "AllGather", ALU.bypass, replica_groups=PAIRS,
            ins=[kb_i.ap().opt()], outs=[kb_o.ap().opt()]),
            waits=[W("sk0", I("sk0").n), W("sk1", I("sk1").n)], inc=I("sAG"))
        E.op("gpsimd", lambda e: e.collective_compute(
            "AllGather", ALU.bypass, replica_groups=PAIRS,
            ins=[vb_i.ap().opt()], outs=[vb_o.ap().opt()]),
            waits=[W("sv0", I("sv0").n), W("sv1", I("sv1").n)], inc=I("sAG"))

        if s1cut == "gather0":
            _finish(locals())
            return nc
        ngroups = 1 if s1cut == "g0" else NG
        for g in range(ngroups):
            gw = [W("sAG", 1), W("sQT", qt_done)]
            if g:
                gw.append(W("sPV", I("sPV").n))
            E.op("sync", lambda e, g=g: e.dma_start(
                out=GA[:, 0:8192].rearrange("p (c w) -> p c w", c=CHK),
                in_=kb_o[:, 256 * g:256 * g + 256].rearrange("(c p) w -> p c w", p=128)),
                waits=gw, inc=I("sGI"), dma=True)
            E.op("sync", lambda e, g=g: e.dma_start(
                out=GA[:, 8192:16512].rearrange("p (c w) -> p c w", c=CHK),
                in_=vb_o[:, 260 * g:260 * g + 260].rearrange("(c p) w -> p c w", p=128)),
                waits=[W("sAG", 2)], inc=I("sGI"), dma=True)
            gi_done = I("sGI").n
            aov_now = I("sAOv").n
            for j in range(CHK):
                emit_transpose8(katm(j),
                                (lambda jj: (lambda ic: kfm(ic)[:, 128 * jj:128 * jj + 128]))(j),
                                [W("sGI", gi_done), W("sRP", rp_all),
                                 W("sAOv", aov_now)], npiece=2)
            E.op("vector", lambda e: e.memset(csc[:, 8:9], 0.0), inc=I("sKF"))
            kf_done = I("sKF").n
            if g == 0:
                dump("kfm", kfm(0), [W("sKF", kf_done)])
                dump("vga", vga(0), [W("sGI", gi_done)])

            def sq_piece(src, out_run, first, wv=()):
                E.op("vector", lambda e, s=src: e.scalar_tensor_tensor(
                    sqscr, s, 1.0, s, ALU.mult, ALU.mult), waits=wv,
                    inc=I("sNC"))
                E.op("tensor", lambda e: e.matmul(PS[0:2, 3072:3584], ON2, sqscr,
                                                  start=True, stop=True),
                     waits=[W("sNC"), W("sRES", I("sRES").n), W("sCT", ct_done)], inc=I("sNC"))
                E.op("vector", lambda e: e.reduce_max(csc[0:2, 9:10], PS[0:2, 3072:3584],
                                                      axis=AX.X), waits=[W("sNC")],
                     inc=I("sST"))
                if first:
                    E.op("vector", lambda e: e.tensor_copy(csc[0:2, 10:11], csc[0:2, 9:10]),
                         waits=[W("sST")], inc=I("sNC"))
                else:
                    E.op("vector", lambda e: e.tensor_max(csc[0:2, 10:11], csc[0:2, 10:11],
                                                          csc[0:2, 9:10]),
                         waits=[W("sST")], inc=I("sNC"))

            first = True
            for ti in (2 * g, 2 * g + 1):
                for piece in range(4):
                    sq_piece(fm(ti)[:, 512 * piece:512 * piece + 512], None, first,
                             wv=[W("sQT", qt_done)])
                    first = False
            emit_preduce(csc[0:2, 10:11], csc[0:1, 12:13], ALU.max,
                         [W("sNC")], I("sNC"), cnt=2)
            first = True
            for ti in range(2):
                for piece in range(8):
                    sq_piece(kfm(ti)[:, 512 * piece:512 * piece + 512], None, first,
                             wv=[W("sKF", kf_done)])
                    first = False
            emit_preduce(csc[0:2, 10:11], csc[0:1, 11:12], ALU.max,
                         [W("sNC")], I("sNC"), cnt=2)
            E.op("vector", lambda e: e.tensor_mul(csc[0:1, 13:14], csc[0:1, 11:12],
                                                  csc[0:1, 12:13]), waits=[W("sNC")],
                 inc=I("sST"))
            E.op("scalar", lambda e: e.activation(csc[0:1, 14:15], csc[0:1, 13:14], AF.Ln),
                 waits=[W("sST")])
            E.op("scalar", lambda e: e.activation(csc[0:1, 13:14], csc[0:1, 14:15], AF.Exp,
                                                  scale=0.5), inc=I("sSB"))
            E.op("vector", lambda e: e.tensor_scalar(csc[0:1, 14:15], csc[0:1, 13:14],
                                                     0.12875, 1.0, ALU.mult, ALU.add),
                 waits=[W("sSB")], inc=I("sST"))
            E.op("vector", lambda e: e.tensor_scalar(csc[0:1, 14:15], csc[0:1, 14:15],
                                                     60.0, -1.0, ALU.min, ALU.mult),
                 waits=[W("sST")], inc=I("sNC"))
            emit_bcast(csc[0:1, 14:15], negC[:, g:g + 1], [W("sNC")], I("sNC"))
            nc_done = I("sNC").n

            trips = [(p, qb, c) for p in range(2) for qb in range(4)
                     for c in range(CHK)]

            def emit_qk(i):
                p, qb, c = trips[i]
                qkb = 2 * (i % 2)

                def mmqk(e, g=g, p=p, qb=qb, c=c, qkb=qkb):
                    kf = kfm(p)
                    qf = fm(2 * g + p)
                    e.ldweights(kf[0:64, 128 * c:128 * c + 128], tile_position=(0, 0))
                    e.ldweights(kf[64:128, 128 * c:128 * c + 128],
                                tile_position=(64, 0))
                    e.matmul(bk(qkb), kf[0:64, 128 * c:128 * c + 128],
                             qf[0:64, 512 * qb:512 * qb + 512],
                             start=True, stop=True, tile_position=(0, 0),
                             skip_group_check=True)
                    return e.matmul(bk(qkb + 1), kf[64:128, 128 * c:128 * c + 128],
                                    qf[64:128, 512 * qb:512 * qb + 512],
                                    start=True, stop=True, tile_position=(64, 0),
                                    skip_group_check=True)
                wq_ = [W("sKF", kf_done), W("sNC", nc_done)]
                if I("sEX").n >= 1:
                    wq_.append(W("sEX", I("sEX").n - 1))
                E.op("tensor", mmqk, waits=wq_, inc=I("sQK"))

            emit_qk(0)
            for i, (p, qb, c) in enumerate(trips):
                pi = i % 2
                qkb = 2 * pi
                pvb = 4 + 2 * (qb % 2)
                ew = [W("sQK"), W("sNC", nc_done)]
                if I("sPV").n >= 1:
                    ew.append(W("sPV", I("sPV").n - 1))
                E.op("scalar", lambda e, pi=pi, qkb=qkb, g=g: e.activation(
                    ES[pi][:, :], PS[:, 512 * qkb:512 * qkb + 1024], AF.Exp,
                    bias=negC[:, g:g + 1], scale=0.125), waits=ew, inc=I("sEX"))
                if i + 1 < len(trips):
                    emit_qk(i + 1)

                def mmpv(e, p=p, c=c, pi=pi, pvb=pvb):
                    inst = None
                    for h in range(2):
                        for qt in range(4):
                            lhs = ES[pi][:, 512 * h + 128 * qt:512 * h + 128 * qt + 128]
                            e.ldweights(lhs)
                            inst = e.matmul(
                                PS[:, 512 * (pvb + h) + 65 * qt:
                                   512 * (pvb + h) + 65 * qt + 65],
                                lhs, vga(c)[:, 65 * (2 * p + h):65 * (2 * p + h) + 65],
                                start=(c == 0 and qt == 0), stop=(c == CHK - 1),
                                skip_group_check=True)
                    return inst
                pw = [W("sEX"), W("sGI", gi_done), W("sAT", I("sAT").n)]
                if I("sAOv").n >= 2:
                    pw.append(W("sAOv", I("sAOv").n - 2))
                E.op("tensor", mmpv, waits=pw, inc=I("sPV"))

                if c == CHK - 1:
                    rq = rQ8

                    def pvrec(e, pvb=pvb, rq=rq):
                        inst = None
                        for h in range(2):
                            for qt in range(4):
                                acc = PS[:, 512 * (pvb + h) + 65 * qt:
                                         512 * (pvb + h) + 65 * qt + 65]
                                inst = e.reciprocal(rq[:, 4 * h + qt:4 * h + qt + 1],
                                                    acc[:, 64:65])
                        return inst
                    E.op("vector", pvrec, waits=[W("sPV")], inc=I("sAOv"))

                    def pvev(e, p=p, qb=qb, pvb=pvb, rq=rq):
                        inst = None
                        for h in range(2):
                            for qt in range(4):
                                acc = PS[:, 512 * (pvb + h) + 65 * qt:
                                         512 * (pvb + h) + 65 * qt + 65]
                                inst = e.tensor_scalar(
                                    aotm(p, 4 * qb + qt)[:, 64 * h:64 * h + 64],
                                    acc[:, 0:64], rq[:, 4 * h + qt:4 * h + qt + 1],
                                    None, ALU.mult)
                        return inst
                    E.op("vector", pvev, waits=[W("sAOv")], inc=I("sAOv"))
            ao_done = I("sAOv").n
            for p in range(2):
                for t in range(NT):
                    emit_transpose8(aotm(p, t),
                                    (lambda pp, tt: (lambda _:
                                     aofm(pp)[:, 128 * tt:128 * tt + 128]))(p, t),
                                    [W("sAOv", ao_done), W("sOP", I("sOP").n)], npiece=1)
            E.op("vector", lambda e: e.memset(csc[:, 8:9], 0.0), inc=I("sAT"))
            at_done = I("sAT").n
            if g == 0:
                dump("es", ES[1], [W("sEX", 2)])
                dump("ao", aofm(0), [W("sAT", at_done)])
            E.op("gpsimd", lambda e, g=g: e.dma_start(
                out=WB[:, 2048 * (g % 2):2048 * (g % 2) + 2048],
                in_=wo_d[:, 2048 * g:2048 * g + 2048]),
                waits=[W("sRP", rp_all), W("sOP", I("sOP").n)], inc=I("sWO"), dma=True)
            wo_done = I("sWO").n
            for t in range(NT):
                pb = 2 * (t % 2)

                def mmo(e, t=t, g=g, pb=pb):
                    inst = None
                    for oh in range(2):
                        for icc in range(2):
                            inst = e.matmul(bk(pb + oh), aofm(icc)[:, 128 * t:128 * t + 128],
                                            WB[:, 2048 * (g % 2) + 1024 * icc + 512 * oh:
                                               2048 * (g % 2) + 1024 * icc + 512 * oh + 512],
                                            start=(icc == 0), stop=(icc == 1))
                    return inst
                wts = [W("sAT", at_done), W("sWO", wo_done)]
                if t >= 2:
                    wts.append(W("sRES", I("sRES").n - 1))
                E.op("tensor", mmo, waits=wts, inc=I("sOP"))

                def resid(e, t=t, pb=pb):
                    inst = None
                    for oh in range(2):
                        inst = e.scalar_tensor_tensor(
                            xr(t)[:, 512 * oh:512 * oh + 512], bk(pb + oh), 1.0,
                            xr(t)[:, 512 * oh:512 * oh + 512], ALU.mult, ALU.add)
                    return inst
                E.op("vector", resid, waits=[W("sOP")], inc=I("sRES"))
        if s1cut == "g0":
            _finish(locals())
            return nc
        dump_x("x1")

        def emit_ternary():
            WF = WB.bitcast(F32)
            wfc = [WF[:, 1024 * i:1024 * i + 1024] for i in range(2)]
            absw = WF[:, 2048:3072]
            sgn = WF[:, 3072:4096]
            res1 = I("sRES").n
            sb_marks = []
            tw_marks = []
            for ic in range(8):
                wts = [W("sRES", res1), W("sOP", I("sOP").n)]
                if ic >= 2:
                    wts.append(W("sSB", sb_marks[ic - 2]))
                E.op("sync", lambda e, ic=ic: e.dma_start(
                    out=wfc[ic % 2], in_=wf_d[:, 1024 * ic:1024 * ic + 1024]),
                    waits=wts, inc=I("sWF"), dma=True)
                aw = [W("sWF")]
                if ic >= 1:
                    aw.append(W("sTW", tw_marks[ic - 1]))
                E.op("scalar", lambda e, ic=ic: e.activation(absw, wfc[ic % 2], AF.Abs),
                     waits=aw, inc=I("sSB"))
                sb_marks.append(I("sSB").n)
                E.op("vector", lambda e: e.reduce_sum(csc[:, 9:10], absw, axis=AX.X),
                     waits=[W("sSB")], inc=I("sST"))
                if ic == 0:
                    E.op("vector", lambda e: e.tensor_copy(csc[:, 10:11], csc[:, 9:10]),
                         waits=[W("sST")], inc=I("sTW"))
                else:
                    E.op("vector", lambda e: e.tensor_add(csc[:, 10:11], csc[:, 10:11],
                                                          csc[:, 9:10]),
                         waits=[W("sST")], inc=I("sTW"))
                tw_marks.append(I("sTW").n)
            emit_preduce(csc[:, 10:11], csc[0:1, 11:12], ALU.add, [W("sTW")], I("sTW"))
            E.op("vector", lambda e: e.tensor_scalar_mul(csc[0:1, 11:12], csc[0:1, 11:12],
                                                         0.7 / (1024.0 * 1024.0)),
                 waits=[W("sTW")], inc=I("sTW"))
            emit_bcast(csc[0:1, 11:12], csc[:, 16:17], [W("sTW")], I("sTW"))
            d_done = I("sTW").n
            sb_marks = []
            tw_marks = []
            for ic in range(8):
                wts = [W("sTW", d_done)]
                if ic >= 2:
                    wts.append(W("sTW", I("sTW").n - 1))
                E.op("sync", lambda e, ic=ic: e.dma_start(
                    out=wfc[ic % 2], in_=wf_d[:, 1024 * ic:1024 * ic + 1024]),
                    waits=wts, inc=I("sWF"), dma=True)
                aw = [W("sWF"), W("sTW", d_done if ic == 0 else tw_marks[ic - 1])]
                E.op("scalar", lambda e, ic=ic: e.activation(absw, wfc[ic % 2], AF.Abs),
                     waits=aw, inc=I("sSB"))
                E.op("scalar", lambda e, ic=ic: e.sign(sgn, wfc[ic % 2]), inc=I("sSB"))
                E.op("vector", lambda e: e.tensor_single_scalar(QSC, absw, csc[:, 16:17],
                                                                ALU.is_gt), waits=[W("sSB")])
                E.op("vector", lambda e, ic=ic: e.tensor_mul(
                    TWS[:, 1024 * ic:1024 * ic + 1024], sgn, QSC))
                E.op("vector", lambda e: e.tensor_mul(absw, absw, QSC))
                E.op("vector", lambda e: e.reduce_sum(csc[:, 9:10], absw, axis=AX.X),
                     inc=I("sST"))
                E.op("vector", lambda e: e.reduce_sum(csc[:, 14:15], QSC, axis=AX.X),
                     inc=I("sST"))
                if ic == 0:
                    E.op("vector", lambda e: e.tensor_copy(csc[:, 10:11], csc[:, 9:10]),
                         waits=[W("sST")])
                    E.op("vector", lambda e: e.tensor_copy(csc[:, 15:16], csc[:, 14:15]),
                         inc=I("sTW"))
                else:
                    E.op("vector", lambda e: e.tensor_add(csc[:, 10:11], csc[:, 10:11],
                                                          csc[:, 9:10]), waits=[W("sST")])
                    E.op("vector", lambda e: e.tensor_add(csc[:, 15:16], csc[:, 15:16],
                                                          csc[:, 14:15]), inc=I("sTW"))
                tw_marks.append(I("sTW").n)
            emit_preduce(csc[:, 10:11], csc[0:1, 9:10], ALU.add, [W("sTW")], I("sTW"))
            emit_preduce(csc[:, 15:16], csc[0:1, 14:15], ALU.add, [W("sTW")], I("sTW"))
            E.op("vector", lambda e: e.tensor_scalar_max(csc[0:1, 14:15], csc[0:1, 14:15],
                                                         1.0), waits=[W("sTW")], inc=I("sST"))
            E.op("vector", lambda e: e.reciprocal(csc[0:1, 15:16], csc[0:1, 14:15]),
                 waits=[W("sST")], inc=I("sST"))
            E.op("vector", lambda e: e.tensor_mul(csc[0:1, 17:18], csc[0:1, 9:10],
                                                  csc[0:1, 15:16]),
                 waits=[W("sST")], inc=I("sTW"))

        def emit_ffn(which):
            res_base = I("sRES").n
            E.op("sync", lambda e: e.dma_start(out=GREP[0:1, :], in_=gbr_d[0:1, :]),
                 waits=[W("sKB", I("sKB").n), W("sRES", res_base)], inc=I("sGB"), dma=True)
            E.op("sync", lambda e: e.dma_start(out=BREP[0:1, :], in_=gbr_d[1:2, :]),
                 inc=I("sGB"), dma=True)
            emit_bcast(GREP[0:1, :], GREP, [W("sGB")], I("sGB"), width=1024)
            emit_bcast(BREP[0:1, :], BREP, [W("sGB", I("sGB").n)], I("sGB"), width=1024)
            gb_done = I("sGB").n
            for t in range(NT):
                emit_ln_tile(t, "ffn", None, [W("sRES", res_base)])
                E.op("vector", lambda e, t=t: e.tensor_scalar(
                    QSC, xr(t), ln_m[:, t:t + 1], ln_r[:, t:t + 1], ALU.subtract, ALU.mult),
                    waits=[W("sST"), W("sGB", gb_done), W("sEX", I("sEX").n)])
                E.op("vector", lambda e: e.tensor_mul(QSC, QSC, GREP))
                E.op("vector", lambda e: e.tensor_add(QSC, QSC, BREP))
                E.op("vector", lambda e: e.reduce_max(csc[:, 9:10], QSC, axis=AX.X,
                                                      apply_absolute_value=True),
                     inc=I("sST"))
                if t == 0:
                    E.op("vector", lambda e: e.tensor_copy(smax, csc[:, 9:10]),
                         waits=[W("sST")], inc=I("sMX"))
                else:
                    E.op("vector", lambda e: e.tensor_max(smax, smax, csc[:, 9:10]),
                         waits=[W("sST")], inc=I("sMX"))
            emit_preduce(smax, mx_s[0:1, 0:1], ALU.max, [W("sMX", I("sMX").n)], I("sMX"))
            E.op("gpsimd", lambda e: e.dma_start(out=mx_i[0:1, 0:1], in_=mx_s[0:1, 0:1]),
                 waits=[W("sMX", I("sMX").n)], inc=I("sMXc"), dma=True)
            E.op("gpsimd", lambda e: e.collective_compute(
                "AllReduce", ALU.max, replica_groups=ALL8,
                ins=[mx_i.ap().opt()], outs=[mx_o.ap().opt()]),
                waits=[W("sMXc")], inc=I("sAG"))
            E.op("gpsimd", lambda e: e.dma_start(out=mx_s[0:1, 1:2], in_=mx_o[0:1, 0:1]),
                 waits=[W("sAG", I("sAG").n)], inc=I("sMXc"), dma=True)
            E.op("vector", lambda e: e.tensor_scalar_max(mx_s[0:1, 2:3], mx_s[0:1, 1:2], 1e-8),
                 waits=[W("sMXc", I("sMXc").n)], inc=I("sST"))
            E.op("vector", lambda e: e.reciprocal(mx_s[0:1, 3:4], mx_s[0:1, 2:3]),
                 waits=[W("sST")], inc=I("sST"))
            E.op("vector", lambda e: e.tensor_scalar_mul(mx_s[0:1, 4:5], mx_s[0:1, 3:4],
                                                         127.0), waits=[W("sST")],
                 inc=I("sMX"))
            emit_bcast(mx_s[0:1, 4:5], csc[:, 18:19], [W("sMX", I("sMX").n)], I("sMX"))
            E.op("vector", lambda e: e.tensor_scalar_mul(mx_s[0:1, 5:6], mx_s[0:1, 2:3],
                                                         1.0 / 127.0),
                 waits=[W("sMX", I("sMX").n), W("sTW", I("sTW").n)], inc=I("sST"))
            E.op("vector", lambda e: e.tensor_mul(mx_s[0:1, 6:7], mx_s[0:1, 5:6],
                                                  csc[0:1, 17:18]),
                 waits=[W("sTW", I("sTW").n), W("sST")], inc=I("sMX"))
            emit_bcast(mx_s[0:1, 6:7], csc[:, 19:20], [W("sMX", I("sMX").n)], I("sMX"))
            so_done = I("sMX").n
            if which == 0:
                dump("tw", TWS[:, 0:1024], [W("sTW", I("sTW").n)])
                dump("scal", mx_s[0:1, 0:8], [W("sMX", so_done)])
            for t in range(NT):
                E.op("vector", lambda e, t=t: e.tensor_scalar(
                    QSC, xr(t), ln_m[:, t:t + 1], ln_r[:, t:t + 1], ALU.subtract, ALU.mult),
                    waits=[W("sMX", so_done)])
                E.op("vector", lambda e: e.tensor_mul(QSC, QSC, GREP))
                E.op("vector", lambda e: e.tensor_add(QSC, QSC, BREP))
                E.op("vector", lambda e: e.tensor_scalar(QSC, QSC, csc[:, 18:19], None,
                                                         ALU.mult))
                E.op("vector", lambda e: e.tensor_scalar(QSC, QSC, -127.0, 127.0,
                                                         ALU.max, ALU.min))
                E.op("vector", lambda e, t=t: e.tensor_scalar(
                    xnrot[t % 2], QSC, MAGIC, MAGIC, ALU.add, ALU.subtract), inc=I("sLN"))
                emit_transpose8(xnrot[t % 2],
                                (lambda tt: (lambda ic: fm(ic)[:, 128 * tt:128 * tt + 128]))(t),
                                [W("sLN")])
            fm_done = I("sFM").n
            for t in range(NT):
                pb = 2 * (t % 2)

                def mmf(e, t=t, pb=pb):
                    inst = None
                    for oh in range(2):
                        for ic in range(8):
                            inst = e.matmul(bk(pb + oh), fm(ic)[:, 128 * t:128 * t + 128],
                                            TWS[:, 1024 * ic + 512 * oh:
                                                1024 * ic + 512 * oh + 512],
                                            start=(ic == 0), stop=(ic == 7))
                    return inst
                wts = [W("sFM", fm_done), W("sTW", I("sTW").n)]
                if t >= 2:
                    wts.append(W("sRES", I("sRES").n - 1))
                E.op("tensor", mmf, waits=wts, inc=I("sOP"))

                def residf(e, t=t, pb=pb):
                    inst = None
                    for oh in range(2):
                        inst = e.scalar_tensor_tensor(
                            xr(t)[:, 512 * oh:512 * oh + 512], bk(pb + oh), csc[:, 19:20],
                            xr(t)[:, 512 * oh:512 * oh + 512], ALU.mult, ALU.add)
                    return inst
                E.op("vector", residf, waits=[W("sOP"), W("sMX", so_done)], inc=I("sRES"))

        def emit_attn_b():
            res_base = I("sRES").n
            E.op("vector", lambda e: e.tensor_scalar(mkA, tri_s, pc_s[:, 1:2], pc_s[:, 0:1],
                                                     ALU.mult, ALU.add),
                 waits=[W("sCT", ct_done), W("sRES", res_base)])
            E.op("vector", lambda e: e.tensor_scalar(mkB, tri_s, pc_s[:, 0:1], None,
                                                     ALU.mult), inc=I("sKB"))
            mk_done = I("sKB").n
            for t in range(NT):
                emit_ln_tile(t, "plain", xnrot[t % 2], [W("sRES", res_base)])
                emit_transpose8(xnrot[t % 2],
                                (lambda tt: (lambda ic: fm(ic)[:, 128 * tt:128 * tt + 128]))(t),
                                [W("sLN")])
            fm_done = I("sFM").n
            E.op("gpsimd", lambda e: e.dma_start(out=WB[:, :], in_=wvb_d[:, :]),
                 waits=[W("sRES", res_base), W("sOP", I("sOP").n)], inc=I("sW1"), dma=True)
            wvb_done = I("sW1").n
            for t in range(NT):
                pb = 2 * (t % 2)
                if t >= 2:
                    E.op("tensor", lambda e: e.nop(), waits=[W("sRP", I("sRP").n - 1)])
                emit_proj_tile(t, WB, [W("sW1", wvb_done), W("sCT", ct_done),
                                       W("sBB", 32)], pb,
                               bias_row=BVB, fm_wait=W("sFM", fm_done))
                bw = [W("sPRJ")]
                if t >= 2:
                    bw.append(W("sv%d" % (t % 2), 16 * (I("sv%d" % (t % 2)).n // 16)))
                E.op("vector", lambda e, t=t, pb=pb, bw=bw: e.tensor_copy(
                    kt_rot[t % 2], PS[:, 512 * pb:512 * pb + 1024]),
                    waits=bw, inc=I("sRP"))
                E.op("sync", lambda e, t=t: e.dma_start(
                    out=vc_i[128 * t:128 * t + 128, :], in_=kt_rot[t % 2]),
                    waits=[W("sRP")], inc=I("sv%d" % (t % 2)), dma=True)
            vo_done = None
            W3S = AOS[:, 10115:10139]
            E.op("gpsimd", lambda e: e.dma_start(out=W3S, in_=w3_d[:, :]),
                 inc=I("sW2"), dma=True)

            def mmp(e):
                inst = None
                for piece in range(4):
                    for ic in range(8):
                        inst = e.matmul(PS[0:3, 2048 + 512 * piece:2048 + 512 * piece + 512],
                                        W3S[:, 3 * ic:3 * ic + 3],
                                        fm(ic)[:, 512 * piece:512 * piece + 512],
                                        start=(ic == 0), stop=(ic == 7))
                return inst
            E.op("tensor", mmp, waits=[W("sFM", fm_done), W("sW2", I("sW2").n),
                                       W("sTR", I("sTR").n)], inc=I("sPP"))
            E.op("vector", lambda e: e.tensor_scalar(POS[0:3, :], PS[0:3, 2048:4096],
                                                     pc_s[:, 2:3][0:3, :], None, ALU.add),
                 waits=[W("sPP"), W("sCT", ct_done)], inc=I("sPP"))
            for piece in range(4):
                E.op("vector", lambda e, p=piece: e.scalar_tensor_tensor(
                    t1r[0:3, :], POS[0:3, 512 * p:512 * p + 512], 1.0,
                    POS[0:3, 512 * p:512 * p + 512], ALU.mult, ALU.mult),
                    waits=[W("sPP", I("sPP").n)], inc=I("sPP"))

                def mmn(e, p=piece):
                    e.matmul(PS[0:4, 2048 + 512 * p:2048 + 512 * p + 512],
                             SEL34A[0:3, 0:4], POS[0:3, 512 * p:512 * p + 512],
                             start=True, stop=False)
                    return e.matmul(PS[0:4, 2048 + 512 * p:2048 + 512 * p + 512],
                                    SEL34B[0:3, 0:4], t1r[0:3, :],
                                    start=False, stop=True)
                E.op("tensor", mmn, waits=[W("sPP"), W("sCT", ct_done)], inc=I("sPP"))
                E.op("vector", lambda e, p=piece: e.tensor_copy(
                    POS[0:4, 512 * p:512 * p + 512],
                    PS[0:4, 2048 + 512 * p:2048 + 512 * p + 512]),
                    waits=[W("sPP")], inc=I("sPP"))
            pp_done = I("sPP").n
            dump("pos", POS[0:16, :], [W("sPP", pp_done)])
            E.op("gpsimd", lambda e: e.dma_start(out=pp_i[:, :], in_=POS[0:4, :]),
                 waits=[W("sPP", pp_done)], inc=I("sPG"), dma=True)
            E.op("gpsimd", lambda e: e.collective_compute(
                "AllGather", ALU.bypass, replica_groups=PAIRS,
                ins=[pp_i.ap().opt()], outs=[pp_o.ap().opt()]),
                waits=[W("sPG")], inc=I("sAG"))
            E.op("gpsimd", lambda e: e.collective_compute(
                "AllGather", ALU.bypass, replica_groups=PAIRS,
                ins=[vc_i.ap().opt()], outs=[vc_o.ap().opt()]),
                waits=[W("sv0", I("sv0").n), W("sv1", I("sv1").n)], inc=I("sAG"))
            ag_done = I("sAG").n
            E.op("sync", lambda e: e.dma_start(
                out=GA[:, 0:16384].rearrange("p (c w) -> p c w", c=16),
                in_=vc_o[0:2048, :].rearrange("(c p) w -> p c w", p=128)),
                waits=[W("sAG", ag_done), W("sPV", I("sPV").n)], inc=I("sGI"), dma=True)
            E.op("sync", lambda e: e.dma_start(
                out=WB[:, :].rearrange("p (c w) -> p c w", c=8),
                in_=vc_o[2048:3072, :].rearrange("(c p) w -> p c w", p=128)),
                waits=[W("sAG", ag_done), W("sPRJ", I("sPRJ").n)], inc=I("sGI"), dma=True)
            E.op("sync", lambda e: e.dma_start(
                out=AOS[:, 0:8192].rearrange("p (c w) -> p c w", c=8),
                in_=vc_o[3072:4096, :].rearrange("(c p) w -> p c w", p=128)),
                waits=[W("sAG", ag_done), W("sAT", I("sAT").n), W("sOP", I("sOP").n)],
                inc=I("sGI"), dma=True)
            gi_done = I("sGI").n
            E.op("vector", lambda e: e.memset(khb[0:5, :], 1.0),
                 waits=[W("sPP", pp_done)], inc=I("sKB"))
            mm_marks = {}
            evac_marks = {}
            src_marks = {}
            for piece in range(12):
                isq = piece >= 8
                if not isq:
                    half, pp = piece // 4, piece % 4

                    def ldp(e, half=half, pp=pp):
                        return e.dma_start(out=POSG[0:4, :],
                                           in_=pp_o[4 * half:4 * half + 4,
                                                    512 * pp:512 * pp + 512])
                    kw = [W("sAG", ag_done), W("sv0", I("sv0").n), W("sv1", I("sv1").n)]
                    if piece:
                        kw.append(W("sKB", src_marks[piece - 1]))
                    E.op("sync", ldp, waits=kw, inc=I("sGI"), dma=True)
                    dst = KB13[:, 2048 * half + 512 * pp:2048 * half + 512 * pp + 512]
                    srcf = POSG[0:4, :]
                    cw = [W("sGI", I("sGI").n)]
                    sela, selb = SELKA[0:5, :], SELKB[0:4, :]
                else:
                    pp = piece - 8
                    dst = QB13[:, 512 * pp:512 * pp + 512]
                    srcf = POS[0:4, 512 * pp:512 * pp + 512]
                    cw = [W("sPP", pp_done)]
                    sela, selb = SELQA[0:5, :], SELQB[0:4, :]
                if piece:
                    cw.append(W("sPP", mm_marks[piece - 1]))
                E.op("vector", lambda e, s=srcf: e.tensor_copy(khb[0:4, :], s),
                     waits=cw)
                E.op("vector", lambda e, s=srcf: e.tensor_sub(klb[0:4, :], s,
                                                              khb[0:4, :]),
                     inc=I("sKB"))
                src_marks[piece] = I("sKB").n
                psb = PS[0:13, 3072 + 512 * (piece % 2):3584 + 512 * (piece % 2)]

                def mmaug(e, psb=psb, sela=sela, selb=selb):
                    e.matmul(psb, sela, khb[0:5, :], start=True, stop=False)
                    return e.matmul(psb, selb, klb[0:4, :], start=False, stop=True)
                mw = [W("sKB", src_marks[piece]), W("sCT", ct_done)]
                if piece >= 2:
                    mw.append(W("sKB", evac_marks[piece - 2]))
                E.op("tensor", mmaug, waits=mw, inc=I("sPP"))
                mm_marks[piece] = I("sPP").n
                E.op("vector", lambda e, d=dst, psb=psb: e.tensor_copy(d[0:13, :], psb),
                     waits=[W("sPP", mm_marks[piece])], inc=I("sKB"))
                evac_marks[piece] = I("sKB").n
            kb_done = I("sKB").n
            dump("kb13", KB13[0:16, :], [W("sKB", kb_done)])
            batches = []
            for t in range(NT):
                J = list(range(0, t + 1)) + list(range(16, 16 + t + 1))
                for bi in range(0, len(J), 4):
                    batches.append((t, bi, J[bi:bi + 4], len(J)))
            rl_marks = {}
            eb_marks = {}
            pv_marks = {}
            rs_marks = {}
            ob_marks = {}

            def emit_mmd(k):
                t, bi, js, nj = batches[k]
                pi = k % 2

                def mmd(e, js=js, pi=pi, t=t):
                    inst = None
                    for idx, j in enumerate(js):
                        inst = e.matmul(
                            PS[:, 512 * pi + 128 * idx:512 * pi + 128 * idx + 128],
                            KB13[0:13, 128 * j:128 * j + 128],
                            QB13[0:13, 128 * t:128 * t + 128],
                            start=(idx == 0), stop=(idx == len(js) - 1),
                            skip_group_check=True)
                    return inst
                wd = [W("sKB", kb_done), W("sRES", res_base)]
                if k >= 2:
                    wd.append(W("sRL", rl_marks[k - 2]))
                E.op("tensor", mmd, waits=wd, inc=I("sDM"))

            emit_mmd(0)
            for k, (t, bi, js, nj) in enumerate(batches):
                pi = k % 2
                s = 512 * (k % 4)
                n4 = len(js)
                fb, lb = bi == 0, bi + 4 >= nj
                rw = [W("sDM", I("sDM").n)]
                if k >= 4:
                    rw.append(W("sPV", pv_marks[k - 4]))
                E.op("vector", lambda e, s=s, pi=pi, n4=n4: e.tensor_scalar(
                    PSTG[:, s:s + 128 * n4],
                    PS[:, 512 * pi:512 * pi + 128 * n4], 1e-20, None, ALU.max),
                    waits=rw, inc=I("sRL"))
                rl_marks[k] = I("sRL").n
                E.op("scalar", lambda e, s=s, n4=n4: e.activation(
                    PSTG[:, s:s + 128 * n4], PSTG[:, s:s + 128 * n4], AF.Ln),
                    waits=[W("sRL", rl_marks[k])])
                E.op("scalar", lambda e, s=s, n4=n4: e.activation(
                    PSTG[:, s:s + 128 * n4], PSTG[:, s:s + 128 * n4], AF.Exp,
                    scale=0.5))
                E.op("scalar", lambda e, s=s, n4=n4: e.activation(
                    PSTG[:, s:s + 128 * n4], PSTG[:, s:s + 128 * n4], AF.Exp,
                    scale=-0.125), inc=I("sEB"))
                eb_marks[k] = I("sEB").n
                if k + 1 < len(batches):
                    emit_mmd(k + 1)
                mw = [W("sEB", eb_marks[k]), W("sKB", mk_done)]
                for li, j in enumerate(js):
                    if j == t:
                        E.op("vector", lambda e, s=s, li=li: e.tensor_mul(
                            PSTG[:, s + 128 * li:s + 128 * li + 128],
                            PSTG[:, s + 128 * li:s + 128 * li + 128], mkA),
                            waits=mw, inc=I("sPB"))
                    if bi + li == nj - 1:
                        E.op("vector", lambda e, s=s, li=li: e.tensor_mul(
                            PSTG[:, s + 128 * li:s + 128 * li + 128],
                            PSTG[:, s + 128 * li:s + 128 * li + 128], mkB),
                            waits=mw, inc=I("sPB"))
                pb_mark = I("sPB").n

                def mms(e, t=t, s=s, n4=n4, fb=fb, lb=lb):
                    inst = None
                    for idx in range(n4):
                        e.ldweights(PSTG[:, s + 128 * idx:s + 128 * idx + 128])
                        inst = e.matmul(PS[:, 1024 + t:1025 + t],
                                        PSTG[:, s + 128 * idx:s + 128 * idx + 128],
                                        ONC, start=(fb and idx == 0),
                                        stop=(lb and idx == n4 - 1),
                                        skip_group_check=True)
                    return inst
                sw = [W("sPB", pb_mark), W("sEB", eb_marks[k]), W("sCT", ct_done)]
                if fb and t >= 1:
                    sw.append(W("sRS", rs_marks[t - 1]))
                E.op("tensor", mms, waits=sw, inc=I("sDM"))

                def mmpv2(e, s=s, js=js, fb=fb, lb=lb):
                    inst = None
                    for ds in range(8):
                        for idx, j in enumerate(js):
                            e.ldweights(vbt(j)[:, 128 * ds:128 * ds + 128])
                            inst = e.matmul(
                                PS[:, 2048 + 128 * ds:2048 + 128 * ds + 128],
                                vbt(j)[:, 128 * ds:128 * ds + 128],
                                PSTG[:, s + 128 * idx:s + 128 * idx + 128],
                                start=((ds == 0 or ds == 4) and fb and idx == 0),
                                stop=(lb and idx == len(js) - 1),
                                skip_group_check=True)
                    return inst
                pw = [W("sPB", pb_mark), W("sEB", eb_marks[k]), W("sGI", gi_done)]
                if fb and t >= 1:
                    pw.append(W("sOB", ob_marks[t - 1]))
                E.op("tensor", mmpv2, waits=pw, inc=I("sPV"))
                pv_marks[k] = I("sPV").n
                if lb:
                    E.op("vector", lambda e, t=t: e.reciprocal(
                        rsumb[:, t:t + 1], PS[:, 1024 + t:1025 + t]),
                        waits=[W("sDM", I("sDM").n)], inc=I("sRS"))
                    rs_marks[t] = I("sRS").n

                    def evb(e, t=t):
                        inst = None
                        for ds in range(8):
                            inst = e.tensor_copy(
                                fm(ds)[:, 128 * t:128 * t + 128],
                                PS[:, 2048 + 128 * ds:2048 + 128 * ds + 128])
                        return inst
                    E.op("vector", evb, waits=[W("sPV", pv_marks[k])], inc=I("sOB"))
                    ob_marks[t] = I("sOB").n
            ob_done = I("sOB").n
            dump("aofb", fm(0), [W("sOB", ob_done)])
            BOREP = GREP
            E.op("sync", lambda e: e.dma_start(out=GREP[0:1, :], in_=bb_d[1:2, :]),
                 waits=[W("sPV", I("sPV").n), W("sOB", I("sOB").n)], inc=I("sGB"), dma=True)
            emit_bcast(GREP[0:1, :], GREP, [W("sGB", I("sGB").n)], I("sGB"), width=1024)
            borep_done = I("sGB").n
            E.op("gpsimd", lambda e: e.dma_start(out=WB[:, :], in_=wob_d[:, :]),
                 waits=[W("sPV", I("sPV").n)], inc=I("sW1"), dma=True)
            wob_done = I("sW1").n
            for t in range(NT):
                pb = 2 * (t % 2)

                def mmob(e, t=t, pb=pb):
                    inst = None
                    for oh in range(2):
                        for ic in range(8):
                            inst = e.matmul(bk(pb + oh), fm(ic)[:, 128 * t:128 * t + 128],
                                            WB[:, 1024 * ic + 512 * oh:
                                               1024 * ic + 512 * oh + 512],
                                            start=(ic == 0), stop=(ic == 7))
                    return inst
                wts = [W("sOB", ob_done), W("sW1", wob_done)]
                if t >= 2:
                    wts.append(W("sRES", I("sRES").n - 1))
                E.op("tensor", mmob, waits=wts, inc=I("sOP"))

                def residb(e, t=t, pb=pb):
                    for oh in range(2):
                        e.scalar_tensor_tensor(
                            xr(t)[:, 512 * oh:512 * oh + 512], bk(pb + oh),
                            rsumb[:, t:t + 1], xr(t)[:, 512 * oh:512 * oh + 512],
                            ALU.mult, ALU.add)
                    return e.tensor_add(xr(t), xr(t), BOREP)
                E.op("vector", residb, waits=[W("sOP"), W("sRS", I("sRS").n),
                                              W("sGB", borep_done)], inc=I("sRES"))

        if stages >= 2:
            emit_ternary()
            emit_ffn(0)
            dump_x("x2")
        if stages >= 3:
            emit_attn_b()
            dump_x("x3")
        if stages >= 4:
            emit_ffn(1)

        for t in range(NT):
            E.op("sync", lambda e, t=t: e.dma_start(out=out_d[128 * t:128 * t + 128, :],
                                                    in_=xr(t)),
                 waits=[W("sRES", I("sRES").n)], inc=I("sX"), dma=True)
        E.op("sync", lambda e: e.nop(), waits=[W("sX", I("sX").n), W("sDP", I("sDP").n)])

        E.emit(nc)
    return nc


def _rope_perm():
    p = []
    for h in range(NH):
        p += [64 * h + 2 * i for i in range(32)]
        p += [64 * h + 2 * i + 1 for i in range(32)]
    return np.array(p, dtype=np.int64)


def _wmat(w, perm=None):
    wt = np.asarray(w).T
    if perm is not None:
        wt = wt[:, perm]
    return np.ascontiguousarray(
        wt.reshape(8, 128, 1024).transpose(1, 0, 2).reshape(128, 8192)).astype(np.float32)


def _stage_inputs(inputs, core):
    b, p = core // 2, core % 2
    x = np.asarray(inputs["L0_attn_a_x"])
    xs = np.ascontiguousarray(x.reshape(4, 32, 128, D)[b, p::2].reshape(TOK, D))
    perm = _rope_perm()
    tpos = ((2 * np.arange(16)[:, None] + p) * 128 + np.arange(128)[None, :]).reshape(-1)
    inv = 1.0 / 10000.0 ** (np.arange(0, HD, 2, dtype=np.float64) / HD)
    ang = tpos[:, None] * inv[None, :]
    ct = np.ascontiguousarray(np.cos(ang).astype(np.float32).reshape(16, 128, 32)
                              .transpose(1, 0, 2).reshape(128, 512))
    st = np.ascontiguousarray(np.sin(ang).astype(np.float32).reshape(16, 128, 32)
                              .transpose(1, 0, 2).reshape(128, 512))
    gbr = np.ascontiguousarray(np.stack([np.asarray(inputs["L0_ffn_gamma"]),
                                         np.asarray(inputs["L0_ffn_beta"])])).astype(np.float32)
    bb = np.ascontiguousarray(np.stack([np.asarray(inputs["L0_attn_b_bv"]),
                                        np.asarray(inputs["L0_attn_b_bo"])])).astype(np.float32)
    pcm = np.zeros((128, 8), np.float32)
    pcm[:, 0] = float(p)
    pcm[:, 1] = 1.0 - float(p)
    pcm[0:3, 2] = np.asarray(inputs["L0_attn_b_b3d"])
    w3 = np.asarray(inputs["L0_attn_b_w3d"]).T
    w3s = np.ascontiguousarray(w3.reshape(8, 128, 3).transpose(1, 0, 2).reshape(128, 24)
                               ).astype(np.float32)
    return {
        "x": xs.astype(np.float32),
        "wq": _wmat(inputs["L0_attn_a_wq"], perm),
        "wk": _wmat(inputs["L0_attn_a_wk"], perm),
        "wv": _wmat(inputs["L0_attn_a_wv"]),
        "wo": _wmat(inputs["L0_attn_a_wo"]),
        "wf": _wmat(inputs["L0_ffn_weight"]),
        "wvb": _wmat(inputs["L0_attn_b_wv"]),
        "wob": _wmat(inputs["L0_attn_b_wo"]),
        "w3": w3s, "gbr": gbr, "bb": bb, "pc": pcm, "ct": ct, "st": st,
    }


_NC_CACHE = {}


def kernel(**inputs):
    key = ("k", STAGES)
    if key not in _NC_CACHE:
        _NC_CACHE[key] = build_program(stages=STAGES)
    nc = _NC_CACHE[key]
    in_maps = [_stage_inputs(inputs, c) for c in range(8)]
    res = run_bass_kernel_spmd(nc, in_maps, core_ids=list(range(8)))
    out = np.zeros((4, 4096, D), np.float32)
    ov = out.reshape(4, 32, 128, D)
    for c in range(8):
        b, p = c // 2, c % 2
        ov[b, p::2] = res.results[c]["out"].reshape(16, 128, D)
    return out


# version 05 (final)
# speedup vs baseline: 1.3543x; 1.0071x over previous
import numpy as np
import ml_dtypes
import concourse.bass as bass
import concourse.mybir as mybir
from concourse.bass_utils import run_bass_kernel_spmd
from concourse import library_config
from concourse.bass_isa import ReduceOp

F32 = mybir.dt.float32
BF16 = mybir.dt.bfloat16
AF = mybir.ActivationFunctionType
ALU = mybir.AluOpType
AX = mybir.AxisListType
bf16 = ml_dtypes.bfloat16

TOK, NT, NH, HD, D = 2048, 16, 16, 64, 1024
NG, CHK = 4, 32
EPS = 1e-5
MAGIC = float(np.float32(1.5 * 2 ** 23))
PAIRS = [[0, 1], [2, 3], [4, 5], [6, 7]]
ALL8 = [list(range(8))]
STAGES = 4


class Sem:
    __slots__ = ("name", "h", "n")

    def __init__(self, name):
        self.name, self.h, self.n = name, None, 0


class Emit:
    ENGINES = ("sync", "scalar", "tensor", "vector", "gpsimd")

    def __init__(self):
        self.items = {e: [] for e in self.ENGINES}
        self.done = {e: {} for e in self.ENGINES}

    def op(self, eng, fn, waits=(), inc=None, dma=False):
        amt = 16 if dma else 1
        d = self.done[eng]
        ws = []
        for s, v in waits:
            if v > 0 and d.get(s, -1) < v:
                ws.append((s, v))
                d[s] = v
        if inc is not None:
            inc.n += amt
        self.items[eng].append((fn, tuple(ws), inc, amt))

    def emit(self, nc):
        with nc.Block() as block:
            for e in self.ENGINES:
                if not self.items[e]:
                    continue

                def mk(items):
                    def body(eng):
                        for fn, ws, inc, amt in items:
                            for s, v in ws:
                                eng.wait_ge(s.h, v)
                            inst = fn(eng)
                            if inc is not None:
                                inst.then_inc(inc.h, amt)
                    return body
                getattr(block, e)(mk(self.items[e]))


def build_program(dumps=(), stages=STAGES, s1cut=None):
    nc = bass.Bass("TRN2", target_bir_lowering=False)
    E = Emit()
    dumps = set(dumps)

    x_d = nc.dram_tensor("x", [TOK, D], F32, kind="ExternalInput")
    out_d = nc.dram_tensor("out", [TOK, D], F32, kind="ExternalOutput")
    wq_d = nc.dram_tensor("wq", [128, 8192], F32, kind="ExternalInput")
    wk_d = nc.dram_tensor("wk", [128, 8192], F32, kind="ExternalInput")
    wv_d = nc.dram_tensor("wv", [128, 8192], F32, kind="ExternalInput")
    wo_d = nc.dram_tensor("wo", [128, 8192], F32, kind="ExternalInput")
    wf_d = nc.dram_tensor("wf", [128, 8192], F32, kind="ExternalInput")
    wvb_d = nc.dram_tensor("wvb", [128, 8192], F32, kind="ExternalInput")
    wob_d = nc.dram_tensor("wob", [128, 8192], F32, kind="ExternalInput")
    w3_d = nc.dram_tensor("w3", [128, 24], F32, kind="ExternalInput")
    gbr_d = nc.dram_tensor("gbr", [2, D], F32, kind="ExternalInput")
    bb_d = nc.dram_tensor("bb", [2, D], F32, kind="ExternalInput")
    pc_d = nc.dram_tensor("pc", [128, 8], F32, kind="ExternalInput")
    ct_d = nc.dram_tensor("ct", [128, 512], F32, kind="ExternalInput")
    st_d = nc.dram_tensor("st", [128, 512], F32, kind="ExternalInput")

    kb_i = nc.dram_tensor("kb_i", [TOK, D], BF16)
    kb_o = nc.dram_tensor("kb_o", [2 * TOK, D], BF16)
    vb_i = nc.dram_tensor("vb_i", [TOK, NH * 65], BF16)
    vb_o = nc.dram_tensor("vb_o", [2 * TOK, NH * 65], BF16)
    vc_i = nc.dram_tensor("vc_i", [TOK, D], BF16)
    vc_o = nc.dram_tensor("vc_o", [2 * TOK, D], BF16)
    pp_i = nc.dram_tensor("pp_i", [4, TOK], F32)
    pp_o = nc.dram_tensor("pp_o", [8, TOK], F32)
    mx_i = nc.dram_tensor("mx_i", [1, 8], F32)
    mx_o = nc.dram_tensor("mx_o", [1, 8], F32)

    ident_d = nc.inline_tensor(np.eye(128, dtype=bf16), name="identb")
    ones2_np = np.zeros((128, 2), dtype=bf16)
    ones2_np[0:64, 0] = 1
    ones2_np[64:128, 1] = 1
    ones2_d = nc.inline_tensor(ones2_np, name="ones2")
    onescol_d = nc.inline_tensor(np.ones((128, 1), dtype=bf16), name="onescol")
    onesrow_d = nc.inline_tensor(np.ones((1, 128), dtype=bf16), name="onesrow")
    identf_d = nc.inline_tensor(np.eye(128, dtype=np.float32), name="identf")
    onesrowf_d = nc.inline_tensor(np.ones((1, 128), dtype=np.float32), name="onesrowf")
    tri_np = np.ascontiguousarray(
        np.tril(np.ones((128, 128), dtype=np.float32)).T.astype(bf16))
    tri_d = nc.inline_tensor(tri_np, name="trikq")
    ska = np.zeros((5, 13), dtype=bf16)
    for i in range(3):
        ska[i, i] = 1; ska[i, 6 + i] = 1
    ska[3, 11] = 1; ska[4, 9] = 1; ska[4, 10] = 1
    skb = np.zeros((4, 13), dtype=bf16)
    for i in range(3):
        skb[i, 3 + i] = 1
    skb[3, 12] = 1
    sqa = np.zeros((5, 13), dtype=bf16)
    for i in range(3):
        sqa[i, i] = -2; sqa[i, 3 + i] = -2
    sqa[3, 9] = 1; sqa[4, 11] = 1; sqa[4, 12] = 1
    sqb = np.zeros((4, 13), dtype=bf16)
    for i in range(3):
        sqb[i, 6 + i] = -2
    sqb[3, 10] = 1
    selka_d = nc.inline_tensor(ska, name="selka")
    selkb_d = nc.inline_tensor(skb, name="selkb")
    selqa_d = nc.inline_tensor(sqa, name="selqa")
    selqb_d = nc.inline_tensor(sqb, name="selqb")
    s34a = np.zeros((3, 4), dtype=np.float32)
    for i in range(3):
        s34a[i, i] = 1
    s34b = np.zeros((3, 4), dtype=np.float32)
    s34b[:, 3] = 1
    sel34a_d = nc.inline_tensor(s34a, name="sel34a")
    sel34b_d = nc.inline_tensor(s34b, name="sel34b")

    dump_t = {}
    for tag, shape, dt in [
        ("xn", (128, D), BF16), ("qtm", (128, 1024), BF16), ("qfm", (128, 2048), BF16),
        ("kfm", (128, 4096), BF16), ("es", (128, 1024), BF16), ("ao", (128, 2048), BF16),
        ("x1", (TOK, D), F32), ("x2", (TOK, D), F32), ("x3", (TOK, D), F32),
        ("tw", (128, 1024), BF16), ("pos", (16, TOK), F32), ("kb13", (16, 4096), BF16),
        ("esb", (128, 4352), BF16), ("scal", (1, 8), F32), ("aofb", (128, 2048), BF16),
        ("vga", (128, 260), BF16), ("xraw", (128, 1024), F32),
        ("stats", (128, 16), F32), ("lnm", (128, 16), F32), ("lnr", (128, 16), F32),
        ("sumsq", (128, 16), F32), ("qsc", (128, 1024), F32),
    ]:
        if tag in dumps:
            dump_t[tag] = nc.dram_tensor("d_" + tag, list(shape), dt, kind="ExternalOutput")

    names = ("sX sCT sW1 sWO sW2 sLN sTR sFM sPRJ sRP sKO sVO sAG sGI sKF sNC sQK "
             "sEX sPV sAOv sAT sAF sOP sRES sDP sMX sMXc sTW sWF sPP sPG sKB sDM "
             "sRL sRS sEB sPB sOB sQT sST sSB sGB "
             "sx0 sx1 sx2 sx3 sk0 sk1 sv0 sv1 sBB").split()
    SM = {n: Sem(n) for n in names}

    def W(sem, v=None):
        return (SM[sem], SM[sem].n if v is None else v)

    def I(sem):
        return SM[sem]

    with (
        nc.sbuf_tensor("XR", [128, NT * 1024], F32) as XR,
        nc.sbuf_tensor("FM", [128, 16384], BF16) as FM,
        nc.sbuf_tensor("GA", [128, 16640], BF16) as GA,
        nc.sbuf_tensor("TWS", [128, 8192], BF16) as TWS,
        nc.sbuf_tensor("WB", [128, 8192], BF16) as WB,
        nc.sbuf_tensor("AOS", [128, 10240], BF16) as AOS,
        nc.sbuf_tensor("PO", [128, 6912], F32) as PO,
        nc.psum_tensor("PS", [128, 4096], F32) as PS,
    ):
        for s in SM.values():
            s.h = nc.alloc_semaphore(s.name)

        def xr(t):
            return XR[:, 1024 * t:1024 * t + 1024]

        def fm(ic):
            return FM[:, 2048 * ic:2048 * ic + 2048]

        def qtm(t):
            return GA[:, 1024 * t:1024 * t + 1024]

        def katm(j):
            return GA[:, 256 * j:256 * j + 256]

        def vga(j):
            return GA[:, 8192 + 260 * j:8192 + 260 * j + 260]

        def kfm(i):
            return TWS[:, 4096 * i:4096 * i + 4096]

        def vbt(j):
            if j < 16:
                return GA[:, 1024 * j:1024 * j + 1024]
            if j < 24:
                return WB[:, 1024 * (j - 16):1024 * (j - 16) + 1024]
            return AOS[:, 1024 * (j - 24):1024 * (j - 24) + 1024]

        def bk(i, w=512):
            return PS[:, 512 * i:512 * i + w]

        ct_s = PO[:, 0:512]
        st_s = PO[:, 512:1024]
        xnrot = [PO[:, 1024 + 512 * i:1536 + 512 * i].bitcast(BF16) for i in range(2)]
        ES = [PO[:, 2048 + 512 * i:2560 + 512 * i].bitcast(BF16) for i in range(2)]
        QSC = PO[:, 2048:3072]
        st6 = PO[:, 3072:3084]
        mv = PO[:, 3084:3086]
        csc = PO[:, 3088:3112]
        dbgS = PO[:, 6296:6312]
        ln_m = PO[:, 3112:3128]
        ln_r = PO[:, 3128:3144]
        smax = PO[:, 3144:3145]
        mx_s = PO[:, 3152:3160]
        rQ8 = PO[:, 3160:3168]
        negC = PO[:, 3168:3172]
        t1r = PO[:, 3200:3712]
        t2r = PO[:, 3712:4224]
        sqscr = PO[:, 3200:3456].bitcast(BF16)
        GREP = PO[:, 4224:5248]
        BREP = PO[:, 5248:6272]
        kvrot = [PO[:, 4224 + 520 * i:4744 + 520 * i].bitcast(BF16) for i in range(2)]
        kt_rot = [PO[:, 5264 + 512 * i:5776 + 512 * i].bitcast(BF16) for i in range(2)]
        pc_s = PO[:, 6288:6296]
        rsumb = PO[:, 6352:6368]
        IDF = PO[:, 6656:6784]
        ONRF = PO[0:1, 6784:6912]
        mkA = PO[:, 6400:6464].bitcast(BF16)
        mkB = PO[:, 6464:6528].bitcast(BF16)
        tri_s = PO[:, 6528:6592].bitcast(BF16)
        POS = PO[:, 0:2048]
        POSG = PO[:, 2048:2560]
        khb = PO[:, 2560:2816].bitcast(BF16)
        klb = PO[:, 2816:3072].bitcast(BF16)
        SEL34A = PO[:, 6312:6316]
        SEL34B = PO[:, 6316:6320]
        SELKA = AOS[:, 10144:10157]
        SELKB = AOS[:, 10160:10173]
        SELQA = AOS[:, 10176:10189]
        SELQB = AOS[:, 10208:10221]
        PSTG = PO[:, 0:2176].bitcast(BF16)
        QB13 = PO[:, 3200:4224].bitcast(BF16)
        KB13 = PO[:, 4224:6272].bitcast(BF16)
        ONES3F = PO[:, 3184:3185]

        ID = AOS[:, 9984:10112]
        ONR2 = AOS[0:1, 8704:8832]
        BVB = AOS[0:1, 8832:9856]
        BOB = AOS[1:2, 8832:9856]

        def aotm(p, t):
            return AOS[:, 2048 * p + 128 * t:2048 * p + 128 * t + 128]

        def aofm(i):
            return AOS[:, 4096 + 2048 * i:4096 + 2048 * i + 2048]

        TRP = [PS[:, 2048 + 512 * i:2560 + 512 * i].bitcast(BF16) for i in range(2)]

        def _finish(_ctx):
            for t in range(NT):
                E.op("sync", lambda e, t=t: e.dma_start(
                    out=out_d[128 * t:128 * t + 128, :], in_=xr(t)),
                    waits=[W("sRES", I("sRES").n), W("sLN", I("sLN").n),
                           W("sFM", I("sFM").n), W("sRP", I("sRP").n),
                           W("sQT", I("sQT").n)],
                    inc=I("sX"), dma=True)
            E.op("sync", lambda e: e.nop(), waits=[W("sX", I("sX").n), W("sDP", I("sDP").n),
                                                   W("sk0", I("sk0").n), W("sk1", I("sk1").n),
                                                   W("sv0", I("sv0").n), W("sv1", I("sv1").n),
                                                   W("sAG", I("sAG").n)])
            E.emit(nc)

        def _preload_tables(e):
            return e.add_instruction(mybir.InstLoadActFuncSet(
                name=nc.get_next_instruction_name(), act_func_set_id=6, ins=[], outs=[]))
        E.op("scalar", _preload_tables)
        for dst, src in [(ct_s, ct_d), (st_s, st_d), (ID, ident_d), (tri_s, tri_d),
                         (pc_s, pc_d)]:
            E.op("sync", lambda e, d=dst, s=src: e.dma_start(out=d, in_=s[:, :]),
                 inc=I("sCT"), dma=True)
        ct_early = I("sCT").n
        for t in range(NT):
            sx = "sx%d" % (t % 4)
            E.op("sync", lambda e, t=t: e.dma_start(out=xr(t), in_=x_d[128 * t:128 * t + 128, :]),
                 waits=([W(sx, 16 * (t // 4))] if t >= 4 else []), inc=I(sx), dma=True)
        ON2 = AOS[:, 10112:10114]
        ONC = AOS[:, 10114:10115]
        E.op("sync", lambda e: e.dma_start(out=ON2, in_=ones2_d[:, :]), inc=I("sCT"), dma=True)
        E.op("sync", lambda e: e.dma_start(out=ONC, in_=onescol_d[:, :]), inc=I("sCT"), dma=True)
        E.op("sync", lambda e: e.dma_start(out=ONR2, in_=onesrow_d[:, :]), inc=I("sCT"), dma=True)
        E.op("sync", lambda e: e.dma_start(out=IDF, in_=identf_d[:, :]), inc=I("sCT"), dma=True)
        E.op("sync", lambda e: e.dma_start(out=ONRF, in_=onesrowf_d[:, :]), inc=I("sCT"), dma=True)
        for dst, src in [(SELKA[0:5, :], selka_d), (SELKB[0:4, :], selkb_d),
                         (SELQA[0:5, :], selqa_d), (SELQB[0:4, :], selqb_d),
                         (SEL34A[0:3, :], sel34a_d), (SEL34B[0:3, :], sel34b_d)]:
            E.op("sync", lambda e, d=dst, s=src: e.dma_start(out=d, in_=s[:, :]),
                 inc=I("sCT"), dma=True)
        E.op("gpsimd", lambda e: e.dma_start(out=BVB, in_=bb_d[0:1, :]), inc=I("sBB"), dma=True)
        E.op("gpsimd", lambda e: e.dma_start(out=BOB, in_=bb_d[1:2, :]), inc=I("sBB"), dma=True)
        E.op("gpsimd", lambda e: e.dma_start(out=WB[:, :], in_=wq_d[:, :]), inc=I("sW1"), dma=True)
        E.op("gpsimd", lambda e: e.dma_start(out=TWS[:, :], in_=wk_d[:, :]), inc=I("sW2"), dma=True)
        ct_done = I("sCT").n

        def dump(tag, ap, waits):
            if tag in dump_t:
                E.op("sync", lambda e, t=dump_t[tag], a=ap: e.dma_start(out=t.ap(), in_=a),
                     waits=waits, inc=I("sDP"), dma=True)

        def dump_x(tag):
            if tag in dump_t:
                for t in range(NT):
                    E.op("sync", lambda e, t=t: e.dma_start(
                        out=dump_t[tag].ap()[128 * t:128 * t + 128, :], in_=xr(t)),
                        waits=[W("sRES")], inc=I("sDP"), dma=True)

        def emit_preduce(src_col, dst11, op, waits, inc_sem, cnt=128):
            E.op("tensor", lambda e: e.transpose(PS[0:1, 3584:3584 + cnt],
                                                 src_col, IDF[0:cnt, 0:cnt]),
                 waits=list(waits) + [W("sCT", ct_done)], inc=I("sNC"))
            E.op("vector", lambda e: e.tensor_reduce(dst11, PS[0:1, 3584:3584 + cnt],
                                                     AX.X, op), waits=[W("sNC")],
                 inc=inc_sem)

        def emit_bcast(src11, dst_col, waits, inc_sem, width=1):
            base = 3584 if width <= 512 else 3072
            def mm(e):
                inst = None
                for o in range(0, width, 512):
                    wdt = min(512, width - o)
                    inst = e.matmul(PS[:, base + o:base + o + wdt], ONRF,
                                    src11[0:1, o:o + wdt], start=True, stop=True)
                return inst
            E.op("tensor", mm, waits=list(waits) + [W("sCT", ct_done)], inc=I("sNC"))
            E.op("vector", lambda e: e.tensor_copy(dst_col, PS[:, base:base + width]),
                 waits=[W("sNC")], inc=inc_sem)

        def emit_ln_tile(t, kind, out_tile, in_waits):
            E.op("vector", lambda e, t=t: e.scalar_tensor_tensor(
                QSC, xr(t), 1.0, xr(t), ALU.mult, ALU.mult), waits=in_waits,
                inc=I("sST"))
            E.op("vector", lambda e, t=t: e.reduce_sum(csc[:, 9:10], xr(t), axis=AX.X),
                 inc=I("sST"))
            E.op("vector", lambda e: e.reduce_sum(csc[:, 10:11], QSC, axis=AX.X),
                 waits=[W("sST")], inc=I("sST"))
            if "sumsq" in dump_t and t < 16:
                E.op("vector", lambda e, t=t: e.tensor_copy(
                    dbgS[:, t:t + 1], csc[:, 10:11]), waits=[W("sST")])
            E.op("vector", lambda e: e.tensor_scalar_mul(mv[:, 0:1], csc[:, 9:10],
                                                         1.0 / 1024.0), waits=[W("sST")])
            E.op("vector", lambda e: e.tensor_scalar_mul(csc[:, 11:12], csc[:, 10:11],
                                                         1.0 / 1024.0), inc=I("sST"))
            E.op("vector", lambda e: e.scalar_tensor_tensor(
                csc[:, 12:13], mv[:, 0:1], 1.0, mv[:, 0:1], ALU.mult, ALU.mult),
                waits=[W("sST")], inc=I("sST"))
            E.op("vector", lambda e: e.tensor_sub(mv[:, 1:2], csc[:, 11:12], csc[:, 12:13]),
                 waits=[W("sST")], inc=I("sST"))
            E.op("vector", lambda e: e.tensor_scalar_add(csc[:, 0:1], mv[:, 1:2], EPS),
                 waits=[W("sST")], inc=I("sST"))
            E.op("scalar", lambda e: e.activation(csc[:, 1:2], csc[:, 0:1], AF.Ln),
                 waits=[W("sST")])
            E.op("scalar", lambda e: e.activation(csc[:, 2:3], csc[:, 1:2], AF.Exp, scale=-0.5),
                 inc=I("sSB"))
            if kind == "ffn":
                E.op("vector", lambda e: e.scalar_tensor_tensor(
                    csc[:, 3:4], csc[:, 2:3], 1.0, csc[:, 2:3], ALU.mult, ALU.mult),
                    waits=[W("sSB")], inc=I("sST"))
                E.op("vector", lambda e: e.tensor_mul(csc[:, 4:5], csc[:, 3:4], mv[:, 1:2]),
                     waits=[W("sST")], inc=I("sST"))
                E.op("vector", lambda e: e.tensor_scalar_add(csc[:, 4:5], csc[:, 4:5], EPS),
                     waits=[W("sST")], inc=I("sST"))
                E.op("scalar", lambda e: e.activation(csc[:, 5:6], csc[:, 4:5], AF.Ln),
                     waits=[W("sST")])
                E.op("scalar", lambda e: e.activation(csc[:, 6:7], csc[:, 5:6], AF.Exp,
                                                      scale=-0.5), inc=I("sSB"))
                E.op("vector", lambda e: e.tensor_mul(csc[:, 7:8], csc[:, 2:3], csc[:, 6:7]),
                     waits=[W("sSB")], inc=I("sST"))
                E.op("vector", lambda e, t=t: e.tensor_copy(ln_m[:, t:t + 1], mv[:, 0:1]))
                E.op("vector", lambda e, t=t: e.tensor_copy(ln_r[:, t:t + 1], csc[:, 7:8]),
                     waits=[W("sST")], inc=I("sST"))
                rcol = csc[:, 7:8]
            else:
                rcol = csc[:, 2:3]
            if out_tile is not None:
                if t < 16:
                    E.op("vector", lambda e, t=t: e.tensor_copy(ln_m[:, t:t + 1],
                                                                mv[:, 0:1]), waits=[W("sSB")])
                    E.op("vector", lambda e, t=t, r=rcol: e.tensor_copy(ln_r[:, t:t + 1], r))
                E.op("vector", lambda e, t=t, o=out_tile, r=rcol: e.tensor_scalar(
                    o, xr(t), mv[:, 0:1], r, ALU.subtract, ALU.mult),
                    waits=[W("sSB"), W("sST")], inc=I("sLN"))

        tr_par = [0]

        def emit_transpose8(src, dest_fn, waits_t, npiece=8):
            pi = tr_par[0] % 2
            tr_par[0] += 1
            fr = tr_par[0] - 2

            def tr(e, s=src, pi=pi, n=npiece):
                inst = None
                for ic in range(n):
                    inst = e.transpose(TRP[pi][:, 128 * ic:128 * ic + 128],
                                       s[:, 128 * ic:128 * ic + 128], ID)
                return inst
            E.op("tensor", tr, waits=list(waits_t) + ([W("sFM", fr)] if fr > 0 else []),
                 inc=I("sTR"))

            def ev(e, pi=pi, n=npiece):
                inst = None
                for ic in range(n):
                    inst = e.tensor_copy(dest_fn(ic), TRP[pi][:, 128 * ic:128 * ic + 128])
                return inst
            E.op("vector", ev, waits=[W("sTR")], inc=I("sFM"))

        def emit_proj_tile(t, wsrc, wwaits, pb, bias_row=None, fm_wait=None):
            def mm(e, t=t, w=wsrc, pb=pb, br=bias_row):
                inst = None
                for oh in range(2):
                    for ic in range(8):
                        inst = e.matmul(bk(pb + oh), fm(ic)[:, 128 * t:128 * t + 128],
                                        w[:, 1024 * ic + 512 * oh:1024 * ic + 512 * oh + 512],
                                        start=(ic == 0), stop=(ic == 7 and br is None))
                    if br is not None:
                        inst = e.matmul(bk(pb + oh), ONR2, br[0:1, 512 * oh:512 * oh + 512],
                                        start=False, stop=True)
                return inst
            E.op("tensor", mm, waits=list(wwaits) + ([fm_wait] if fm_wait else []),
                 inc=I("sPRJ"))

        def rope_slices(ap2d):
            r = ap2d.rearrange("p (h half i) -> p h half i", h=16, half=2, i=32)
            return r[:, :, 0, :], r[:, :, 1, :]

        def emit_rope_evac(t, pb, dst):
            ps2 = PS[:, 512 * pb:512 * pb + 1024]
            xrr, xii = rope_slices(ps2)
            dr, di = rope_slices(dst)
            cb = ct_s[:, 32 * t:32 * t + 32].unsqueeze(1).broadcast_to([128, 16, 32])
            sb = st_s[:, 32 * t:32 * t + 32].unsqueeze(1).broadcast_to([128, 16, 32])
            t1 = t1r.rearrange("p (h i) -> p h i", h=16)
            t2 = t2r.rearrange("p (h i) -> p h i", h=16)
            E.op("vector", lambda e: e.tensor_tensor(t1, xrr, cb, ALU.mult),
                 waits=[W("sPRJ"), W("sCT", ct_done)])
            E.op("vector", lambda e: e.tensor_tensor(t2, xii, sb, ALU.mult))
            E.op("vector", lambda e: e.tensor_sub(dr, t1, t2))
            E.op("vector", lambda e: e.tensor_tensor(t1, xrr, sb, ALU.mult))
            E.op("vector", lambda e: e.tensor_tensor(t2, xii, cb, ALU.mult))
            E.op("vector", lambda e: e.tensor_add(di, t1, t2), inc=I("sRP"))

        for t in range(NT):
            emit_ln_tile(t, "plain", xnrot[t % 2],
                         [W("sx%d" % (t % 4), 16 * (t // 4 + 1)),
                          W("sCT", ct_early)])
            if s1cut not in ('ln0', 'ln0b'):
                emit_transpose8(xnrot[t % 2],
                                (lambda tt: (lambda ic: fm(ic)[:, 128 * tt:128 * tt + 128]))(t),
                                [W("sLN"), W("sCT", ct_early)])
        xn_done = I("sFM").n
        dump("xn", FM[:, 0:1024], [W("sFM", xn_done)])
        dump("xraw", xr(0), [W("sLN", 1)])
        dump("stats", PO[:, 3072:3088], [W("sLN", 1)])

        if s1cut in ("ln", "ln0", "ln0b"):
            dump("lnm", ln_m, [W("sLN", I("sLN").n)])
            dump("lnr", ln_r, [W("sLN", I("sLN").n)])
            dump("sumsq", dbgS, [W("sLN", I("sLN").n)])
            dump("qsc", QSC, [W("sLN", I("sLN").n)])
            _finish(locals())
            return nc
        for t in range(NT):
            pb = 2 * (t % 2)
            if t >= 2:
                E.op("tensor", lambda e, v=I("sRP").n - 1: e.nop(), waits=[W("sRP", I("sRP").n - 1)])
            emit_proj_tile(t, WB, [W("sW1", 16)], pb, fm_wait=W("sFM", xn_done))
            emit_rope_evac(t, pb, qtm(t))
        dump("qtm", qtm(0), [W("sRP", 1)])
        E.op("gpsimd", lambda e: e.dma_start(out=WB[:, :], in_=wv_d[:, :]),
             waits=[W("sRP", NT)], inc=I("sW1"), dma=True)
        for t in range(NT):
            pb = 2 * (t % 2)
            E.op("tensor", lambda e: e.nop(), waits=[W("sRP", NT + t - 1)])
            emit_proj_tile(t, TWS, [W("sW2", 16)], pb, fm_wait=W("sFM", xn_done))
            if t >= 2:
                E.op("vector", lambda e, t=t: e.memset(csc[:, 8:9], 0.0),
                     waits=[W("sk%d" % (t % 2), 16 * (t // 2))])
            emit_rope_evac(t, pb, kt_rot[t % 2])
            E.op("sync", lambda e, t=t: e.dma_start(out=kb_i[128 * t:128 * t + 128, :],
                                                    in_=kt_rot[t % 2]),
                 waits=[W("sRP")], inc=I("sk%d" % (t % 2)), dma=True)
        for t in range(NT):
            pb = 2 * (t % 2)
            E.op("tensor", lambda e: e.nop(), waits=[W("sRP", 2 * NT + t - 1)])
            emit_proj_tile(t, WB, [W("sW1", 32)], pb, fm_wait=W("sFM", xn_done))

            def vev(e, t=t, pb=pb):
                ps2 = PS[:, 512 * pb:512 * pb + 1024]
                dst = kvrot[t % 2].rearrange("p (g c) -> p g c", g=16, c=65)
                e.tensor_copy(dst[:, :, 0:64], ps2.rearrange("p (g c) -> p g c", g=16, c=64))
                return e.memset(dst[:, :, 64:65], 1.0)
            vw = [W("sPRJ")]
            if t >= 2:
                vw.append(W("sv%d" % (t % 2), 16 * (t // 2)))
            E.op("vector", vev, waits=vw, inc=I("sRP"))
            E.op("sync", lambda e, t=t: e.dma_start(out=vb_i[128 * t:128 * t + 128, :],
                                                    in_=kvrot[t % 2]),
                 waits=[W("sRP")], inc=I("sv%d" % (t % 2)), dma=True)
        rp_all = I("sRP").n
        for t in range(NT):
            emit_transpose8(qtm(t),
                            (lambda tt: (lambda ic: fm(ic)[:, 128 * tt:128 * tt + 128]))(t),
                            [W("sRP", rp_all)])
        E.op("vector", lambda e: e.memset(csc[:, 8:9], 0.0), inc=I("sQT"))
        qt_done = I("sQT").n
        dump("qfm", fm(0), [W("sQT", qt_done)])

        if s1cut == "proj":
            _finish(locals())
            return nc
        E.op("gpsimd", lambda e: e.collective_compute(
            "AllGather", ALU.bypass, replica_groups=PAIRS,
            ins=[kb_i.ap().opt()], outs=[kb_o.ap().opt()]),
            waits=[W("sk0", I("sk0").n), W("sk1", I("sk1").n)], inc=I("sAG"))
        E.op("gpsimd", lambda e: e.collective_compute(
            "AllGather", ALU.bypass, replica_groups=PAIRS,
            ins=[vb_i.ap().opt()], outs=[vb_o.ap().opt()]),
            waits=[W("sv0", I("sv0").n), W("sv1", I("sv1").n)], inc=I("sAG"))

        if s1cut == "gather0":
            _finish(locals())
            return nc
        ngroups = 1 if s1cut == "g0" else NG
        for g in range(ngroups):
            gw = [W("sAG", 1), W("sQT", qt_done)]
            if g:
                gw.append(W("sPV", I("sPV").n))
            E.op("sync", lambda e, g=g: e.dma_start(
                out=GA[:, 0:8192].rearrange("p (c w) -> p c w", c=CHK),
                in_=kb_o[:, 256 * g:256 * g + 256].rearrange("(c p) w -> p c w", p=128)),
                waits=gw, inc=I("sGI"), dma=True)
            E.op("sync", lambda e, g=g: e.dma_start(
                out=GA[:, 8192:16512].rearrange("p (c w) -> p c w", c=CHK),
                in_=vb_o[:, 260 * g:260 * g + 260].rearrange("(c p) w -> p c w", p=128)),
                waits=[W("sAG", 2)], inc=I("sGI"), dma=True)
            gi_done = I("sGI").n
            aov_now = I("sAOv").n
            for j in range(CHK):
                emit_transpose8(katm(j),
                                (lambda jj: (lambda ic: kfm(ic)[:, 128 * jj:128 * jj + 128]))(j),
                                [W("sGI", gi_done), W("sRP", rp_all),
                                 W("sAOv", aov_now)], npiece=2)
            E.op("vector", lambda e: e.memset(csc[:, 8:9], 0.0), inc=I("sKF"))
            kf_done = I("sKF").n
            if g == 0:
                dump("kfm", kfm(0), [W("sKF", kf_done)])
                dump("vga", vga(0), [W("sGI", gi_done)])

            def sq_piece(src, out_run, first, wv=()):
                E.op("vector", lambda e, s=src: e.scalar_tensor_tensor(
                    sqscr, s, 1.0, s, ALU.mult, ALU.mult), waits=wv,
                    inc=I("sNC"))
                E.op("tensor", lambda e: e.matmul(PS[0:2, 3072:3584], ON2, sqscr,
                                                  start=True, stop=True),
                     waits=[W("sNC"), W("sRES", I("sRES").n), W("sCT", ct_done)], inc=I("sNC"))
                E.op("vector", lambda e: e.reduce_max(csc[0:2, 9:10], PS[0:2, 3072:3584],
                                                      axis=AX.X), waits=[W("sNC")],
                     inc=I("sST"))
                if first:
                    E.op("vector", lambda e: e.tensor_copy(csc[0:2, 10:11], csc[0:2, 9:10]),
                         waits=[W("sST")], inc=I("sNC"))
                else:
                    E.op("vector", lambda e: e.tensor_max(csc[0:2, 10:11], csc[0:2, 10:11],
                                                          csc[0:2, 9:10]),
                         waits=[W("sST")], inc=I("sNC"))

            first = True
            for ti in (2 * g, 2 * g + 1):
                for piece in range(4):
                    sq_piece(fm(ti)[:, 512 * piece:512 * piece + 512], None, first,
                             wv=[W("sQT", qt_done)])
                    first = False
            emit_preduce(csc[0:2, 10:11], csc[0:1, 12:13], ALU.max,
                         [W("sNC")], I("sNC"), cnt=2)
            first = True
            for ti in range(2):
                for piece in range(8):
                    sq_piece(kfm(ti)[:, 512 * piece:512 * piece + 512], None, first,
                             wv=[W("sKF", kf_done)])
                    first = False
            emit_preduce(csc[0:2, 10:11], csc[0:1, 11:12], ALU.max,
                         [W("sNC")], I("sNC"), cnt=2)
            E.op("vector", lambda e: e.tensor_mul(csc[0:1, 13:14], csc[0:1, 11:12],
                                                  csc[0:1, 12:13]), waits=[W("sNC")],
                 inc=I("sST"))
            E.op("scalar", lambda e: e.activation(csc[0:1, 14:15], csc[0:1, 13:14], AF.Ln),
                 waits=[W("sST")])
            E.op("scalar", lambda e: e.activation(csc[0:1, 13:14], csc[0:1, 14:15], AF.Exp,
                                                  scale=0.5), inc=I("sSB"))
            E.op("vector", lambda e: e.tensor_scalar(csc[0:1, 14:15], csc[0:1, 13:14],
                                                     0.12875, 1.0, ALU.mult, ALU.add),
                 waits=[W("sSB")], inc=I("sST"))
            E.op("vector", lambda e: e.tensor_scalar(csc[0:1, 14:15], csc[0:1, 14:15],
                                                     60.0, -1.0, ALU.min, ALU.mult),
                 waits=[W("sST")], inc=I("sNC"))
            emit_bcast(csc[0:1, 14:15], negC[:, g:g + 1], [W("sNC")], I("sNC"))
            nc_done = I("sNC").n

            trips = [(p, qb, c) for p in range(2) for qb in range(4)
                     for c in range(CHK)]

            def emit_qk(i):
                p, qb, c = trips[i]
                qkb = 2 * (i % 2)

                def mmqk(e, g=g, p=p, qb=qb, c=c, qkb=qkb):
                    kf = kfm(p)
                    qf = fm(2 * g + p)
                    e.ldweights(kf[0:64, 128 * c:128 * c + 128], tile_position=(0, 0))
                    e.ldweights(kf[64:128, 128 * c:128 * c + 128],
                                tile_position=(64, 0))
                    e.matmul(bk(qkb), kf[0:64, 128 * c:128 * c + 128],
                             qf[0:64, 512 * qb:512 * qb + 512],
                             start=True, stop=True, tile_position=(0, 0),
                             skip_group_check=True)
                    return e.matmul(bk(qkb + 1), kf[64:128, 128 * c:128 * c + 128],
                                    qf[64:128, 512 * qb:512 * qb + 512],
                                    start=True, stop=True, tile_position=(64, 0),
                                    skip_group_check=True)
                wq_ = [W("sKF", kf_done), W("sNC", nc_done)]
                if I("sEX").n >= 1:
                    wq_.append(W("sEX", I("sEX").n - 1))
                E.op("tensor", mmqk, waits=wq_, inc=I("sQK"))

            emit_qk(0)
            for i, (p, qb, c) in enumerate(trips):
                pi = i % 2
                qkb = 2 * pi
                pvb = 4 + 2 * (qb % 2)
                ew = [W("sQK"), W("sNC", nc_done)]
                if I("sPV").n >= 1:
                    ew.append(W("sPV", I("sPV").n - 1))
                E.op("scalar", lambda e, pi=pi, qkb=qkb, g=g: e.activation(
                    ES[pi][:, :], PS[:, 512 * qkb:512 * qkb + 1024], AF.Exp,
                    bias=negC[:, g:g + 1], scale=0.125), waits=ew, inc=I("sEX"))
                if i + 1 < len(trips):
                    emit_qk(i + 1)

                def mmpv(e, p=p, c=c, pi=pi, pvb=pvb):
                    inst = None
                    for h in range(2):
                        for qt in range(4):
                            lhs = ES[pi][:, 512 * h + 128 * qt:512 * h + 128 * qt + 128]
                            e.ldweights(lhs)
                            inst = e.matmul(
                                PS[:, 512 * (pvb + h) + 65 * qt:
                                   512 * (pvb + h) + 65 * qt + 65],
                                lhs, vga(c)[:, 65 * (2 * p + h):65 * (2 * p + h) + 65],
                                start=(c == 0 and qt == 0), stop=(c == CHK - 1),
                                skip_group_check=True)
                    return inst
                pw = [W("sEX"), W("sGI", gi_done), W("sAT", I("sAT").n)]
                if I("sAOv").n >= 2:
                    pw.append(W("sAOv", I("sAOv").n - 2))
                E.op("tensor", mmpv, waits=pw, inc=I("sPV"))

                if c == CHK - 1:
                    rq = rQ8

                    def pvrec(e, pvb=pvb, rq=rq):
                        inst = None
                        for h in range(2):
                            for qt in range(4):
                                acc = PS[:, 512 * (pvb + h) + 65 * qt:
                                         512 * (pvb + h) + 65 * qt + 65]
                                inst = e.reciprocal(rq[:, 4 * h + qt:4 * h + qt + 1],
                                                    acc[:, 64:65])
                        return inst
                    E.op("vector", pvrec, waits=[W("sPV")], inc=I("sAOv"))

                    def pvev(e, p=p, qb=qb, pvb=pvb, rq=rq):
                        inst = None
                        for h in range(2):
                            for qt in range(4):
                                acc = PS[:, 512 * (pvb + h) + 65 * qt:
                                         512 * (pvb + h) + 65 * qt + 65]
                                inst = e.tensor_scalar(
                                    aotm(p, 4 * qb + qt)[:, 64 * h:64 * h + 64],
                                    acc[:, 0:64], rq[:, 4 * h + qt:4 * h + qt + 1],
                                    None, ALU.mult)
                        return inst
                    E.op("vector", pvev, waits=[W("sAOv")], inc=I("sAOv"))
            ao_done = I("sAOv").n
            for p in range(2):
                for t in range(NT):
                    emit_transpose8(aotm(p, t),
                                    (lambda pp, tt: (lambda _:
                                     aofm(pp)[:, 128 * tt:128 * tt + 128]))(p, t),
                                    [W("sAOv", ao_done), W("sOP", I("sOP").n)], npiece=1)
            E.op("vector", lambda e: e.memset(csc[:, 8:9], 0.0), inc=I("sAT"))
            at_done = I("sAT").n
            if g == 0:
                dump("es", ES[1], [W("sEX", 2)])
                dump("ao", aofm(0), [W("sAT", at_done)])
            E.op("gpsimd", lambda e, g=g: e.dma_start(
                out=WB[:, 2048 * (g % 2):2048 * (g % 2) + 2048],
                in_=wo_d[:, 2048 * g:2048 * g + 2048]),
                waits=[W("sRP", rp_all), W("sOP", I("sOP").n)], inc=I("sWO"), dma=True)
            wo_done = I("sWO").n
            for t in range(NT):
                pb = 2 * (t % 2)

                def mmo(e, t=t, g=g, pb=pb):
                    inst = None
                    for oh in range(2):
                        for icc in range(2):
                            inst = e.matmul(bk(pb + oh), aofm(icc)[:, 128 * t:128 * t + 128],
                                            WB[:, 2048 * (g % 2) + 1024 * icc + 512 * oh:
                                               2048 * (g % 2) + 1024 * icc + 512 * oh + 512],
                                            start=(icc == 0), stop=(icc == 1))
                    return inst
                wts = [W("sAT", at_done), W("sWO", wo_done)]
                if t >= 2:
                    wts.append(W("sRES", I("sRES").n - 1))
                E.op("tensor", mmo, waits=wts, inc=I("sOP"))

                def resid(e, t=t, pb=pb):
                    inst = None
                    for oh in range(2):
                        inst = e.scalar_tensor_tensor(
                            xr(t)[:, 512 * oh:512 * oh + 512], bk(pb + oh), 1.0,
                            xr(t)[:, 512 * oh:512 * oh + 512], ALU.mult, ALU.add)
                    return inst
                E.op("vector", resid, waits=[W("sOP")], inc=I("sRES"))
        if s1cut == "g0":
            _finish(locals())
            return nc
        dump_x("x1")

        def emit_ternary():
            WF = WB.bitcast(F32)
            wfc = [WF[:, 1024 * i:1024 * i + 1024] for i in range(2)]
            absw = WF[:, 2048:3072]
            sgn = WF[:, 3072:4096]
            res1 = I("sRES").n
            sb_marks = []
            tw_marks = []
            for ic in range(8):
                wts = [W("sRES", res1), W("sOP", I("sOP").n)]
                if ic >= 2:
                    wts.append(W("sSB", sb_marks[ic - 2]))
                E.op("sync", lambda e, ic=ic: e.dma_start(
                    out=wfc[ic % 2], in_=wf_d[:, 1024 * ic:1024 * ic + 1024]),
                    waits=wts, inc=I("sWF"), dma=True)
                aw = [W("sWF")]
                if ic >= 1:
                    aw.append(W("sTW", tw_marks[ic - 1]))
                E.op("scalar", lambda e, ic=ic: e.activation(absw, wfc[ic % 2], AF.Abs),
                     waits=aw, inc=I("sSB"))
                sb_marks.append(I("sSB").n)
                E.op("vector", lambda e: e.reduce_sum(csc[:, 9:10], absw, axis=AX.X),
                     waits=[W("sSB")], inc=I("sST"))
                if ic == 0:
                    E.op("vector", lambda e: e.tensor_copy(csc[:, 10:11], csc[:, 9:10]),
                         waits=[W("sST")], inc=I("sTW"))
                else:
                    E.op("vector", lambda e: e.tensor_add(csc[:, 10:11], csc[:, 10:11],
                                                          csc[:, 9:10]),
                         waits=[W("sST")], inc=I("sTW"))
                tw_marks.append(I("sTW").n)
            emit_preduce(csc[:, 10:11], csc[0:1, 11:12], ALU.add, [W("sTW")], I("sTW"))
            E.op("vector", lambda e: e.tensor_scalar_mul(csc[0:1, 11:12], csc[0:1, 11:12],
                                                         0.7 / (1024.0 * 1024.0)),
                 waits=[W("sTW")], inc=I("sTW"))
            emit_bcast(csc[0:1, 11:12], csc[:, 16:17], [W("sTW")], I("sTW"))
            d_done = I("sTW").n
            sb_marks = []
            tw_marks = []
            for ic in range(8):
                wts = [W("sTW", d_done)]
                if ic >= 2:
                    wts.append(W("sTW", I("sTW").n - 1))
                E.op("sync", lambda e, ic=ic: e.dma_start(
                    out=wfc[ic % 2], in_=wf_d[:, 1024 * ic:1024 * ic + 1024]),
                    waits=wts, inc=I("sWF"), dma=True)
                aw = [W("sWF"), W("sTW", d_done if ic == 0 else tw_marks[ic - 1])]
                E.op("scalar", lambda e, ic=ic: e.activation(absw, wfc[ic % 2], AF.Abs),
                     waits=aw, inc=I("sSB"))
                E.op("scalar", lambda e, ic=ic: e.sign(sgn, wfc[ic % 2]), inc=I("sSB"))
                E.op("vector", lambda e: e.tensor_single_scalar(QSC, absw, csc[:, 16:17],
                                                                ALU.is_gt), waits=[W("sSB")])
                E.op("vector", lambda e, ic=ic: e.tensor_mul(
                    TWS[:, 1024 * ic:1024 * ic + 1024], sgn, QSC))
                E.op("vector", lambda e: e.tensor_mul(absw, absw, QSC))
                E.op("vector", lambda e: e.reduce_sum(csc[:, 9:10], absw, axis=AX.X),
                     inc=I("sST"))
                E.op("vector", lambda e: e.reduce_sum(csc[:, 14:15], QSC, axis=AX.X),
                     inc=I("sST"))
                if ic == 0:
                    E.op("vector", lambda e: e.tensor_copy(csc[:, 10:11], csc[:, 9:10]),
                         waits=[W("sST")])
                    E.op("vector", lambda e: e.tensor_copy(csc[:, 15:16], csc[:, 14:15]),
                         inc=I("sTW"))
                else:
                    E.op("vector", lambda e: e.tensor_add(csc[:, 10:11], csc[:, 10:11],
                                                          csc[:, 9:10]), waits=[W("sST")])
                    E.op("vector", lambda e: e.tensor_add(csc[:, 15:16], csc[:, 15:16],
                                                          csc[:, 14:15]), inc=I("sTW"))
                tw_marks.append(I("sTW").n)
            emit_preduce(csc[:, 10:11], csc[0:1, 9:10], ALU.add, [W("sTW")], I("sTW"))
            emit_preduce(csc[:, 15:16], csc[0:1, 14:15], ALU.add, [W("sTW")], I("sTW"))
            E.op("vector", lambda e: e.tensor_scalar_max(csc[0:1, 14:15], csc[0:1, 14:15],
                                                         1.0), waits=[W("sTW")], inc=I("sST"))
            E.op("vector", lambda e: e.reciprocal(csc[0:1, 15:16], csc[0:1, 14:15]),
                 waits=[W("sST")], inc=I("sST"))
            E.op("vector", lambda e: e.tensor_mul(csc[0:1, 17:18], csc[0:1, 9:10],
                                                  csc[0:1, 15:16]),
                 waits=[W("sST")], inc=I("sTW"))

        def emit_ffn(which):
            res_base = I("sRES").n
            E.op("sync", lambda e: e.dma_start(out=GREP[0:1, :], in_=gbr_d[0:1, :]),
                 waits=[W("sKB", I("sKB").n), W("sRES", res_base)], inc=I("sGB"), dma=True)
            E.op("sync", lambda e: e.dma_start(out=BREP[0:1, :], in_=gbr_d[1:2, :]),
                 inc=I("sGB"), dma=True)
            emit_bcast(GREP[0:1, :], GREP, [W("sGB")], I("sGB"), width=1024)
            emit_bcast(BREP[0:1, :], BREP, [W("sGB", I("sGB").n)], I("sGB"), width=1024)
            gb_done = I("sGB").n
            for t in range(NT):
                emit_ln_tile(t, "ffn", None, [W("sRES", res_base)])
                E.op("vector", lambda e, t=t: e.tensor_scalar(
                    QSC, xr(t), ln_m[:, t:t + 1], ln_r[:, t:t + 1], ALU.subtract, ALU.mult),
                    waits=[W("sST"), W("sGB", gb_done), W("sEX", I("sEX").n)])
                E.op("vector", lambda e: e.tensor_mul(QSC, QSC, GREP))
                E.op("vector", lambda e: e.tensor_add(QSC, QSC, BREP))
                E.op("vector", lambda e: e.reduce_max(csc[:, 9:10], QSC, axis=AX.X,
                                                      apply_absolute_value=True),
                     inc=I("sST"))
                if t == 0:
                    E.op("vector", lambda e: e.tensor_copy(smax, csc[:, 9:10]),
                         waits=[W("sST")], inc=I("sMX"))
                else:
                    E.op("vector", lambda e: e.tensor_max(smax, smax, csc[:, 9:10]),
                         waits=[W("sST")], inc=I("sMX"))
            emit_preduce(smax, mx_s[0:1, 0:1], ALU.max, [W("sMX", I("sMX").n)], I("sMX"))
            E.op("gpsimd", lambda e: e.dma_start(out=mx_i[0:1, 0:1], in_=mx_s[0:1, 0:1]),
                 waits=[W("sMX", I("sMX").n)], inc=I("sMXc"), dma=True)
            E.op("gpsimd", lambda e: e.collective_compute(
                "AllReduce", ALU.max, replica_groups=ALL8,
                ins=[mx_i.ap().opt()], outs=[mx_o.ap().opt()]),
                waits=[W("sMXc")], inc=I("sAG"))
            E.op("gpsimd", lambda e: e.dma_start(out=mx_s[0:1, 1:2], in_=mx_o[0:1, 0:1]),
                 waits=[W("sAG", I("sAG").n)], inc=I("sMXc"), dma=True)
            E.op("vector", lambda e: e.tensor_scalar_max(mx_s[0:1, 2:3], mx_s[0:1, 1:2], 1e-8),
                 waits=[W("sMXc", I("sMXc").n)], inc=I("sST"))
            E.op("vector", lambda e: e.reciprocal(mx_s[0:1, 3:4], mx_s[0:1, 2:3]),
                 waits=[W("sST")], inc=I("sST"))
            E.op("vector", lambda e: e.tensor_scalar_mul(mx_s[0:1, 4:5], mx_s[0:1, 3:4],
                                                         127.0), waits=[W("sST")],
                 inc=I("sMX"))
            emit_bcast(mx_s[0:1, 4:5], csc[:, 18:19], [W("sMX", I("sMX").n)], I("sMX"))
            E.op("vector", lambda e: e.tensor_scalar_mul(mx_s[0:1, 5:6], mx_s[0:1, 2:3],
                                                         1.0 / 127.0),
                 waits=[W("sMX", I("sMX").n), W("sTW", I("sTW").n)], inc=I("sST"))
            E.op("vector", lambda e: e.tensor_mul(mx_s[0:1, 6:7], mx_s[0:1, 5:6],
                                                  csc[0:1, 17:18]),
                 waits=[W("sTW", I("sTW").n), W("sST")], inc=I("sMX"))
            emit_bcast(mx_s[0:1, 6:7], csc[:, 19:20], [W("sMX", I("sMX").n)], I("sMX"))
            so_done = I("sMX").n
            if which == 0:
                dump("tw", TWS[:, 0:1024], [W("sTW", I("sTW").n)])
                dump("scal", mx_s[0:1, 0:8], [W("sMX", so_done)])
            for t in range(NT):
                E.op("vector", lambda e, t=t: e.tensor_scalar(
                    QSC, xr(t), ln_m[:, t:t + 1], ln_r[:, t:t + 1], ALU.subtract, ALU.mult),
                    waits=[W("sMX", so_done)])
                E.op("vector", lambda e: e.tensor_mul(QSC, QSC, GREP))
                E.op("vector", lambda e: e.tensor_add(QSC, QSC, BREP))
                E.op("vector", lambda e: e.tensor_scalar(QSC, QSC, csc[:, 18:19], None,
                                                         ALU.mult))
                E.op("vector", lambda e: e.tensor_scalar(QSC, QSC, -127.0, 127.0,
                                                         ALU.max, ALU.min))
                E.op("vector", lambda e, t=t: e.tensor_scalar(
                    xnrot[t % 2], QSC, MAGIC, MAGIC, ALU.add, ALU.subtract), inc=I("sLN"))
                emit_transpose8(xnrot[t % 2],
                                (lambda tt: (lambda ic: fm(ic)[:, 128 * tt:128 * tt + 128]))(t),
                                [W("sLN")])
            fm_done = I("sFM").n
            for t in range(NT):
                pb = 2 * (t % 2)

                def mmf(e, t=t, pb=pb):
                    inst = None
                    for oh in range(2):
                        for ic in range(8):
                            inst = e.matmul(bk(pb + oh), fm(ic)[:, 128 * t:128 * t + 128],
                                            TWS[:, 1024 * ic + 512 * oh:
                                                1024 * ic + 512 * oh + 512],
                                            start=(ic == 0), stop=(ic == 7))
                    return inst
                wts = [W("sFM", fm_done), W("sTW", I("sTW").n)]
                if t >= 2:
                    wts.append(W("sRES", I("sRES").n - 1))
                E.op("tensor", mmf, waits=wts, inc=I("sOP"))

                def residf(e, t=t, pb=pb):
                    inst = None
                    for oh in range(2):
                        inst = e.scalar_tensor_tensor(
                            xr(t)[:, 512 * oh:512 * oh + 512], bk(pb + oh), csc[:, 19:20],
                            xr(t)[:, 512 * oh:512 * oh + 512], ALU.mult, ALU.add)
                    return inst
                E.op("vector", residf, waits=[W("sOP"), W("sMX", so_done)], inc=I("sRES"))

        def emit_attn_b():
            res_base = I("sRES").n
            E.op("vector", lambda e: e.tensor_scalar(mkA, tri_s, pc_s[:, 1:2], pc_s[:, 0:1],
                                                     ALU.mult, ALU.add),
                 waits=[W("sCT", ct_done), W("sRES", res_base)])
            E.op("vector", lambda e: e.tensor_scalar(mkB, tri_s, pc_s[:, 0:1], None,
                                                     ALU.mult), inc=I("sKB"))
            mk_done = I("sKB").n
            for t in range(NT):
                emit_ln_tile(t, "plain", xnrot[t % 2], [W("sRES", res_base)])
                emit_transpose8(xnrot[t % 2],
                                (lambda tt: (lambda ic: fm(ic)[:, 128 * tt:128 * tt + 128]))(t),
                                [W("sLN")])
            fm_done = I("sFM").n
            E.op("gpsimd", lambda e: e.dma_start(out=WB[:, :], in_=wvb_d[:, :]),
                 waits=[W("sRES", res_base), W("sOP", I("sOP").n)], inc=I("sW1"), dma=True)
            wvb_done = I("sW1").n
            for t in range(NT):
                pb = 2 * (t % 2)
                if t >= 2:
                    E.op("tensor", lambda e: e.nop(), waits=[W("sRP", I("sRP").n - 1)])
                emit_proj_tile(t, WB, [W("sW1", wvb_done), W("sCT", ct_done),
                                       W("sBB", 32)], pb,
                               bias_row=BVB, fm_wait=W("sFM", fm_done))
                bw = [W("sPRJ")]
                if t >= 2:
                    bw.append(W("sv%d" % (t % 2), 16 * (I("sv%d" % (t % 2)).n // 16)))
                E.op("vector", lambda e, t=t, pb=pb, bw=bw: e.tensor_copy(
                    kt_rot[t % 2], PS[:, 512 * pb:512 * pb + 1024]),
                    waits=bw, inc=I("sRP"))
                E.op("sync", lambda e, t=t: e.dma_start(
                    out=vc_i[128 * t:128 * t + 128, :], in_=kt_rot[t % 2]),
                    waits=[W("sRP")], inc=I("sv%d" % (t % 2)), dma=True)
            vo_done = None
            W3S = AOS[:, 10115:10139]
            E.op("gpsimd", lambda e: e.dma_start(out=W3S, in_=w3_d[:, :]),
                 inc=I("sW2"), dma=True)

            def mmp(e):
                inst = None
                for piece in range(4):
                    for ic in range(8):
                        inst = e.matmul(PS[0:3, 2048 + 512 * piece:2048 + 512 * piece + 512],
                                        W3S[:, 3 * ic:3 * ic + 3],
                                        fm(ic)[:, 512 * piece:512 * piece + 512],
                                        start=(ic == 0), stop=(ic == 7))
                return inst
            E.op("tensor", mmp, waits=[W("sFM", fm_done), W("sW2", I("sW2").n),
                                       W("sTR", I("sTR").n)], inc=I("sPP"))
            E.op("vector", lambda e: e.tensor_scalar(POS[0:3, :], PS[0:3, 2048:4096],
                                                     pc_s[:, 2:3][0:3, :], None, ALU.add),
                 waits=[W("sPP"), W("sCT", ct_done)], inc=I("sPP"))
            for piece in range(4):
                E.op("vector", lambda e, p=piece: e.scalar_tensor_tensor(
                    t1r[0:3, :], POS[0:3, 512 * p:512 * p + 512], 1.0,
                    POS[0:3, 512 * p:512 * p + 512], ALU.mult, ALU.mult),
                    waits=[W("sPP", I("sPP").n)], inc=I("sPP"))

                def mmn(e, p=piece):
                    e.matmul(PS[0:4, 2048 + 512 * p:2048 + 512 * p + 512],
                             SEL34A[0:3, 0:4], POS[0:3, 512 * p:512 * p + 512],
                             start=True, stop=False)
                    return e.matmul(PS[0:4, 2048 + 512 * p:2048 + 512 * p + 512],
                                    SEL34B[0:3, 0:4], t1r[0:3, :],
                                    start=False, stop=True)
                E.op("tensor", mmn, waits=[W("sPP"), W("sCT", ct_done)], inc=I("sPP"))
                E.op("vector", lambda e, p=piece: e.tensor_copy(
                    POS[0:4, 512 * p:512 * p + 512],
                    PS[0:4, 2048 + 512 * p:2048 + 512 * p + 512]),
                    waits=[W("sPP")], inc=I("sPP"))
            pp_done = I("sPP").n
            dump("pos", POS[0:16, :], [W("sPP", pp_done)])
            E.op("gpsimd", lambda e: e.dma_start(out=pp_i[:, :], in_=POS[0:4, :]),
                 waits=[W("sPP", pp_done)], inc=I("sPG"), dma=True)
            E.op("gpsimd", lambda e: e.collective_compute(
                "AllGather", ALU.bypass, replica_groups=PAIRS,
                ins=[pp_i.ap().opt()], outs=[pp_o.ap().opt()]),
                waits=[W("sPG")], inc=I("sAG"))
            E.op("gpsimd", lambda e: e.collective_compute(
                "AllGather", ALU.bypass, replica_groups=PAIRS,
                ins=[vc_i.ap().opt()], outs=[vc_o.ap().opt()]),
                waits=[W("sv0", I("sv0").n), W("sv1", I("sv1").n)], inc=I("sAG"))
            ag_done = I("sAG").n
            E.op("sync", lambda e: e.dma_start(
                out=GA[:, 0:16384].rearrange("p (c w) -> p c w", c=16),
                in_=vc_o[0:2048, :].rearrange("(c p) w -> p c w", p=128)),
                waits=[W("sAG", ag_done), W("sPV", I("sPV").n)], inc=I("sGI"), dma=True)
            E.op("sync", lambda e: e.dma_start(
                out=WB[:, :].rearrange("p (c w) -> p c w", c=8),
                in_=vc_o[2048:3072, :].rearrange("(c p) w -> p c w", p=128)),
                waits=[W("sAG", ag_done), W("sPRJ", I("sPRJ").n)], inc=I("sGI"), dma=True)
            E.op("sync", lambda e: e.dma_start(
                out=AOS[:, 0:8192].rearrange("p (c w) -> p c w", c=8),
                in_=vc_o[3072:4096, :].rearrange("(c p) w -> p c w", p=128)),
                waits=[W("sAG", ag_done), W("sAT", I("sAT").n), W("sOP", I("sOP").n)],
                inc=I("sGI"), dma=True)
            gi_done = I("sGI").n
            E.op("vector", lambda e: e.memset(khb[0:5, :], 1.0),
                 waits=[W("sPP", pp_done)], inc=I("sKB"))
            mm_marks = {}
            evac_marks = {}
            src_marks = {}
            for piece in range(12):
                isq = piece >= 8
                if not isq:
                    half, pp = piece // 4, piece % 4

                    def ldp(e, half=half, pp=pp):
                        return e.dma_start(out=POSG[0:4, :],
                                           in_=pp_o[4 * half:4 * half + 4,
                                                    512 * pp:512 * pp + 512])
                    kw = [W("sAG", ag_done), W("sv0", I("sv0").n), W("sv1", I("sv1").n)]
                    if piece:
                        kw.append(W("sKB", src_marks[piece - 1]))
                    E.op("sync", ldp, waits=kw, inc=I("sGI"), dma=True)
                    dst = KB13[:, 2048 * half + 512 * pp:2048 * half + 512 * pp + 512]
                    srcf = POSG[0:4, :]
                    cw = [W("sGI", I("sGI").n)]
                    sela, selb = SELKA[0:5, :], SELKB[0:4, :]
                else:
                    pp = piece - 8
                    dst = QB13[:, 512 * pp:512 * pp + 512]
                    srcf = POS[0:4, 512 * pp:512 * pp + 512]
                    cw = [W("sPP", pp_done)]
                    sela, selb = SELQA[0:5, :], SELQB[0:4, :]
                if piece:
                    cw.append(W("sPP", mm_marks[piece - 1]))
                E.op("vector", lambda e, s=srcf: e.tensor_copy(khb[0:4, :], s),
                     waits=cw)
                E.op("vector", lambda e, s=srcf: e.tensor_sub(klb[0:4, :], s,
                                                              khb[0:4, :]),
                     inc=I("sKB"))
                src_marks[piece] = I("sKB").n
                psb = PS[0:13, 3072 + 512 * (piece % 2):3584 + 512 * (piece % 2)]

                def mmaug(e, psb=psb, sela=sela, selb=selb):
                    e.matmul(psb, sela, khb[0:5, :], start=True, stop=False)
                    return e.matmul(psb, selb, klb[0:4, :], start=False, stop=True)
                mw = [W("sKB", src_marks[piece]), W("sCT", ct_done)]
                if piece >= 2:
                    mw.append(W("sKB", evac_marks[piece - 2]))
                E.op("tensor", mmaug, waits=mw, inc=I("sPP"))
                mm_marks[piece] = I("sPP").n
                E.op("vector", lambda e, d=dst, psb=psb: e.tensor_copy(d[0:13, :], psb),
                     waits=[W("sPP", mm_marks[piece])], inc=I("sKB"))
                evac_marks[piece] = I("sKB").n
            kb_done = I("sKB").n
            dump("kb13", KB13[0:16, :], [W("sKB", kb_done)])
            batches = []
            for t in range(NT):
                J = list(range(0, t + 1)) + list(range(16, 16 + t + 1))
                for bi in range(0, len(J), 4):
                    batches.append((t, bi, J[bi:bi + 4], len(J)))
            rl_marks = {}
            eb_marks = {}
            pv_marks = {}
            rs_marks = {}
            ob_marks = {}

            def emit_mmd(k):
                t, bi, js, nj = batches[k]
                pi = k % 2

                def mmd(e, js=js, pi=pi, t=t):
                    inst = None
                    for idx, j in enumerate(js):
                        inst = e.matmul(
                            PS[:, 512 * pi + 128 * idx:512 * pi + 128 * idx + 128],
                            KB13[0:13, 128 * j:128 * j + 128],
                            QB13[0:13, 128 * t:128 * t + 128],
                            start=(idx == 0), stop=(idx == len(js) - 1),
                            skip_group_check=True)
                    return inst
                wd = [W("sKB", kb_done), W("sRES", res_base)]
                if k >= 2:
                    wd.append(W("sRL", rl_marks[k - 2]))
                E.op("tensor", mmd, waits=wd, inc=I("sDM"))

            emit_mmd(0)
            for k, (t, bi, js, nj) in enumerate(batches):
                pi = k % 2
                s = 512 * (k % 4)
                n4 = len(js)
                fb, lb = bi == 0, bi + 4 >= nj
                rw = [W("sDM", I("sDM").n)]
                if k >= 4:
                    rw.append(W("sPV", pv_marks[k - 4]))
                E.op("vector", lambda e, s=s, pi=pi, n4=n4: e.tensor_scalar(
                    PSTG[:, s:s + 128 * n4],
                    PS[:, 512 * pi:512 * pi + 128 * n4], 1e-20, None, ALU.max),
                    waits=rw, inc=I("sRL"))
                rl_marks[k] = I("sRL").n
                E.op("scalar", lambda e, s=s, n4=n4: e.activation(
                    PSTG[:, s:s + 128 * n4], PSTG[:, s:s + 128 * n4], AF.Ln),
                    waits=[W("sRL", rl_marks[k])])
                E.op("scalar", lambda e, s=s, n4=n4: e.activation(
                    PSTG[:, s:s + 128 * n4], PSTG[:, s:s + 128 * n4], AF.Exp,
                    scale=0.5))
                E.op("scalar", lambda e, s=s, n4=n4: e.activation(
                    PSTG[:, s:s + 128 * n4], PSTG[:, s:s + 128 * n4], AF.Exp,
                    scale=-0.125), inc=I("sEB"))
                eb_marks[k] = I("sEB").n
                if k + 1 < len(batches):
                    emit_mmd(k + 1)
                mw = [W("sEB", eb_marks[k]), W("sKB", mk_done)]
                for li, j in enumerate(js):
                    if j == t:
                        E.op("vector", lambda e, s=s, li=li: e.tensor_mul(
                            PSTG[:, s + 128 * li:s + 128 * li + 128],
                            PSTG[:, s + 128 * li:s + 128 * li + 128], mkA),
                            waits=mw, inc=I("sPB"))
                    if bi + li == nj - 1:
                        E.op("vector", lambda e, s=s, li=li: e.tensor_mul(
                            PSTG[:, s + 128 * li:s + 128 * li + 128],
                            PSTG[:, s + 128 * li:s + 128 * li + 128], mkB),
                            waits=mw, inc=I("sPB"))
                pb_mark = I("sPB").n

                def mms(e, t=t, s=s, n4=n4, fb=fb, lb=lb):
                    inst = None
                    for idx in range(n4):
                        e.ldweights(PSTG[:, s + 128 * idx:s + 128 * idx + 128])
                        inst = e.matmul(PS[:, 1024 + t:1025 + t],
                                        PSTG[:, s + 128 * idx:s + 128 * idx + 128],
                                        ONC, start=(fb and idx == 0),
                                        stop=(lb and idx == n4 - 1),
                                        skip_group_check=True)
                    return inst
                sw = [W("sPB", pb_mark), W("sEB", eb_marks[k]), W("sCT", ct_done)]
                if fb and t >= 1:
                    sw.append(W("sRS", rs_marks[t - 1]))
                E.op("tensor", mms, waits=sw, inc=I("sDM"))

                def mmpv2(e, s=s, js=js, fb=fb, lb=lb):
                    inst = None
                    for ds in range(8):
                        for idx, j in enumerate(js):
                            e.ldweights(vbt(j)[:, 128 * ds:128 * ds + 128])
                            inst = e.matmul(
                                PS[:, 2048 + 128 * ds:2048 + 128 * ds + 128],
                                vbt(j)[:, 128 * ds:128 * ds + 128],
                                PSTG[:, s + 128 * idx:s + 128 * idx + 128],
                                start=((ds == 0 or ds == 4) and fb and idx == 0),
                                stop=(lb and idx == len(js) - 1),
                                skip_group_check=True)
                    return inst
                pw = [W("sPB", pb_mark), W("sEB", eb_marks[k]), W("sGI", gi_done)]
                if fb and t >= 1:
                    pw.append(W("sOB", ob_marks[t - 1]))
                E.op("tensor", mmpv2, waits=pw, inc=I("sPV"))
                pv_marks[k] = I("sPV").n
                if lb:
                    E.op("vector", lambda e, t=t: e.reciprocal(
                        rsumb[:, t:t + 1], PS[:, 1024 + t:1025 + t]),
                        waits=[W("sDM", I("sDM").n)], inc=I("sRS"))
                    rs_marks[t] = I("sRS").n

                    def evb(e, t=t):
                        inst = None
                        for ds in range(8):
                            inst = e.tensor_copy(
                                fm(ds)[:, 128 * t:128 * t + 128],
                                PS[:, 2048 + 128 * ds:2048 + 128 * ds + 128])
                        return inst
                    E.op("vector", evb, waits=[W("sPV", pv_marks[k])], inc=I("sOB"))
                    ob_marks[t] = I("sOB").n
            ob_done = I("sOB").n
            dump("aofb", fm(0), [W("sOB", ob_done)])
            BOREP = GREP
            E.op("sync", lambda e: e.dma_start(out=GREP[0:1, :], in_=bb_d[1:2, :]),
                 waits=[W("sPV", I("sPV").n), W("sOB", I("sOB").n)], inc=I("sGB"), dma=True)
            emit_bcast(GREP[0:1, :], GREP, [W("sGB", I("sGB").n)], I("sGB"), width=1024)
            borep_done = I("sGB").n
            E.op("gpsimd", lambda e: e.dma_start(out=WB[:, :], in_=wob_d[:, :]),
                 waits=[W("sPV", I("sPV").n)], inc=I("sW1"), dma=True)
            wob_done = I("sW1").n
            for t in range(NT):
                pb = 2 * (t % 2)

                def mmob(e, t=t, pb=pb):
                    inst = None
                    for oh in range(2):
                        for ic in range(8):
                            inst = e.matmul(bk(pb + oh), fm(ic)[:, 128 * t:128 * t + 128],
                                            WB[:, 1024 * ic + 512 * oh:
                                               1024 * ic + 512 * oh + 512],
                                            start=(ic == 0), stop=(ic == 7))
                    return inst
                wts = [W("sOB", ob_done), W("sW1", wob_done)]
                if t >= 2:
                    wts.append(W("sRES", I("sRES").n - 1))
                E.op("tensor", mmob, waits=wts, inc=I("sOP"))

                def residb(e, t=t, pb=pb):
                    for oh in range(2):
                        e.scalar_tensor_tensor(
                            xr(t)[:, 512 * oh:512 * oh + 512], bk(pb + oh),
                            rsumb[:, t:t + 1], xr(t)[:, 512 * oh:512 * oh + 512],
                            ALU.mult, ALU.add)
                    return e.tensor_add(xr(t), xr(t), BOREP)
                E.op("vector", residb, waits=[W("sOP"), W("sRS", I("sRS").n),
                                              W("sGB", borep_done)], inc=I("sRES"))

        if stages >= 2:
            emit_ternary()
            emit_ffn(0)
            dump_x("x2")
        if stages >= 3:
            emit_attn_b()
            dump_x("x3")
        if stages >= 4:
            emit_ffn(1)

        for t in range(NT):
            E.op("sync", lambda e, t=t: e.dma_start(out=out_d[128 * t:128 * t + 128, :],
                                                    in_=xr(t)),
                 waits=[W("sRES", I("sRES").n)], inc=I("sX"), dma=True)
        E.op("sync", lambda e: e.nop(), waits=[W("sX", I("sX").n), W("sDP", I("sDP").n)])

        E.emit(nc)
    return nc


def _rope_perm():
    p = []
    for h in range(NH):
        p += [64 * h + 2 * i for i in range(32)]
        p += [64 * h + 2 * i + 1 for i in range(32)]
    return np.array(p, dtype=np.int64)


def _wmat(w, perm=None):
    wt = np.asarray(w).T
    if perm is not None:
        wt = wt[:, perm]
    return np.ascontiguousarray(
        wt.reshape(8, 128, 1024).transpose(1, 0, 2).reshape(128, 8192)).astype(np.float32)


def _stage_inputs(inputs, core):
    b, p = core // 2, core % 2
    x = np.asarray(inputs["L0_attn_a_x"])
    xs = np.ascontiguousarray(x.reshape(4, 32, 128, D)[b, p::2].reshape(TOK, D))
    perm = _rope_perm()
    tpos = ((2 * np.arange(16)[:, None] + p) * 128 + np.arange(128)[None, :]).reshape(-1)
    inv = 1.0 / 10000.0 ** (np.arange(0, HD, 2, dtype=np.float64) / HD)
    ang = tpos[:, None] * inv[None, :]
    ct = np.ascontiguousarray(np.cos(ang).astype(np.float32).reshape(16, 128, 32)
                              .transpose(1, 0, 2).reshape(128, 512))
    st = np.ascontiguousarray(np.sin(ang).astype(np.float32).reshape(16, 128, 32)
                              .transpose(1, 0, 2).reshape(128, 512))
    gbr = np.ascontiguousarray(np.stack([np.asarray(inputs["L0_ffn_gamma"]),
                                         np.asarray(inputs["L0_ffn_beta"])])).astype(np.float32)
    bb = np.ascontiguousarray(np.stack([np.asarray(inputs["L0_attn_b_bv"]),
                                        np.asarray(inputs["L0_attn_b_bo"])])).astype(np.float32)
    pcm = np.zeros((128, 8), np.float32)
    pcm[:, 0] = float(p)
    pcm[:, 1] = 1.0 - float(p)
    pcm[0:3, 2] = np.asarray(inputs["L0_attn_b_b3d"])
    w3 = np.asarray(inputs["L0_attn_b_w3d"]).T
    w3s = np.ascontiguousarray(w3.reshape(8, 128, 3).transpose(1, 0, 2).reshape(128, 24)
                               ).astype(np.float32)
    return {
        "x": xs.astype(np.float32),
        "wq": _wmat(inputs["L0_attn_a_wq"], perm),
        "wk": _wmat(inputs["L0_attn_a_wk"], perm),
        "wv": _wmat(inputs["L0_attn_a_wv"]),
        "wo": _wmat(inputs["L0_attn_a_wo"]),
        "wf": _wmat(inputs["L0_ffn_weight"]),
        "wvb": _wmat(inputs["L0_attn_b_wv"]),
        "wob": _wmat(inputs["L0_attn_b_wo"]),
        "w3": w3s, "gbr": gbr, "bb": bb, "pc": pcm, "ct": ct, "st": st,
    }


_NC_CACHE = {}


def kernel(**inputs):
    key = ("k", STAGES)
    if key not in _NC_CACHE:
        _NC_CACHE[key] = build_program(stages=STAGES)
    nc = _NC_CACHE[key]
    in_maps = [_stage_inputs(inputs, c) for c in range(8)]
    res = run_bass_kernel_spmd(nc, in_maps, core_ids=list(range(8)))
    out = np.zeros((4, 4096, D), np.float32)
    ov = out.reshape(4, 32, 128, D)
    for c in range(8):
        b, p = c // 2, c % 2
        ov[b, p::2] = res.results[c]["out"].reshape(16, 128, D)
    return out
